# version 13 (final)
# baseline (speedup 1.0000x reference)
import contextlib
import math

import numpy as np
import ml_dtypes

import concourse.bass as bass
import concourse.mybir as mybir
from concourse.bass_utils import run_bass_kernel_spmd

F32 = mybir.dt.float32
BF16 = mybir.dt.bfloat16
BF = ml_dtypes.bfloat16
AF = mybir.ActivationFunctionType
ALU = mybir.AluOpType

ENGS = ("sync", "tensor", "vector", "scalar", "gpsimd")
LN_EPS = 1e-5
C_OFF = 10.0


def make_cfg(small=False):
    c = {}
    if small:
        c.update(S=1024, D=512, I=1024, H=2, TOKB=256,
                 bounds=(0, 256, 512, 768, 1024))
    else:
        c.update(S=8192, D=2048, I=8192, H=16, TOKB=512,
                 bounds=(0, 2560, 4608, 6656, 8192))
    c["B"] = 2
    c["DH"] = 128
    c["ND"] = c["D"] // 128
    c["NI"] = c["I"] // 128
    c["NMAX"] = max(b - a for a, b in zip(c["bounds"], c["bounds"][1:]))
    c["SQ"] = math.sqrt(float(c["DH"]))
    c["GRP"] = min(1536, c["S"])
    return c


def hadamard(n):
    h = np.array([[1.0]], dtype=np.float32)
    while h.shape[0] < n:
        h = np.block([[h, h], [h, -h]])
    return h


def host_consts(cfg):
    TOKB = cfg["TOKB"]
    masks = np.zeros((4, 128, TOKB), dtype=np.float32)
    for m in range(4):
        a = m * 128
        for p in range(128):
            masks[m, p, :] = (a + p <= np.arange(TOKB)).astype(np.float32)
    return {
        "eye32": np.eye(128, dtype=np.float32),
        "eye16": np.eye(128, dtype=np.float32).astype(BF),
        "h128c": hadamard(128).astype(BF),
        "hnic": hadamard(cfg["NI"]).astype(BF),
        "masksc": masks.astype(BF),
        "ones_row": np.ones((1, TOKB), dtype=np.float32),
        "ones_r16": np.ones((1, TOKB), dtype=np.float32).astype(BF),
        "ones_c16": np.ones((128, 1), dtype=np.float32).astype(BF),
        "ones_c32": np.ones((128, 1), dtype=np.float32),
        "o1x128": np.ones((1, 128), dtype=np.float32).astype(BF),
        "o1x128_32": np.ones((1, 128), dtype=np.float32),
    }


def _esize(dt):
    return 2 if dt == BF16 else 4


class KB:
    def __init__(self, cfg):
        self.cfg = cfg
        self.nc = bass.Bass("TRN2", target_bir_lowering=False, debug=False)
        self.es = contextlib.ExitStack()
        self.sems = {}
        self.cnt = {}
        self.var_ops = [{e: [] for e in ENGS} for _ in range(4)]
        self.pre_ops = {e: [] for e in ENGS}
        self.post_ops = {e: [] for e in ENGS}
        self.cur = None
        self._persist_off = 0
        self._semmap = {}
        self._phase_base = None
        self._phase_off = {}
        self._declare_io()
        self._alloc()

    ARENA_BYTES = 190 * 1024

    def _salloc(self, name, shape, dtype, pool):
        if not hasattr(self, "_arena_base"):
            arena = self.nc.alloc_sbuf_tensor("arena0", [128, self.ARENA_BYTES],
                                              mybir.dt.uint8)
            self._arena_base = self.nc.lookup_mloc(arena).addr
            self._persist_off = self._arena_base
        nbytes = int(np.prod(shape[1:])) * _esize(dtype)
        nbytes = (nbytes + 31) & ~31
        if pool == "persist":
            off = self._persist_off
            self._persist_off = off + nbytes
        else:
            base = self._phase_base
            if pool in ("wp", "wp2"):
                base = self._phase_base + 108 * 1024
            off = self._phase_off.get(pool, base)
            self._phase_off[pool] = off + nbytes
            if pool == "p1":
                assert off + nbytes <= self._phase_base + 108 * 1024, (name, off + nbytes)
        assert off + nbytes <= self._arena_base + self.ARENA_BYTES, (
            name, pool, off + nbytes - self._arena_base)
        t = self.nc.alloc_sbuf_tensor_at(name, shape, dtype, offset=off)
        return t

    _ENG_PHYS = {"tensor": "sT", "vector": "sV", "scalar": "sS",
                 "gpsimd": "sG", "sync": "sQ"}
    _DMA_PHYS = {"sync": "dS", "scalar": "dA", "gpsimd": "dG",
                 "vector": "dV", "tensor": "dT"}

    def _phys(self, name):
        return self._semmap[name]

    def sem_handle(self, phys):
        if phys not in self.sems:
            self.sems[phys] = self.es.enter_context(self.nc.semaphore(phys))
            self.cnt.setdefault(phys, 0)
        return self.sems[phys]

    def _map_name(self, name, eng, is_dma):
        if name in self._semmap:
            return self._semmap[name]
        if name.startswith("bar"):
            phys = name
        elif is_dma:
            import zlib
            digit = int(name[-1]) if name[-1] in "01" else 0
            base = name.rstrip("01")
            parity = (2 * (zlib.crc32(base.encode()) % 2) + digit) % 4
            phys = self._DMA_PHYS[eng] + str(parity)
        else:
            phys = self._ENG_PHYS[eng]
        self._semmap[name] = phys
        return phys

    def inc(self, name, amt, eng=None, is_dma=False):
        phys = self._map_name(name, eng, is_dma)
        s = self.sem_handle(phys)
        self.cnt[phys] = self.cnt.get(phys, 0) + amt
        return s, self.cnt[phys]

    def tot(self, name):
        if name not in self._semmap:
            return 0
        return self.cnt.get(self._semmap[name], 0)

    def on(self, eng, fn):
        self.cur[eng].append(fn)

    def dma(self, eng, out, in_, semname, transpose=False):
        phys = self._map_name(semname, eng, True)
        prev = self.cnt.get(phys, 0)
        s = self.sem_handle(phys)
        if prev > 0:
            self.on(eng, lambda h, s=s, prev=prev: h.wait_ge(s, prev))
        self.cnt[phys] = prev + 16
        def fn(h, out=out, in_=in_, s=s, transpose=transpose):
            h.dma_start(out=out, in_=in_, transpose=transpose).then_inc(s, 16)
        self.on(eng, fn)
        return prev + 16

    def wait(self, eng, semname, val):
        if val <= 0:
            return
        if semname not in self._semmap:
            return
        s = self.sem_handle(self._semmap[semname])
        self.on(eng, lambda h, s=s, val=val: h.wait_ge(s, val))

    def op(self, eng, fn, semname=None, amt=1):
        if semname is None:
            self.on(eng, fn)
            return None
        s, v = self.inc(semname, amt, eng=eng)
        def wrapped(h, fn=fn, s=s, amt=amt):
            fn(h).then_inc(s, amt)
        self.on(eng, wrapped)
        return v

    def _declare_io(self):
        nc, c = self.nc, self.cfg
        D, I, H, S, TOKB = c["D"], c["I"], c["H"], c["S"], c["TOKB"]
        NM, ND, NI = c["NMAX"], c["ND"], c["NI"]
        dt = nc.dram_tensor
        self.xin = dt("xin", [S, D], F32, kind="ExternalInput").ap()
        self.wq = dt("wq", [H, D, 128], F32, kind="ExternalInput").ap()
        self.wk = dt("wk", [H, D, 128], F32, kind="ExternalInput").ap()
        self.wv = dt("wv", [H, D, 128], F32, kind="ExternalInput").ap()
        self.wo = dt("wo", [H, 128, D], F32, kind="ExternalInput").ap()
        self.bq = dt("bq", [1, H * 128], F32, kind="ExternalInput").ap()
        self.bk = dt("bk", [1, H * 128], F32, kind="ExternalInput").ap()
        self.bv = dt("bv", [1, H * 128], F32, kind="ExternalInput").ap()
        self.bo = dt("bo", [ND, 128], F32, kind="ExternalInput").ap()
        self.su_ug = dt("su_ug", [ND, 128], F32, kind="ExternalInput").ap()
        self.su_dp = dt("su_dp", [NI, 128], F32, kind="ExternalInput").ap()
        self.ws_ug = dt("ws_ug", [2 * NI, 128], F32, kind="ExternalInput").ap()
        self.ws_dp = dt("ws_dp", [ND, 128], F32, kind="ExternalInput").ap()
        self.w_up = dt("w_up", [I, D], F32, kind="ExternalInput").ap()
        self.w_gate = dt("w_gate", [I, D], F32, kind="ExternalInput").ap()
        self.w_down = dt("w_down", [D, I], F32, kind="ExternalInput").ap()
        for nm, sh in {"eye32": [128, 128], "ones_row": [1, TOKB],
                       "ones_c32": [128, 1], "o1x128_32": [1, 128]}.items():
            setattr(self, nm, dt(nm, sh, F32, kind="ExternalInput").ap())
        shapes16 = {"eye16": [128, 128], "h128c": [128, 128], "hnic": [NI, NI],
                    "masksc": [4, 128, TOKB], "ones_r16": [1, TOKB],
                    "ones_c16": [128, 1], "o1x128": [1, 128]}
        for nm, sh in shapes16.items():
            setattr(self, nm, dt(nm, sh, BF16, kind="ExternalInput").ap())
        self.out = dt("out", [NM, D], F32, kind="ExternalOutput").ap()
        self.wqkv16 = dt("wqkv16", [3, H, D, 128], BF16)
        self.wo16s = dt("wo16s", [H, 128, D], BF16)
        self.k_s = dt("k_s", [H, 128, S], BF16)
        self.v_s = dt("v_s", [H, S, 128], BF16)
        self.q_s = dt("q_s", [H, 128, NM], BF16)
        self.qn_s = dt("qn_s", [H, NM], BF16)
        self.z_s = dt("z_s", [H, 128, NM], BF16)
        self.xfm_s = dt("xfm_s", [ND, 128, NM], F32)
        self.h_s = dt("h_s", [ND, 128, NM], F32)
        self.wup16 = dt("wup16", [I, D], BF16)
        self.wgt16 = dt("wgt16", [I, D], BF16)
        self.wdn16 = dt("wdn16", [D, I], BF16)
        self.wupT = dt("wupT", [I // TOKB, ND, 128, TOKB], BF16)
        self.wgtT = dt("wgtT", [I // TOKB, ND, 128, TOKB], BF16)
        self.u_s = dt("u_s", [NI, 128, D], BF16)
        self.wdnT = dt("wdnT", [NI, 128, D], BF16)
        self.g_s = dt("g_s", [ND, NI, 128, 128], BF16)

    def _alloc(self):
        nc, c = self.nc, self.cfg
        D, I, H, TOKB, ND, NI = c["D"], c["I"], c["H"], c["TOKB"], c["ND"], c["NI"]
        NM, S, GRP = c["NMAX"], c["S"], c["GRP"]
        P = lambda name, shape, dtype: self._salloc(name, shape, dtype, "persist")
        self.c_eye32 = P("c_eye32", [128, 128], F32)
        self.c_eye16 = P("c_eye16", [128, 128], BF16)
        self.c_h128 = P("c_h128", [128, 128], BF16)
        self.c_hni = P("c_hni", [NI, NI], BF16)
        self.c_masks = P("c_masks", [128, 4, TOKB], BF16)
        self.c_ones_row = P("c_ones_row", [1, TOKB], F32)
        self.c_ones_r16 = P("c_ones_r16", [1, TOKB], BF16)
        self.c_ones_c16 = P("c_ones_c16", [128, 1], BF16)
        self.c_ones_c32 = P("c_ones_c32", [128, 1], F32)
        self.c_o1x128 = P("c_o1x128", [1, 128], BF16)
        self.c_o1x128_32 = P("c_o1x128_32", [1, 128], F32)
        self.v_su_ug = P("v_su_ug", [128, ND], F32)
        self.v_sup = P("v_sup", [128, NI], F32)
        self.v_sgt = P("v_sgt", [128, NI], F32)
        self.v_sdn = P("v_sdn", [128, ND], F32)
        self.v_sudp = P("v_sudp", [128, NI], F32)
        self.v_bo = P("v_bo", [128, ND], F32)
        self.r_bq = P("r_bq", [1, H * 128], BF16)
        self.r_bk = P("r_bk", [1, H * 128], BF16)
        self.r_bv = P("r_bv", [1, H * 128], BF16)
        
        self.r_stage = P("r_stage", [128, 128], F32)
        self.knmax = P("knmax", [1, H * (S // TOKB)], F32)
        self.mkneg = P("mkneg", [1, H], F32)
        self._phase_base = ((self._persist_off + 63) & ~63)
        budget = self._arena_base + self.ARENA_BYTES
        assert self._phase_base - self._arena_base < 60 * 1024, self._phase_base

        def A(name, shape, dtype, pool):
            t = self._salloc(name, shape, dtype, pool)
            assert self._phase_off[pool] <= budget, (name, pool, self._phase_off[pool])
            return t
        self.nrm = A("nrm", [1, TOKB], F32, "p0")
        self.tmpr = A("tmpr", [1, TOKB], F32, "p0")
        self.xt32 = A("xt32", [128, TOKB // 128, D], F32, "p0")
        self.xt16 = A("xt16", [128, TOKB // 128, D], BF16, "p0")
        self.xfm16 = A("xfm16", [128, ND, GRP], BF16, "p0")
        self.xfm32 = A("xfm32", [128, ND, 128], F32, "p0")
        self.wh16 = A("wh16", [128, 2, 3, ND, 128], BF16, "p0")
        self.kvq16 = A("kvq16", [128, 3, 2, TOKB], BF16, "p0")
        self.vtr16 = A("vtr16", [128, 2, TOKB // 128, 128], BF16, "p0")
        self.sq16 = A("sq16", [128, 2, TOKB], BF16, "p0")
        self.nrow = A("nrow", [1, 2, TOKB], BF16, "p0")
        self.r_b32 = A("r_b32", [1, H * 128], F32, "p0")
        self.wcast32 = A("wcast32", [128, 2, 2048], F32, "wp2")
        self.wcast16 = A("wcast16", [128, 2, 2048], BF16, "wp2")
        self.wtT = A("wtT", [128, 16, TOKB], BF16, "wp")
        self.ubuf = A("ubuf", [128, 2, TOKB], BF16, "wp")
        self.sbB = A("sbB", [NI, 12, TOKB], BF16, "wp")
        self.gout = A("gout", [NI, 2, TOKB], BF16, "wp")
        self.kbuf = A("kbuf", [128, 2, S], BF16, "p1")
        self.vbuf = A("vbuf", [128, 2, S // 128, 128], BF16, "p1")
        self.qbuf = A("qbuf", [128, 2, NM], BF16, "p1")
        self.qnrow = A("qnrow", [1, 2, NM], BF16, "p1")
        self.ebuf = A("ebuf", [128, 3, 2, TOKB], BF16, "p1")
        self.emsk = A("emsk", [128, 2, 2, TOKB], BF16, "p1")
        self.dacc = A("dacc", [128, 2, TOKB], F32, "p1")
        self.crow = A("crow", [1, 2, TOKB], BF16, "p1")
        self.dinv16 = A("dinv16", [1, TOKB], BF16, "p1")
        self.dinvb = A("dinvb", [128, TOKB], BF16, "p1")
        self.lnrow = A("lnrow", [1, TOKB], F32, "p1")
        self.dinv32 = A("dinv32", [1, TOKB], F32, "p1")
        self.z16 = A("z16", [128, 2, TOKB], BF16, "p1")
        self.wob = A("wob", [128, 2, H * 128], BF16, "p23")
        self.xst = A("xst", [128, 2, TOKB], F32, "p23")
        self.y32 = A("y32", [128, ND, TOKB], F32, "p23")
        self.hst = A("hst", [128, 2, TOKB], F32, "p23")
        self.hb16 = A("hb16", [128, 2, ND, TOKB], BF16, "p23")
        assert int(math.log2(ND)) % 2 == 0
        self.h2b = self.hb16[:, 1]
        self.wtb = A("wtb", [128, 2, 2, ND, 128], BF16, "p23")
        self.g16 = A("g16", [128, TOKB], BF16, "p23")
        self.z1 = A("z1", [128, NI // 2, TOKB], BF16, "p23")
        self.gwb = A("gwb", [128, 2, NI // 4, 128], BF16, "p23")
        self.mrow = A("mrow", [1, TOKB], F32, "p23")
        self.vrow = A("vrow", [1, TOKB], F32, "p23")
        self.mb = A("mb", [128, TOKB], F32, "p23")
        self.rb = A("rb", [128, TOKB], F32, "p23")
        self.ot32 = A("ot32", [128, 2, TOKB // 128, 128], F32, "p23")
        self.ysq = A("ysq", [128, 2, TOKB], F32, "p23")
        self.oap = A("oap", [128, 2, TOKB], F32, "p23")
        self.wrow = A("wrow", [1, TOKB], F32, "p23")
        self.rrow = A("rrow", [1, TOKB], F32, "p23")
        for pool, off in self._phase_off.items():
            assert off <= budget, (pool, off - self._arena_base)
        PS = nc.alloc_psum_tensor
        self.psA = PS("psA", [128, 2, 512], F32)
        self.psB = PS("psB", [128, 2, 512], F32)
        self.psC = PS("psC", [128, 512], F32)
        self.psD = PS("psD", [128, 512], F32)
        self.psE = PS("psE", [128, 512], F32)
        self.psF = PS("psF", [128, 512], F32)


    def emit_pre(self):
        b, c = self, self.cfg
        D, I, H, TOKB, ND, NI = c["D"], c["I"], c["H"], c["TOKB"], c["ND"], c["NI"]
        self.cur = self.pre_ops
        consts = [
            (self.c_eye32, self.eye32), (self.c_eye16, self.eye16),
            (self.c_h128, self.h128c), (self.c_hni, self.hnic),
            (self.c_ones_row, self.ones_row), (self.c_ones_r16, self.ones_r16),
            (self.c_ones_c16, self.ones_c16), (self.c_ones_c32, self.ones_c32),
            (self.c_o1x128, self.o1x128), (self.c_o1x128_32, self.o1x128_32),
        ]
        for dst, src in consts:
            b.dma("sync", dst[:], src, "c_ld")
        b.dma("sync", self.c_masks[:], self.masksc.rearrange("m p t -> p m t"), "c_ld")
        for i, (dst, srcb) in enumerate([(self.r_bq, self.bq), (self.r_bk, self.bk),
                                         (self.r_bv, self.bv)]):
            if i > 0:
                b.wait("sync", "pre_dve", b.tot("pre_dve"))
            b.dma("sync", self.r_b32[:], srcb, "b_ld")
            b.wait("vector", "b_ld", b.tot("b_ld"))
            b.op("vector", lambda h, dst=dst: h.tensor_copy(
                out=dst[:], in_=self.r_b32[:]), "pre_dve")
        vecs = [
            (self.su_ug, ND, [(self.v_su_ug, 0, ND, 1.0)]),
            (self.su_dp, NI, [(self.v_sudp, 0, NI, 1.0)]),
            (self.ws_ug, 2 * NI, [(self.v_sup, 0, NI, 1.0 / math.sqrt(D)),
                                  (self.v_sgt, NI, NI, 1.0 / math.sqrt(D))]),
            (self.ws_dp, ND, [(self.v_sdn, 0, ND, 1.0 / math.sqrt(I))]),
            (self.bo, ND, [(self.v_bo, 0, ND, 1.0)]),
        ]
        for src, rows, outs in vecs:
            b.wait("sync", "pre_pe", b.tot("pre_pe"))
            b.dma("sync", self.r_stage[0:rows, :], src, "v_ld")
            b.wait("tensor", "v_ld", b.tot("v_ld"))
            b.wait("tensor", "pre_dve", b.tot("pre_dve"))
            b.op("tensor", lambda h, rows=rows: h.transpose(
                out=self.psF[0:128, 0:rows], in_=self.r_stage[0:rows, :],
                identity=self.c_eye32[0:rows, 0:rows]), "pre_pe")
            b.wait("vector", "pre_pe", b.tot("pre_pe"))
            for dst, off, n, scale in outs:
                if scale == 1.0:
                    b.op("vector", lambda h, dst=dst, off=off, n=n: h.tensor_copy(
                        out=dst[:, 0:n], in_=self.psF[:, off:off + n]), "pre_dve")
                else:
                    b.op("vector", lambda h, dst=dst, off=off, n=n, s=scale: h.tensor_scalar(
                        out=dst[:, 0:n], in0=self.psF[:, off:off + n],
                        scalar1=s, scalar2=None, op0=ALU.mult), "pre_dve")
        b.wait("vector", "pre_dve", b.tot("pre_dve"))
        b.op("vector", lambda h: h.tensor_tensor(
            out=self.v_sup[:], in0=self.v_sup[:], in1=self.v_sudp[:],
            op=ALU.mult), "pre_dve")
        jobs = []
        for m, w in [(0, self.wq), (1, self.wk), (2, self.wv)]:
            for hh in range(H):
                jobs.append((w[hh].rearrange("(c p) e -> p c e", p=128),
                             self.wqkv16[m, hh].rearrange("(c p) e -> p c e", p=128)))
        for hh in range(H):
            jobs.append((self.wo[hh], self.wo16s[hh]))
        list(self._cast_jobs(jobs, "wc"))
        self.pre_totals = dict(self.cnt)

    def barrier_pre(self):
        for eng in ENGS:
            for phys, val in self.pre_totals.items():
                if val > 0 and not phys.startswith("bar"):
                    h = self.sem_handle(phys)
                    self.on(eng, lambda hh, h=h, val=val: hh.wait_ge(h, val))

    def emit_p0(self, v):
        b, c = self, self.cfg
        D, H, TOKB, ND = c["D"], c["H"], c["TOKB"], c["ND"]
        GRP, S = c["GRP"], c["S"]
        s_v, e_v = c["bounds"][v], c["bounds"][v + 1]
        self._p0mm_after = {}
        self._wslot_after = {}
        NT = S // TOKB
        TS = TOKB // 128
        eye16, eye32 = self.c_eye16, self.c_eye32
        psq = [self.psA[:, 0], self.psA[:, 1], self.psB[:, 0], self.psB[:, 1]]
        n_own = 0
        ngr = (e_v + GRP - 1) // GRP
        qd_n = 0
        fq_n = 0
        it_n = 0
        xe_hist, xef_hist = {}, {}
        xfs_last = 0

        for g in range(ngr):
            gs, ge = g * GRP, min(e_v, (g + 1) * GRP)
            tiles = list(range(gs // TOKB, ge // TOKB))
            for ti, t in enumerate(tiles):
                own = (t * TOKB >= s_v)
                xr = self.xin[t * TOKB:(t + 1) * TOKB].rearrange(
                    "(s p) d -> p s d", p=128)
                xld_h, xc_h = {}, {}
                for ts in range(TS):
                    if g + ti > 0:
                        b.wait("sync", "xc", b.tot("xc"))
                        if b.tot("ptf") > 0:
                            b.wait("sync", "ptf", b.tot("ptf"))
                    b.dma("sync", self.xt32[:, ts], xr[:, ts], "xld")
                    xld_h[ts] = b.tot("xld")
                for ts in range(TS):
                    b.wait("scalar", "xld", xld_h[ts])
                    if b.tot("pt") > 0:
                        b.wait("scalar", "pt", b.tot("pt") if ts == 0 else 0)
                    b.op("scalar", lambda h, ts=ts: h.copy(
                        out=self.xt16[:, ts], in_=self.xt32[:, ts]), "xc")
                    xc_h[ts] = b.tot("xc")
                for ts in range(TS):
                    xc_now = xc_h[ts]
                    for dq in range(ND // 4):
                        slot = qd_n % 4
                        ps16 = psq[slot].bitcast(BF16)
                        b.wait("tensor", "xc", xc_now)
                        if qd_n >= 4:
                            b.wait("tensor", "xe", xe_hist[qd_n - 4])
                        for j in range(4):
                            dc = dq * 4 + j
                            b.op("tensor",
                                 lambda h, ts=ts, dc=dc, ps16=ps16, j=j: h.transpose(
                                     out=ps16[:, j * 128:(j + 1) * 128],
                                     in_=self.xt16[:, ts, dc * 128:(dc + 1) * 128],
                                     identity=eye16[:]),
                                 "pt" if j == 3 else None)
                        b.wait("vector", "pt", b.tot("pt"))
                        go = (t * TOKB - gs) + ts * 128
                        b.op("vector", lambda h, dq=dq, ps16=ps16, go=go: h.tensor_copy(
                            out=self.xfm16[:, dq * 4:(dq + 1) * 4, go:go + 128],
                            in_=ps16[:, 0:512].rearrange("p (c e) -> p c e", e=128)),
                            "xe")
                        xe_hist[qd_n] = b.tot("xe")
                        qd_n += 1
                if own:
                    for ts in range(TS):
                        if n_own > 0 or ts > 0:
                            b.wait("vector", "xfs", xfs_last)
                        for dq in range(ND // 4):
                            ps32 = [self.psC, self.psF][fq_n % 2]
                            if fq_n >= 2:
                                b.wait("tensor", "xef", xef_hist[fq_n - 2])
                            for j in range(4):
                                dc = dq * 4 + j
                                b.op("tensor",
                                     lambda h, ts=ts, dc=dc, ps32=ps32, j=j: h.transpose(
                                         out=ps32[:, j * 128:(j + 1) * 128],
                                         in_=self.xt32[:, ts, dc * 128:(dc + 1) * 128],
                                         identity=eye32[:]),
                                     "ptf" if j == 3 else None)
                            b.wait("vector", "ptf", b.tot("ptf"))
                            b.op("vector", lambda h, dq=dq, ps32=ps32: h.tensor_copy(
                                out=self.xfm32[:, dq * 4:(dq + 1) * 4, :],
                                in_=ps32.rearrange("p (c e) -> p c e", e=128)), "xef")
                            xef_hist[fq_n] = b.tot("xef")
                            fq_n += 1
                        b.wait("gpsimd", "xef", b.tot("xef"))
                        xfs_last = b.dma("gpsimd",
                            self.xfm_s.rearrange("c p n -> p c n")[
                                :, :, (t * TOKB - s_v) + ts * 128:
                                (t * TOKB - s_v) + ts * 128 + 128],
                            self.xfm32[:], "xfs")
                    n_own += 1
            for hh in range(H):
                slot = hh % 2
                wsem = f"whl{slot}"
                if slot in self._wslot_after:
                    b.wait("sync", "p0v", self._wslot_after[slot])
                for m in range(3):
                    b.dma("sync", self.wh16[:, slot, m],
                          self.wqkv16[m, hh].rearrange("(c p) e -> p c e", p=128), wsem)
                b.wait("tensor", wsem, b.tot(wsem))
                for ti, t in enumerate(tiles):
                    own = (t * TOKB >= s_v)
                    sl = it_n % 2
                    goff = t * TOKB - gs
                    if it_n >= 2:
                        b.wait("vector", f"ks_st{sl}", b.tot(f"ks_st{sl}"))
                        b.wait("vector", f"vt_st{sl}", b.tot(f"vt_st{sl}"))
                        b.wait("vector", f"qs_st{sl}", b.tot(f"qs_st{sl}"))
                    b.wait("tensor", "xe", b.tot("xe"))
                    for dc in range(ND):
                        b.op("tensor", lambda h, dc=dc, slot=slot, goff=goff: h.matmul(
                            out=self.psA[:, 0, 0:TOKB], lhsT=self.wh16[:, slot, 1, dc],
                            rhs=self.xfm16[:, dc, goff:goff + TOKB],
                            start=(dc == 0), stop=False, skip_group_check=True))
                    b.op("tensor", lambda h, hh=hh: h.matmul(
                        out=self.psA[:, 0, 0:TOKB], lhsT=self.r_bk[0:1, hh * 128:(hh + 1) * 128],
                        rhs=self.c_ones_r16[0:1, :], start=False, stop=True,
                        skip_group_check=True), "p0k")
                    for dc in range(ND):
                        b.op("tensor", lambda h, dc=dc, slot=slot, goff=goff: h.matmul(
                            out=self.psA[:, 1, 0:TOKB], lhsT=self.wh16[:, slot, 2, dc],
                            rhs=self.xfm16[:, dc, goff:goff + TOKB],
                            start=(dc == 0), stop=False, skip_group_check=True))
                    b.op("tensor", lambda h, hh=hh: h.matmul(
                        out=self.psA[:, 1, 0:TOKB], lhsT=self.r_bv[0:1, hh * 128:(hh + 1) * 128],
                        rhs=self.c_ones_r16[0:1, :], start=False, stop=True,
                        skip_group_check=True), "p0v")
                    if own:
                        for dc in range(ND):
                            b.op("tensor", lambda h, dc=dc, slot=slot, goff=goff: h.matmul(
                                out=self.psB[:, 0, 0:TOKB], lhsT=self.wh16[:, slot, 0, dc],
                                rhs=self.xfm16[:, dc, goff:goff + TOKB],
                                start=(dc == 0), stop=False, skip_group_check=True))
                        b.op("tensor", lambda h, hh=hh: h.matmul(
                            out=self.psB[:, 0, 0:TOKB], lhsT=self.r_bq[0:1, hh * 128:(hh + 1) * 128],
                            rhs=self.c_ones_r16[0:1, :], start=False, stop=True,
                            skip_group_check=True), "p0q")
                    self._wslot_after[slot] = b.tot("p0v")
                    b.wait("vector", "p0k", b.tot("p0k"))
                    b.op("vector", lambda h, sl=sl: h.tensor_copy(
                        out=self.kvq16[:, 1, sl], in_=self.psA[:, 0, 0:TOKB]), "p0ek")
                    b.wait("vector", "p0v", b.tot("p0v"))
                    b.op("vector", lambda h, sl=sl: h.tensor_copy(
                        out=self.kvq16[:, 2, sl], in_=self.psA[:, 1, 0:TOKB]), "p0ev")
                    if own:
                        b.wait("vector", "p0q", b.tot("p0q"))
                        b.op("vector", lambda h, sl=sl: h.tensor_copy(
                            out=self.kvq16[:, 0, sl], in_=self.psB[:, 0, 0:TOKB]), "p0eq")
                    b.wait("tensor", "p0ev", b.tot("p0ev"))
                    if it_n >= 1:
                        b.wait("tensor", "p0evt", b.tot("p0evt"))
                    psv16 = self.psB[:, 1].bitcast(BF16)
                    for ts in range(TS):
                        b.op("tensor", lambda h, ts=ts, sl=sl, psv16=psv16: h.transpose(
                            out=psv16[:, ts * 128:(ts + 1) * 128],
                            in_=self.kvq16[:, 2, sl][:, ts * 128:(ts + 1) * 128],
                            identity=eye16[:]), "p0vt" if ts == TS - 1 else None)
                    b.wait("vector", "p0vt", b.tot("p0vt"))
                    b.op("vector", lambda h, sl=sl, psv16=psv16: h.tensor_copy(
                        out=self.vtr16[:, sl], in_=psv16[:, 0:TS * 128].rearrange(
                            "p (s e) -> p s e", e=128)), "p0evt")
                    b.wait("scalar", "p0ek", b.tot("p0ek"))
                    if it_n >= 1:
                        b.wait("scalar", "p0n", b.tot("p0n"))
                    b.op("scalar", lambda h, sl=sl: h.square(
                        out=self.sq16[:, sl], in_=self.kvq16[:, 1, sl]), "p0s")
                    b.wait("tensor", "p0s", b.tot("p0s"))
                    if it_n >= 1:
                        b.wait("tensor", "p0r", b.tot("p0r"))
                        b.wait("tensor", "p0rs", b.tot("p0rs"))
                    b.op("tensor", lambda h, sl=sl: h.matmul(
                        out=self.psE[0:1, 0:TOKB], lhsT=self.c_ones_c16[:],
                        rhs=self.sq16[:, sl], start=True, stop=True,
                        skip_group_check=True), "p0n")
                    b.wait("vector", "p0n", b.tot("p0n"))
                    b.op("vector", lambda h, hh=hh, t=t: h.reduce_max(
                        out=self.knmax[0:1, hh * NT + t:hh * NT + t + 1],
                        in_=self.psE[0:1, 0:TOKB], axis=mybir.AxisListType.X), "p0r")
                    if own:
                        b.wait("scalar", "p0eq", b.tot("p0eq"))
                        b.wait("scalar", "p0n", b.tot("p0n"))
                        b.op("scalar", lambda h, sl=sl: h.square(
                            out=self.sq16[:, sl], in_=self.kvq16[:, 0, sl]), "p0s")
                        b.wait("tensor", "p0s", b.tot("p0s"))
                        b.wait("tensor", "p0r", b.tot("p0r"))
                        b.wait("tensor", "p0rs", b.tot("p0rs"))
                        b.op("tensor", lambda h, sl=sl: h.matmul(
                            out=self.psE[0:1, 0:TOKB], lhsT=self.c_ones_c16[:],
                            rhs=self.sq16[:, sl], start=True, stop=True,
                            skip_group_check=True), "p0n")
                        b.wait("scalar", "p0n", b.tot("p0n"))
                        b.wait("scalar", f"qn_st{sl}", b.tot(f"qn_st{sl}"))
                        b.op("scalar", lambda h, sl=sl: h.activation(
                            out=self.nrow[0:1, sl], in_=self.psE[0:1, 0:TOKB],
                            func=AF.Sqrt), "p0rs")
                    b.wait("gpsimd", "p0ek", b.tot("p0ek"))
                    b.dma("gpsimd", self.k_s[hh][:, t * TOKB:(t + 1) * TOKB],
                          self.kvq16[:, 1, sl], f"ks_st{sl}")
                    b.wait("gpsimd", "p0evt", b.tot("p0evt"))
                    b.dma("gpsimd", self.v_s[hh].rearrange(
                        "(s p) e -> p s e", p=128)[:, t * TS:(t + 1) * TS, :],
                        self.vtr16[:, sl], f"vt_st{sl}")
                    if own:
                        b.wait("gpsimd", "p0eq", b.tot("p0eq"))
                        b.dma("gpsimd", self.q_s[hh][:, (t * TOKB - s_v):
                                                     (t * TOKB - s_v) + TOKB],
                              self.kvq16[:, 0, sl], f"qs_st{sl}")
                        b.wait("gpsimd", "p0rs", b.tot("p0rs"))
                        b.dma("gpsimd", self.qn_s[hh:hh + 1,
                                                  (t * TOKB - s_v):(t * TOKB - s_v) + TOKB],
                              self.nrow[0:1, sl], f"qn_st{sl}")
                    it_n += 1
        b.wait("vector", "p0r", b.tot("p0r"))
        for hh in range(H):
            b.op("vector", lambda h, hh=hh: h.reduce_max(
                out=self.tmpr[0:1, hh:hh + 1],
                in_=self.knmax[0:1, hh * NT:hh * NT + (e_v // TOKB)],
                axis=mybir.AxisListType.X), "mkr")
        b.wait("scalar", "mkr", b.tot("mkr"))
        b.op("scalar", lambda h: h.activation(
            out=self.nrm[0:1, 0:H], in_=self.tmpr[0:1, 0:H], func=AF.Sqrt), "mks")
        b.wait("gpsimd", "mks", b.tot("mks"))
        b.op("gpsimd", lambda h: h.tensor_scalar_mul(
            out=self.mkneg[0:1, 0:H], in0=self.nrm[0:1, 0:H], scalar1=-1.0), "mkn")


    def barrier(self, name):
        s, tot = self.inc(name, 5, eng="sync")
        for eng in ENGS:
            phys = self._ENG_PHYS[eng]
            if self.cnt.get(phys, 0) > 0:
                hh = self.sem_handle(phys)
                val = self.cnt[phys]
                self.on(eng, lambda h, hh=hh, val=val: h.wait_ge(hh, val))
            self.on(eng, lambda h, s=s: h.nop().then_inc(s, 1))
        for eng in ENGS:
            self.on(eng, lambda h, s=s, tot=tot: h.wait_ge(s, tot))

    def _cast_jobs(self, jobs, tag, chunk=0, cast_eng="scalar"):
        b = self
        lh = {}

        def bufs(n):
            sap, dap = jobs[n]
            sl = n % 2
            w = int(np.prod(sap.shape[1:]))
            src32 = self.wcast32[:, sl, 0:w]
            src16 = self.wcast16[:, sl, 0:w]
            shp = sap.shape[1:]
            if len(shp) == 2:
                src32 = src32.rearrange("p (c e) -> p c e", e=shp[1])
                src16 = src16.rearrange("p (c e) -> p c e", e=shp[1])
            return sap, dap, src32, src16, sl

        def load(n):
            sap, dap, src32, src16, sl = bufs(n)
            if n >= 2:
                b.wait("sync", f"{tag}_c", b.tot(f"{tag}_c"))
            b.dma("sync", src32, sap, f"{tag}_l{sl}")
            lh[n] = b.tot(f"{tag}_l{sl}")

        load(0)
        for n in range(len(jobs)):
            sap, dap, src32, src16, sl = bufs(n)
            if n + 1 < len(jobs):
                load(n + 1)
            b.wait(cast_eng, f"{tag}_l{sl}", lh[n])
            b.wait(cast_eng, f"{tag}_s{sl}", b.tot(f"{tag}_s{sl}"))
            if cast_eng == "scalar":
                b.op(cast_eng, lambda h, a=src16, bb=src32: h.copy(out=a, in_=bb),
                     f"{tag}_c")
            else:
                b.op(cast_eng, lambda h, a=src16, bb=src32: h.tensor_copy(
                    out=a, in_=bb), f"{tag}_c")
            b.wait("gpsimd", f"{tag}_c", b.tot(f"{tag}_c"))
            b.dma("gpsimd", dap, src16, f"{tag}_s{sl}")
            if chunk and n % chunk == chunk - 1:
                yield

    def emit_wprep(self, v):
        b, c = self, self.cfg
        D, I, TOKB, ND, NI = c["D"], c["I"], c["TOKB"], c["ND"], c["NI"]
        jobs = []
        for wsrc, wdst in ((self.w_up, self.wup16), (self.w_gate, self.wgt16),
                           (self.w_down, self.wdn16)):
            wid = wsrc.shape[1]
            cw = min(2048, wid)
            for r in range(wsrc.shape[0] // 128):
                for col in range(max(1, wid // cw)):
                    jobs.append((wsrc[r * 128:(r + 1) * 128, col * cw:(col + 1) * cw],
                                 wdst[r * 128:(r + 1) * 128, col * cw:(col + 1) * cw]))
        yield from self._cast_jobs(jobs, "wp", chunk=8, cast_eng="vector")
        for eng in ("sync", "gpsimd"):
            b.wait(eng, "wp_s0", b.tot("wp_s0"))
            b.wait(eng, "wp_s1", b.tot("wp_s1"))
        tjobs = []
        for wsrc, wdst in ((self.wup16, self.wupT), (self.wgt16, self.wgtT)):
            for ug in range(I // TOKB):
                for dc in range(ND):
                    tjobs.append((wsrc[ug * TOKB:(ug + 1) * TOKB,
                                       dc * 128:(dc + 1) * 128], wdst[ug, dc]))
        for a in range(NI):
            for ob in range(D // TOKB):
                tjobs.append((self.wdn16[ob * TOKB:(ob + 1) * TOKB,
                                         a * 128:(a + 1) * 128],
                              self.wdnT[a][:, ob * TOKB:(ob + 1) * TOKB]))
        tl_hist = {}

        def load_t(n):
            sl = n % 16
            b.wait("sync", f"wt_s{sl % 2}", b.tot(f"wt_s{sl % 2}"))
            b.dma("sync", self.wtT[:, sl, 0:TOKB], tjobs[n][0],
                  f"wt_l{sl % 2}", transpose=True)
            tl_hist[n] = b.tot(f"wt_l{sl % 2}")

        for k in range(min(12, len(tjobs))):
            load_t(k)
        for n in range(len(tjobs)):
            sl = n % 16
            if n + 12 < len(tjobs):
                load_t(n + 12)
            b.wait("gpsimd", f"wt_l{sl % 2}", tl_hist[n])
            b.dma("gpsimd", tjobs[n][1], self.wtT[:, sl, 0:TOKB], f"wt_s{sl % 2}")
            if n % 16 == 15:
                yield
        b.wait("sync", "wt_s0", b.tot("wt_s0"))
        b.wait("sync", "wt_s1", b.tot("wt_s1"))
        ja = [(a, ob) for a in range(NI) for ob in range(D // TOKB)]
        wal_hist = {}

        def load_a(n):
            a, ob = ja[n]
            sl = n % 16
            b.wait("sync", f"wa_m{sl % 2}", b.tot(f"wa_m{sl % 2}"))
            b.dma("sync", self.wtT[:, sl, 0:TOKB],
                  self.wdnT[a][:, ob * TOKB:(ob + 1) * TOKB], f"wa_l{sl % 2}")
            wal_hist[n] = b.tot(f"wa_l{sl % 2}")

        for k in range(min(12, len(ja))):
            load_a(k)
        for n in range(len(ja)):
            a, ob = ja[n]
            sl = n % 16
            esl = n % 2
            if n + 12 < len(ja):
                load_a(n + 12)
            b.wait("tensor", f"wa_l{sl % 2}", wal_hist[n])
            b.wait("tensor", f"wa_e{esl}", b.tot(f"wa_e{esl}"))
            b.op("tensor", lambda h, sl=sl, esl=esl: h.matmul(
                out=self.psA[:, esl, 0:TOKB], lhsT=self.c_h128[:],
                rhs=self.wtT[:, sl, 0:TOKB], start=True, stop=True,
                skip_group_check=True), f"wa_m{sl % 2}")
            b.wait("vector", f"wa_m{sl % 2}", b.tot(f"wa_m{sl % 2}"))
            b.wait("vector", f"wa_s{esl}", b.tot(f"wa_s{esl}"))
            b.op("vector", lambda h, esl=esl: h.tensor_copy(
                out=self.ubuf[:, esl, 0:TOKB], in_=self.psA[:, esl, 0:TOKB]),
                f"wa_e{esl}")
            b.wait("gpsimd", f"wa_e{esl}", b.tot(f"wa_e{esl}"))
            b.dma("gpsimd", self.u_s[a][:, ob * TOKB:(ob + 1) * TOKB],
                  self.ubuf[:, esl, 0:TOKB], f"wa_s{esl}")
            if n % 16 == 15:
                yield
        b.wait("sync", "wa_s0", b.tot("wa_s0"))
        b.wait("sync", "wa_s1", b.tot("wa_s1"))
        jb = [(bb, ob) for bb in range(128) for ob in range(D // TOKB)]
        wbl_hist = {}
        TS4 = TOKB // 128

        def load_b(n):
            bb, ob = jb[n]
            sl = n % 12
            b.wait("sync", f"wb_m{sl % 2}", b.tot(f"wb_m{sl % 2}"))
            b.dma("sync", self.sbB[:, sl, 0:TOKB],
                  self.u_s[:, bb, ob * TOKB:(ob + 1) * TOKB], f"wb_l{sl % 2}")
            wbl_hist[n] = b.tot(f"wb_l{sl % 2}")

        for k in range(min(10, len(jb))):
            load_b(k)
        for n in range(len(jb)):
            bb, ob = jb[n]
            sl = n % 12
            esl = n % 2
            if n + 10 < len(jb):
                load_b(n + 10)
            b.wait("tensor", f"wb_l{sl % 2}", wbl_hist[n])
            b.wait("tensor", f"wb_e{esl}", b.tot(f"wb_e{esl}"))
            b.op("tensor", lambda h, sl=sl, esl=esl: h.matmul(
                out=self.psB[0:NI, esl, 0:TOKB], lhsT=self.c_hni[:],
                rhs=self.sbB[0:NI, sl, 0:TOKB], start=True, stop=True,
                skip_group_check=True), f"wb_m{sl % 2}")
            b.wait("vector", f"wb_m{sl % 2}", b.tot(f"wb_m{sl % 2}"))
            b.wait("vector", f"wb_s{esl}", b.tot(f"wb_s{esl}"))
            b.op("vector", lambda h, esl=esl: h.tensor_copy(
                out=self.gout[:, esl, 0:TOKB], in_=self.psB[0:NI, esl, 0:TOKB]),
                f"wb_e{esl}")
            b.wait("gpsimd", f"wb_e{esl}", b.tot(f"wb_e{esl}"))
            b.dma("gpsimd",
                  self.g_s.rearrange("c i b o -> i b c o")[
                      :, bb, ob * TS4:(ob + 1) * TS4, :],
                  self.gout[0:NI, esl, 0:TOKB].rearrange(
                      "a (c o) -> a c o", o=128), f"wb_s{esl}")
            if n % 16 == 15:
                yield

    def emit_p1(self, v, wgen=None):
        b, c = self, self.cfg
        H, TOKB, S = c["H"], c["TOKB"], c["S"]
        s_v, e_v = c["bounds"][v], c["bounds"][v + 1]
        n_v = e_v - s_v
        TS = TOKB // 128
        nqb = n_v // TOKB
        gp = 0
        gq = 0
        exp_hist, dacc_hist, av_hist, dm_hist = {}, {}, {}, {}
        self._crow_hist = {}
        after = {0: {}, 1: {}}
        lv_hist = {}
        dz_hist, pd_hist = {}, {}
        pend_close = []

        def load_head(hh):
            hs = hh % 2
            lsem = f"p1l{hs}"
            for key, val in after[hs].items():
                b.wait("sync", key, val)
            b.dma("sync", self.kbuf[:, hs, 0:e_v], self.k_s[hh][:, 0:e_v], lsem)
            b.dma("sync", self.vbuf[:, hs, 0:e_v // 128, :],
                  self.v_s[hh].rearrange("(s p) e -> p s e", p=128)[:, 0:e_v // 128, :],
                  lsem)
            b.dma("sync", self.qbuf[:, hs, 0:n_v], self.q_s[hh][:, 0:n_v], lsem)
            b.dma("sync", self.qnrow[0:1, hs, 0:n_v], self.qn_s[hh:hh + 1, 0:n_v], lsem)
            lv_hist[hh] = b.tot(lsem)

        load_head(0)
        for hh in range(H):
            hs = hh % 2
            lsem = f"p1l{hs}"
            if hh + 1 < H:
                load_head(hh + 1)
            lv = lv_hist[hh]
            for eng in ("tensor", "vector"):
                b.wait(eng, lsem, lv)
            for qbi in range(nqb):
                qa = s_v + qbi * TOKB
                n_k = (qa + TOKB) // 128
                npairs = n_k // 2
                cs = qbi % 2
                qpar = gq % 2
                psAV = [self.psC, self.psF][qpar]
                dslice = self.dacc[:, qpar]
                b.wait("vector", "p1s", self._crow_hist.get(cs, 0))
                b.op("vector", lambda h, hs=hs, qbi=qbi, hh=hh, cs=cs: h.tensor_scalar(
                    out=self.crow[0:1, cs, 0:TOKB],
                    in0=self.qnrow[0:1, hs, qbi * TOKB:(qbi + 1) * TOKB],
                    scalar1=self.mkneg[0:1, hh:hh + 1],
                    scalar2=C_OFF * c["SQ"], op0=ALU.mult, op1=ALU.add), "d_cr")
                b.wait("tensor", "d_cr", b.tot("d_cr"))

                pend = None

                def emit_av(srcs, lp, gpp, psAV=psAV, qpar=qpar, n_k=n_k, hs=hs):
                    if lp == 0:
                        b.wait("tensor", "d_z", dz_hist.get(qpar, 0))
                    for j in range(2):
                        kc = 2 * lp + j
                        b.op("tensor", lambda h, hs=hs, kc=kc, sj=srcs[j], lp=lp,
                             j=j, n_k=n_k, psAV=psAV: h.matmul(
                                 out=psAV[:, 0:TOKB],
                                 lhsT=self.vbuf[:, hs, kc, :],
                                 rhs=sj, start=(lp == 0 and j == 0),
                                 stop=(kc == n_k - 1),
                                 skip_group_check=True), "p_av" if j == 1 else None)
                    av_hist[gpp] = b.tot("p_av")

                for lp in range(npairs):
                    psX = [self.psA, self.psB][gp % 2]
                    se = gp % 3
                    if gp >= 2:
                        b.wait("tensor", "a_e", exp_hist[gp - 2])
                    for j in range(2):
                        kc = 2 * lp + j
                        b.op("tensor", lambda h, psX=psX, j=j, hs=hs, kc=kc, qbi=qbi:
                             h.matmul(out=psX[:, j, 0:TOKB],
                                      lhsT=self.kbuf[:, hs, kc * 128:(kc + 1) * 128],
                                      rhs=self.qbuf[:, hs, qbi * TOKB:(qbi + 1) * TOKB],
                                      start=True, stop=False, skip_group_check=True))
                        b.op("tensor", lambda h, psX=psX, j=j, cs=cs: h.matmul(
                            out=psX[:, j, 0:TOKB], lhsT=self.c_o1x128[:],
                            rhs=self.crow[0:1, cs, 0:TOKB], start=False, stop=True,
                            skip_group_check=True), "p1s" if j == 1 else None)
                    if pend is not None:
                        b.wait("tensor", "a_e", exp_hist[gp - 1])
                        b.wait("tensor", "d_m", dm_hist.get(gp - 1, 0))
                        emit_av(*pend)
                        pend = None
                    b.wait("scalar", "p1s", b.tot("p1s"))
                    if gp >= 3:
                        b.wait("scalar", "d_d", dacc_hist[gp - 3])
                        b.wait("scalar", "p_av", av_hist.get(gp - 3) or 0)
                    b.op("scalar", lambda h, psX=psX, se=se: h.activation(
                        out=self.ebuf[:, se], in_=psX[:, :, 0:TOKB], func=AF.Exp,
                        scale=1.0 / c["SQ"]), "a_e")
                    exp_hist[gp] = b.tot("a_e")
                    b.wait("vector", "a_e", b.tot("a_e"))
                    srcs = []
                    ep = gp % 2
                    for j in range(2):
                        kc = 2 * lp + j
                        if kc >= qa // 128:
                            m = kc - qa // 128
                            b.wait("vector", "p_av", b.tot("p_av"))
                            b.op("vector", lambda h, se=se, j=j, m=m, ep=ep:
                                 h.tensor_tensor(
                                     out=self.emsk[:, ep, j], in0=self.ebuf[:, se, j],
                                     in1=self.c_masks[:, m, :], op=ALU.mult), "d_m")
                            b.wait("vector", "d_m", b.tot("d_m"))
                            srcs.append(self.emsk[:, ep, j])
                        else:
                            srcs.append(self.ebuf[:, se, j])
                    dm_hist[gp] = b.tot("d_m")
                    for j in range(2):
                        if lp == 0 and j == 0:
                            b.wait("vector", "p_d", pd_hist.get(qpar, 0))
                            b.op("vector", lambda h, s0=srcs[0], dslice=dslice:
                                 h.tensor_copy(out=dslice, in_=s0), "d_d")
                        else:
                            b.wait("vector", "d_d", b.tot("d_d"))
                            b.op("vector", lambda h, sj=srcs[j], dslice=dslice:
                                 h.tensor_tensor(
                                     out=dslice, in0=dslice,
                                     in1=sj, op=ALU.add), "d_d")
                    dacc_hist[gp] = b.tot("d_d")
                    pend = (srcs, lp, gp)
                    gp += 1
                    if pend_close and lp <= 1:
                        pend_close.pop(0)()
                b.wait("tensor", "a_e", exp_hist[gp - 1])
                b.wait("tensor", "d_m", dm_hist.get(gp - 1, 0))
                emit_av(*pend)
                self._crow_hist[cs] = b.tot("p1s")
                while pend_close:
                    pend_close.pop(0)()
                dd_tot = b.tot("d_d")
                pav_tot = b.tot("p_av")
                zs = qbi % 2

                def close1(dd_tot=dd_tot, dslice=dslice, qpar=qpar):
                    b.wait("tensor", "d_d", dd_tot)
                    b.wait("tensor", "a_l", b.tot("a_l"))
                    b.op("tensor", lambda h, dslice=dslice: h.matmul(
                        out=self.psE[0:1, 0:TOKB], lhsT=self.c_ones_c32[:],
                        rhs=dslice, start=True, stop=True,
                        skip_group_check=True), "p_d")
                    pd_hist[qpar] = b.tot("p_d")
                    b.wait("scalar", "p_d", b.tot("p_d"))
                    b.op("scalar", lambda h: h.activation(
                        out=self.lnrow[0:1, 0:TOKB], in_=self.psE[0:1, 0:TOKB],
                        func=AF.Ln), "a_l")
                    b.wait("scalar", "a_l", b.tot("a_l"))
                    b.wait("scalar", "d_c", b.tot("d_c"))
                    b.op("scalar", lambda h: h.activation(
                        out=self.dinv32[0:1, 0:TOKB], in_=self.lnrow[0:1, 0:TOKB],
                        func=AF.Exp, scale=-1.0), "a_l")
                    b.wait("vector", "a_l", b.tot("a_l"))
                    b.wait("vector", "p_b", b.tot("p_b"))
                    b.op("vector", lambda h: h.tensor_copy(
                        out=self.dinv16[0:1, 0:TOKB],
                        in_=self.dinv32[0:1, 0:TOKB]), "d_c")

                def close2(pav_tot=pav_tot, psAV=psAV, zs=zs, hh=hh, qbi=qbi,
                           qpar=qpar):
                    b.wait("tensor", "d_c", b.tot("d_c"))
                    b.wait("tensor", "d_bv", b.tot("d_bv"))
                    b.op("tensor", lambda h: h.matmul(
                        out=self.psD[:, 0:TOKB], lhsT=self.c_o1x128[:],
                        rhs=self.dinv16[0:1, 0:TOKB], start=True, stop=True,
                        skip_group_check=True), "p_b")
                    b.wait("vector", "p_b", b.tot("p_b"))
                    b.op("vector", lambda h: h.tensor_copy(
                        out=self.dinvb[:, 0:TOKB], in_=self.psD[:, 0:TOKB]), "d_bv")
                    b.wait("vector", "d_bv", b.tot("d_bv"))
                    b.wait("vector", "p_av", pav_tot)
                    b.wait("vector", f"z_st{zs}", b.tot(f"z_st{zs}"))
                    b.op("vector", lambda h, zs=zs, psAV=psAV: h.tensor_tensor(
                        out=self.z16[:, zs], in0=psAV[:, 0:TOKB],
                        in1=self.dinvb[:, 0:TOKB], op=ALU.mult), "d_z")
                    dz_hist[qpar] = b.tot("d_z")
                    b.wait("gpsimd", "d_z", b.tot("d_z"))
                    b.dma("gpsimd", self.z_s[hh][:, qbi * TOKB:(qbi + 1) * TOKB],
                          self.z16[:, zs], f"z_st{zs}")

                pend_close = [close1, close2]
                gq += 1
            after[hs] = {"p1s": b.tot("p1s"), "p_av": b.tot("p_av"),
                         "d_cr": b.tot("d_cr")}
            if wgen is not None:
                for _ in range(7):
                    if next(wgen, "END") == "END":
                        wgen = None
                        break
        while pend_close:
            pend_close.pop(0)()

    def emit_p23(self, v):
        b, c = self, self.cfg
        D, I, H, TOKB, ND, NI = c["D"], c["I"], c["H"], c["TOKB"], c["ND"], c["NI"]
        s_v, e_v = c["bounds"][v], c["bounds"][v + 1]
        n_v = e_v - s_v
        TS = TOKB // 128
        nst = int(math.log2(ND))
        ntb = n_v // TOKB
        wob_hist, psA_hist = {}, {}
        p3u_hist, p3g_hist, p3h_hist, h2_hist = {}, {}, {}, {}
        for tb in range(ntb):
            t0 = tb * TOKB
            b.wait("sync", "p3dn", b.tot("p3dn"))
            b.wait("sync", "d_u", b.tot("d_u"))
            for hh2 in range(H):
                b.dma("sync", self.z1[:, hh2, 0:TOKB],
                      self.z_s[hh2][:, t0:t0 + TOKB], "zl")
            zl_tot = b.tot("zl")
            owl_hist = {}

            def load_oproj(dc):
                slw = dc % 2
                if slw in wob_hist:
                    b.wait("scalar", "p2o", wob_hist[slw])
                b.dma("scalar", self.wob[:, slw, :].rearrange("p (hh d) -> p hh d", d=128),
                      self.wo16s.rearrange("hh p d -> p hh d")[:, :, dc * 128:(dc + 1) * 128],
                      f"p2wl{slw}")
                b.wait("sync", "dve_y", psA_hist.get(slw, 0))
                b.dma("sync", self.xst[:, slw, 0:TOKB],
                      self.xfm_s[dc][:, t0:t0 + TOKB], f"p2xl{slw}")
                owl_hist[dc] = (b.tot(f"p2wl{slw}"), b.tot(f"p2xl{slw}"))

            load_oproj(0)
            for dc in range(ND):
                slw = dc % 2
                if dc + 1 < ND:
                    load_oproj(dc + 1)
                b.wait("tensor", f"p2wl{slw}", owl_hist[dc][0])
                b.wait("tensor", f"p2xl{slw}", owl_hist[dc][1])
                if dc == 0:
                    b.wait("tensor", "zl", zl_tot)
                b.wait("tensor", "dve_y", psA_hist.get(slw, 0))
                for hh2 in range(H):
                    b.op("tensor", lambda h, slw=slw, hh2=hh2: h.matmul(
                        out=self.psA[:, slw, 0:TOKB],
                        lhsT=self.wob[:, slw, hh2 * 128:(hh2 + 1) * 128],
                        rhs=self.z1[:, hh2, 0:TOKB],
                        start=(hh2 == 0), stop=(hh2 == H - 1),
                        skip_group_check=True), "p2o" if hh2 == H - 1 else None)
                wob_hist[slw] = b.tot("p2o")
                b.wait("vector", "p2o", b.tot("p2o"))
                b.op("vector", lambda h, slw=slw, dc=dc: h.scalar_tensor_tensor(
                    out=self.y32[:, dc, 0:TOKB], in0=self.psA[:, slw, 0:TOKB],
                    scalar=self.v_bo[:, dc:dc + 1], in1=self.xst[:, slw, 0:TOKB],
                    op0=ALU.add, op1=ALU.add), "dve_y")
                psA_hist[slw] = b.tot("dve_y")
                b.wait("scalar", "dve_y", b.tot("dve_y"))
                b.wait("scalar", "p2s2", b.tot("p2s2"))
                b.op("scalar", lambda h, dc=dc: h.square(
                    out=self.ysq[:, dc % 2], in_=self.y32[:, dc, 0:TOKB]), "a_sq")
                b.wait("tensor", "dve_y", b.tot("dve_y"))
                b.op("tensor", lambda h, dc=dc: h.matmul(
                    out=self.psE[0:1, 0:TOKB], lhsT=self.c_ones_c32[:],
                    rhs=self.y32[:, dc, 0:TOKB], start=(dc == 0), stop=(dc == ND - 1),
                    skip_group_check=True), "p2sy")
                b.wait("tensor", "a_sq", b.tot("a_sq"))
                b.op("tensor", lambda h, dc=dc: h.matmul(
                    out=self.psF[0:1, 0:TOKB], lhsT=self.c_ones_c32[:],
                    rhs=self.ysq[:, dc % 2], start=(dc == 0), stop=(dc == ND - 1),
                    skip_group_check=True), "p2s2")
            self._ln_rows(v, "p2sy", "p2s2", D)
            for dc in range(ND):
                hsl = dc % 2
                if dc == 0:
                    b.wait("vector", "d_b", b.tot("d_b"))
                b.op("vector", lambda h, dc=dc: h.tensor_sub(
                    out=self.ysq[:, dc % 2], in0=self.y32[:, dc, 0:TOKB],
                    in1=self.mb[:, 0:TOKB]), "g_t")
                b.wait("vector", f"h_st{hsl}", b.tot(f"h_st{hsl}"))
                b.op("vector", lambda h, hsl=hsl, dc=dc: h.tensor_mul(
                    out=self.hst[:, hsl, 0:TOKB], in0=self.ysq[:, dc % 2],
                    in1=self.rb[:, 0:TOKB]), "d_h")
                b.op("vector", lambda h, dc=dc, hsl=hsl: h.tensor_scalar_mul(
                    out=self.hb16[:, 0, dc, 0:TOKB], in0=self.hst[:, hsl, 0:TOKB],
                    scalar1=self.v_su_ug[:, dc:dc + 1]), "g_su")
                b.wait("gpsimd", "d_h", b.tot("d_h"))
                b.dma("gpsimd", self.h_s[dc][:, t0:t0 + TOKB],
                      self.hst[:, hsl, 0:TOKB], f"h_st{hsl}")
            b.wait("vector", "g_su", b.tot("g_su"))
            b.wait("vector", "p3u", b.tot("p3u"))
            for st in range(nst):
                bit = 1 << st
                if st > 0:
                    b.wait("vector", "d_f", b.tot("d_f"))
                for dc in range(ND):
                    src0 = dc & ~bit
                    src1 = dc | bit
                    if dc & bit:
                        b.op("vector", lambda h, st=st, dc=dc, src0=src0, src1=src1:
                             h.tensor_sub(
                                 out=self.hb16[:, (st + 1) % 2, dc, 0:TOKB],
                                 in0=self.hb16[:, st % 2, src0, 0:TOKB],
                                 in1=self.hb16[:, st % 2, src1, 0:TOKB]), "d_f")
                    else:
                        b.op("vector", lambda h, st=st, dc=dc, src0=src0, src1=src1:
                             h.tensor_add(
                                 out=self.hb16[:, (st + 1) % 2, dc, 0:TOKB],
                                 in0=self.hb16[:, st % 2, src0, 0:TOKB],
                                 in1=self.hb16[:, st % 2, src1, 0:TOKB]), "d_f")
            fb = nst % 2
            b.wait("tensor", "d_f", b.tot("d_f"))
            for dc in range(ND):
                b.wait("tensor", "d_h2", h2_hist.get(dc % 2, 0))
                b.wait("tensor", "dve_y", psA_hist.get(dc % 2, 0))
                b.op("tensor", lambda h, dc=dc, fb=fb: h.matmul(
                    out=self.psA[:, dc % 2, 0:TOKB], lhsT=self.c_h128[:],
                    rhs=self.hb16[:, fb, dc, 0:TOKB], start=True, stop=True,
                    skip_group_check=True), "p2h")
                b.wait("vector", "p2h", b.tot("p2h"))
                b.op("vector", lambda h, dc=dc: h.tensor_copy(
                    out=self.h2b[:, dc, 0:TOKB], in_=self.psA[:, dc % 2, 0:TOKB]),
                    "d_h2")
                h2_hist[dc % 2] = b.tot("d_h2")
            b.wait("tensor", "d_h2", b.tot("d_h2"))
            b.wait("tensor", "a_g", b.tot("a_g"))
            NH = NI // 2
            uwl_hist = {}

            def load_ug(uc):
                sl = uc % 2
                ug, wi = uc // TS, uc % TS
                if uc >= 2:
                    b.wait("sync", "d_u", p3u_hist.get(sl, 0))
                    b.wait("gpsimd", "d_u", p3u_hist.get(sl, 0))
                b.dma("sync", self.wtb[:, sl, 0],
                      self.wupT[ug].rearrange("cc p u -> p cc u")[
                          :, :, wi * 128:(wi + 1) * 128], f"p3wl{sl}")
                b.dma("gpsimd", self.wtb[:, sl, 1],
                      self.wgtT[ug].rearrange("cc p u -> p cc u")[
                          :, :, wi * 128:(wi + 1) * 128], f"p3wg{sl}")
                uwl_hist[uc] = (b.tot(f"p3wl{sl}"), b.tot(f"p3wg{sl}"))

            for hf in range(2):
                b.wait("tensor", "d_y0", b.tot("d_y0"))
                b.wait("tensor", "d_y2", b.tot("d_y2"))
                if hf == 0:
                    load_ug(0)
                for uci in range(NH):
                    uc = hf * NH + uci
                    sl = uc % 2
                    if uc + 1 < NI:
                        load_ug(uc + 1)
                    b.wait("tensor", f"p3wl{sl}", uwl_hist[uc][0])
                    b.wait("tensor", f"p3wg{sl}", uwl_hist[uc][1])
                    if uc >= 2:
                        b.wait("tensor", "d_u", p3u_hist.get(sl, 0))
                    for dc in range(ND):
                        b.op("tensor", lambda h, sl=sl, dc=dc: h.matmul(
                            out=self.psA[:, sl, 0:TOKB],
                            lhsT=self.wtb[:, sl, 0, dc],
                            rhs=self.h2b[:, dc, 0:TOKB], start=(dc == 0),
                            stop=(dc == ND - 1), skip_group_check=True))
                    for dc in range(ND):
                        b.op("tensor", lambda h, sl=sl, dc=dc: h.matmul(
                            out=self.psB[:, sl, 0:TOKB],
                            lhsT=self.wtb[:, sl, 1, dc],
                            rhs=self.h2b[:, dc, 0:TOKB], start=(dc == 0),
                            stop=(dc == ND - 1), skip_group_check=True),
                            "p3u" if dc == ND - 1 else None)
                    b.wait("scalar", "p3u", b.tot("p3u"))
                    b.wait("scalar", "d_u", b.tot("d_u"))
                    b.op("scalar", lambda h, sl=sl, uc=uc: h.activation(
                        out=self.g16[:, 0:TOKB], in_=self.psB[:, sl, 0:TOKB],
                        func=AF.Silu, scale=self.v_sgt[:, uc:uc + 1]), "a_g")
                    b.wait("vector", "a_g", b.tot("a_g"))
                    b.op("vector", lambda h, sl=sl, uc=uc, uci=uci: h.scalar_tensor_tensor(
                        out=self.z1[:, uci, 0:TOKB], in0=self.psA[:, sl, 0:TOKB],
                        scalar=self.v_sup[:, uc:uc + 1], in1=self.g16[:, 0:TOKB],
                        op0=ALU.mult, op1=ALU.mult), "d_u")
                    p3u_hist[sl] = b.tot("d_u")
                b.wait("tensor", "d_u", b.tot("d_u"))
                gl_hist = {}
                hl_hist = {}

                def load_g(oc, q):
                    s2 = (oc * 2 + q) % 2
                    b.wait("scalar", "p3dn", p3g_hist.get(s2, 0))
                    qa0 = hf * NH + q * (NI // 4)
                    b.dma("scalar", self.gwb[:, s2],
                          self.g_s[oc, qa0:qa0 + NI // 4].rearrange(
                              "i b o -> b i o"), f"p3gl{s2}")
                    gl_hist[(oc, q)] = b.tot(f"p3gl{s2}")

                def load_h(oc):
                    so = oc % 2
                    b.wait("gpsimd", "d_y2", p3h_hist.get(so, 0))
                    b.dma("gpsimd", self.hst[:, so, 0:TOKB],
                          self.h_s[oc][:, t0:t0 + TOKB], f"p3hl{so}")
                    hl_hist[oc] = b.tot(f"p3hl{so}")

                load_g(0, 0)
                if hf == 1:
                    b.wait("gpsimd", "h_st0", b.tot("h_st0"))
                    b.wait("gpsimd", "h_st1", b.tot("h_st1"))
                    load_h(0)
                for oc in range(ND):
                    so = oc % 2
                    if oc + 1 < ND:
                        if hf == 1:
                            load_h(oc + 1)
                    b.wait("tensor", "d_y0", b.tot("d_y0"))
                    b.wait("tensor", "d_y2", b.tot("d_y2"))
                    for q in range(2):
                        s2 = (oc * 2 + q) % 2
                        if (oc, q) != (ND - 1, 1):
                            nxt = (oc, q + 1) if q == 0 else (oc + 1, 0)
                            load_g(*nxt)
                        b.wait("tensor", f"p3gl{s2}", gl_hist[(oc, q)])
                        for i in range(NI // 4):
                            uci = q * (NI // 4) + i
                            b.op("tensor", lambda h, s2=s2, i=i, oc=oc, uci=uci:
                                 h.matmul(
                                     out=self.psA[:, oc % 2, 0:TOKB],
                                     lhsT=self.gwb[:, s2, i, :],
                                     rhs=self.z1[:, uci, 0:TOKB],
                                     start=(uci == 0), stop=(uci == NH - 1),
                                     skip_group_check=True),
                                 "p3dn" if i == NI // 4 - 1 else None)
                        p3g_hist[s2] = b.tot("p3dn")
                    b.wait("vector", "p3dn", b.tot("p3dn"))
                    if hf == 0:
                        b.op("vector", lambda h, oc=oc: h.tensor_scalar_mul(
                            out=self.y32[:, oc, 0:TOKB],
                            in0=self.psA[:, oc % 2, 0:TOKB],
                            scalar1=self.v_sdn[:, oc:oc + 1]), "d_y0")
                    else:
                        b.wait("vector", f"p3hl{so}", hl_hist[oc])
                        b.op("vector", lambda h, oc=oc, so=so: h.scalar_tensor_tensor(
                            out=self.ysq[:, oc % 2], in0=self.psA[:, oc % 2, 0:TOKB],
                            scalar=self.v_sdn[:, oc:oc + 1], in1=self.hst[:, so, 0:TOKB],
                            op0=ALU.mult, op1=ALU.add), "d_y2")
                        b.wait("vector", "d_y2", b.tot("d_y2"))
                        b.op("vector", lambda h, oc=oc: h.tensor_tensor(
                            out=self.y32[:, oc, 0:TOKB], in0=self.y32[:, oc, 0:TOKB],
                            in1=self.ysq[:, oc % 2], op=ALU.add), "d_y2")
                        p3h_hist[so] = b.tot("d_y2")
                        b.wait("scalar", "d_y2", b.tot("d_y2"))
                        b.wait("scalar", "p3s2", b.tot("p3s2"))
                        b.wait("scalar", "d_h", b.tot("d_h"))
                        b.op("scalar", lambda h, oc=oc: h.square(
                            out=self.ysq[:, oc % 2], in_=self.y32[:, oc, 0:TOKB]),
                            "a_sq")
                        b.wait("tensor", "d_y2", b.tot("d_y2"))
                        b.op("tensor", lambda h, oc=oc: h.matmul(
                            out=self.psE[0:1, 0:TOKB], lhsT=self.c_ones_c32[:],
                            rhs=self.y32[:, oc, 0:TOKB], start=(oc == 0),
                            stop=(oc == ND - 1), skip_group_check=True), "p3sy")
                        b.wait("tensor", "a_sq", b.tot("a_sq"))
                        b.op("tensor", lambda h, oc=oc: h.matmul(
                            out=self.psF[0:1, 0:TOKB], lhsT=self.c_ones_c32[:],
                            rhs=self.ysq[:, oc % 2], start=(oc == 0),
                            stop=(oc == ND - 1), skip_group_check=True), "p3s2")
            self._ln_rows(v, "p3sy", "p3s2", D)
            trb_hist = {}
            dot_hist = {}
            for dc in range(ND):
                osl = dc % 2
                pstr = [self.psF, self.psC][osl]
                if dc == 0:
                    b.wait("vector", "d_b", b.tot("d_b"))
                b.op("vector", lambda h, dc=dc: h.tensor_sub(
                    out=self.ysq[:, dc % 2], in0=self.y32[:, dc, 0:TOKB],
                    in1=self.mb[:, 0:TOKB]), "g_t2")
                b.wait("vector", "p3tr", trb_hist.get(osl, 0))
                b.op("vector", lambda h, osl=osl, dc=dc: h.tensor_mul(
                    out=self.oap[:, osl, 0:TOKB], in0=self.ysq[:, dc % 2],
                    in1=self.rb[:, 0:TOKB]), "d_o")
                b.wait("tensor", "d_o", b.tot("d_o"))
                b.wait("tensor", "d_ot", dot_hist.get(osl, 0))
                for ts in range(TS):
                    b.op("tensor", lambda h, osl=osl, ts=ts, pstr=pstr: h.transpose(
                        out=pstr[:, ts * 128:(ts + 1) * 128],
                        in_=self.oap[:, osl, ts * 128:(ts + 1) * 128],
                        identity=self.c_eye32[:]), "p3tr" if ts == TS - 1 else None)
                trb_hist[osl] = b.tot("p3tr")
                b.wait("vector", "p3tr", b.tot("p3tr"))
                b.wait("vector", f"o_st{osl}", b.tot(f"o_st{osl}"))
                b.op("vector", lambda h, osl=osl, pstr=pstr: h.tensor_copy(
                    out=self.ot32[:, osl, 0:TS, :],
                    in_=pstr[:, 0:TS * 128].rearrange("p (s e) -> p s e", e=128)),
                    "d_ot")
                dot_hist[osl] = b.tot("d_ot")
                b.wait("gpsimd", "d_ot", b.tot("d_ot"))
                b.dma("gpsimd",
                      self.out.rearrange("(t p) d -> p t d", p=128)[
                          :, tb * TS:(tb + 1) * TS, dc * 128:(dc + 1) * 128],
                      self.ot32[:, osl, 0:TS, :], f"o_st{osl}")
        b.wait("gpsimd", "o_st0", b.tot("o_st0"))
        b.wait("gpsimd", "o_st1", b.tot("o_st1"))

    def _ln_rows(self, v, sy, s2, D):
        b = self
        TOKB = self.cfg["TOKB"]
        b.wait("vector", sy, b.tot(sy))
        b.wait("vector", "a_r", b.tot("a_r"))
        b.op("vector", lambda h: h.tensor_scalar(
            out=self.mrow[0:1, 0:TOKB], in0=self.psE[0:1, 0:TOKB],
            scalar1=1.0 / D, scalar2=None, op0=ALU.mult), "d_r")
        b.wait("vector", s2, b.tot(s2))
        b.op("vector", lambda h: h.tensor_scalar(
            out=self.vrow[0:1, 0:TOKB], in0=self.psF[0:1, 0:TOKB],
            scalar1=1.0 / D, scalar2=None, op0=ALU.mult), "d_r")
        b.wait("scalar", "d_r", b.tot("d_r"))
        b.op("scalar", lambda h: h.square(
            out=self.wrow[0:1, 0:TOKB], in_=self.mrow[0:1, 0:TOKB]), "a_r")
        b.wait("vector", "a_r", b.tot("a_r"))
        b.op("vector", lambda h: h.scalar_tensor_tensor(
            out=self.vrow[0:1, 0:TOKB], in0=self.vrow[0:1, 0:TOKB],
            scalar=LN_EPS, in1=self.wrow[0:1, 0:TOKB],
            op0=ALU.add, op1=ALU.subtract), "d_r")
        b.wait("scalar", "d_r", b.tot("d_r"))
        b.op("scalar", lambda h: h.activation(
            out=self.wrow[0:1, 0:TOKB], in_=self.vrow[0:1, 0:TOKB],
            func=AF.Ln), "a_r")
        b.wait("scalar", "a_r", b.tot("a_r"))
        b.op("scalar", lambda h: h.activation(
            out=self.rrow[0:1, 0:TOKB], in_=self.wrow[0:1, 0:TOKB],
            func=AF.Exp, scale=-0.5), "a_r")
        b.wait("tensor", "d_r", b.tot("d_r"))
        b.wait("tensor", "a_r", b.tot("a_r"))
        b.wait("tensor", "d_b", b.tot("d_b"))
        b.wait("tensor", "d_ot", b.tot("d_ot"))
        b.op("tensor", lambda h: h.matmul(
            out=self.psD[:, 0:TOKB], lhsT=self.c_o1x128_32[:],
            rhs=self.mrow[0:1, 0:TOKB], start=True, stop=True,
            skip_group_check=True), "p_br")
        b.op("tensor", lambda h: h.matmul(
            out=self.psC[:, 0:TOKB], lhsT=self.c_o1x128_32[:],
            rhs=self.rrow[0:1, 0:TOKB], start=True, stop=True,
            skip_group_check=True), "p_br")
        b.wait("vector", "p_br", b.tot("p_br"))
        b.op("vector", lambda h: h.tensor_copy(
            out=self.mb[:, 0:TOKB], in_=self.psD[:, 0:TOKB]), "d_b")
        b.op("vector", lambda h: h.tensor_copy(
            out=self.rb[:, 0:TOKB], in_=self.psC[:, 0:TOKB]), "d_b")

    def build(self):
        b = self
        self.cur = self.pre_ops
        self.emit_pre()
        pre_snapshot = dict(self.cnt)
        for v in range(4):
            self.cnt = dict(pre_snapshot)
            self.cur = self.var_ops[v]
            self.barrier_pre()
            self.emit_p0(v)
            self._drain_gpsimd(["ks_st0", "ks_st1", "vt_st0", "vt_st1",
                                "qs_st0", "qs_st1", "qn_st0", "qn_st1",
                                "xfs", "mkn"])
            self.barrier("bar0")
            wgen = self.emit_wprep(v)
            self.emit_p1(v, wgen)
            for _ in wgen:
                pass
            self._drain_gpsimd(["wp_s0", "wp_s1", "wt_s0", "wt_s1",
                                "wa_s0", "wa_s1", "wb_s0", "wb_s1",
                                "z_st0", "z_st1"])
            self.barrier("bar2")
            self.emit_p23(v)
        with self.es, self.nc.Block() as block:
            def section(eng_name):
                def emit(h):
                    for fn in self.pre_ops[eng_name]:
                        fn(h)
                    idx = h.bass.partition_id() % 4
                    for vv in h.Switch(idx, 4):
                        for fn in self.var_ops[vv][eng_name]:
                            fn(h)
                return emit
            block.sync(section("sync"))
            block.tensor(section("tensor"))
            block.vector(section("vector"))
            block.scalar(section("scalar"))
            block.gpsimd(section("gpsimd"))
        return self.nc

    def _drain_gpsimd(self, semnames):
        for s in semnames:
            if self.tot(s) > 0:
                self.wait("gpsimd", s, self.tot(s))


def build_nc(cfg):
    kb = KB(cfg)
    return kb.build()


_NC_CACHE = {}


def kernel(L0_attn_x, L0_attn_W_Q, L0_attn_W_K, L0_attn_W_V, L0_attn_W_O,
           L0_attn_b_Q, L0_attn_b_K, L0_attn_b_V, L0_attn_b_O,
           L0_ffn_SU_ug, L0_ffn_SU_dp, L0_ffn_Wscale_ug, L0_ffn_Wscale_dp,
           L0_ffn_W_up, L0_ffn_W_gate, L0_ffn_W_down, _small_cfg=None,
           _run=None):
    cfg = _small_cfg or make_cfg()
    H, D, S, NM = cfg["H"], cfg["D"], cfg["S"], cfg["NMAX"]
    x = np.ascontiguousarray(np.asarray(L0_attn_x), dtype=np.float32)
    consts = host_consts(cfg)
    base = {
        "wq": np.ascontiguousarray(L0_attn_W_Q, dtype=np.float32),
        "wk": np.ascontiguousarray(L0_attn_W_K, dtype=np.float32),
        "wv": np.ascontiguousarray(L0_attn_W_V, dtype=np.float32),
        "wo": np.ascontiguousarray(L0_attn_W_O, dtype=np.float32),
        "bq": np.ascontiguousarray(L0_attn_b_Q, dtype=np.float32).reshape(1, H * 128),
        "bk": np.ascontiguousarray(L0_attn_b_K, dtype=np.float32).reshape(1, H * 128),
        "bv": np.ascontiguousarray(L0_attn_b_V, dtype=np.float32).reshape(1, H * 128),
        "bo": np.ascontiguousarray(L0_attn_b_O, dtype=np.float32).reshape(-1, 128),
        "su_ug": np.ascontiguousarray(L0_ffn_SU_ug, dtype=np.float32).reshape(-1, 128),
        "su_dp": np.ascontiguousarray(L0_ffn_SU_dp, dtype=np.float32).reshape(-1, 128),
        "ws_ug": np.ascontiguousarray(L0_ffn_Wscale_ug, dtype=np.float32).reshape(-1, 128),
        "ws_dp": np.ascontiguousarray(L0_ffn_Wscale_dp, dtype=np.float32).reshape(-1, 128),
        "w_up": np.ascontiguousarray(L0_ffn_W_up, dtype=np.float32),
        "w_gate": np.ascontiguousarray(L0_ffn_W_gate, dtype=np.float32),
        "w_down": np.ascontiguousarray(L0_ffn_W_down, dtype=np.float32),
    }
    base.update(consts)
    in_maps = []
    for core in range(8):
        m = dict(base)
        m["xin"] = np.ascontiguousarray(x[core // 4])
        in_maps.append(m)
    key = (cfg["S"], cfg["D"])
    if key not in _NC_CACHE:
        _NC_CACHE[key] = build_nc(cfg)
    nc = _NC_CACHE[key]
    if _run is not None:
        results = _run(nc, in_maps)
    else:
        res = run_bass_kernel_spmd(nc, in_maps, core_ids=list(range(8)))
        results = [res.results[i] for i in range(8)]
    outp = np.empty((2, S, D), dtype=np.float32)
    bounds = cfg["bounds"]
    for core in range(8):
        bb, v = core // 4, core % 4
        s_v, e_v = bounds[v], bounds[v + 1]
        outp[bb, s_v:e_v, :] = np.asarray(results[core]["out"]).reshape(NM, D)[
            0:e_v - s_v]
    return outp


# version 14
# speedup vs baseline: 1.1072x; 1.1072x over previous
import contextlib
import math

import numpy as np
import ml_dtypes

import concourse.bass as bass
import concourse.mybir as mybir
from concourse.bass_utils import run_bass_kernel_spmd

F32 = mybir.dt.float32
BF16 = mybir.dt.bfloat16
BF = ml_dtypes.bfloat16
AF = mybir.ActivationFunctionType
ALU = mybir.AluOpType

ENGS = ("sync", "tensor", "vector", "scalar", "gpsimd")
LN_EPS = 1e-5
C_OFF = 10.0


def make_cfg(small=False):
    c = {}
    if small:
        c.update(S=1024, D=512, I=1024, H=2, TOKB=256,
                 bounds=(0, 256, 512, 768, 1024))
    else:
        c.update(S=8192, D=2048, I=8192, H=16, TOKB=512,
                 bounds=(0, 2560, 4608, 6656, 8192))
    c["B"] = 2
    c["DH"] = 128
    c["ND"] = c["D"] // 128
    c["NI"] = c["I"] // 128
    c["NMAX"] = max(b - a for a, b in zip(c["bounds"], c["bounds"][1:]))
    c["SQ"] = math.sqrt(float(c["DH"]))
    c["GRP"] = min(1536, c["S"])
    return c


def hadamard(n):
    h = np.array([[1.0]], dtype=np.float32)
    while h.shape[0] < n:
        h = np.block([[h, h], [h, -h]])
    return h


def host_consts(cfg):
    TOKB = cfg["TOKB"]
    masks = np.zeros((4, 128, TOKB), dtype=np.float32)
    for m in range(4):
        a = m * 128
        for p in range(128):
            masks[m, p, :] = (a + p <= np.arange(TOKB)).astype(np.float32)
    return {
        "eye32": np.eye(128, dtype=np.float32),
        "eye16": np.eye(128, dtype=np.float32).astype(BF),
        "h128c": hadamard(128).astype(BF),
        "hnic": hadamard(cfg["NI"]).astype(BF),
        "masksc": masks.astype(BF),
        "ones_row": np.ones((1, TOKB), dtype=np.float32),
        "ones_r16": np.ones((1, TOKB), dtype=np.float32).astype(BF),
        "ones_c16": np.ones((128, 1), dtype=np.float32).astype(BF),
        "ones_c32": np.ones((128, 1), dtype=np.float32),
        "o1x128": np.ones((1, 128), dtype=np.float32).astype(BF),
        "o1x128_32": np.ones((1, 128), dtype=np.float32),
    }


def _esize(dt):
    return 2 if dt == BF16 else 4


class KB:
    def __init__(self, cfg):
        self.cfg = cfg
        self.nc = bass.Bass("TRN2", target_bir_lowering=False, debug=False)
        self.es = contextlib.ExitStack()
        self.sems = {}
        self.cnt = {}
        self.var_ops = [{e: [] for e in ENGS} for _ in range(4)]
        self.pre_ops = {e: [] for e in ENGS}
        self.post_ops = {e: [] for e in ENGS}
        self.cur = None
        self._persist_off = 0
        self._semmap = {}
        self._phase_base = None
        self._phase_off = {}
        self._declare_io()
        self._alloc()

    ARENA_BYTES = 190 * 1024

    def _salloc(self, name, shape, dtype, pool):
        if not hasattr(self, "_arena_base"):
            arena = self.nc.alloc_sbuf_tensor("arena0", [128, self.ARENA_BYTES],
                                              mybir.dt.uint8)
            self._arena_base = self.nc.lookup_mloc(arena).addr
            self._persist_off = self._arena_base
        nbytes = int(np.prod(shape[1:])) * _esize(dtype)
        nbytes = (nbytes + 31) & ~31
        if pool == "persist":
            off = self._persist_off
            self._persist_off = off + nbytes
        else:
            base = self._phase_base
            if pool in ("wp", "wp2"):
                base = self._phase_base + 110 * 1024
            off = self._phase_off.get(pool, base)
            self._phase_off[pool] = off + nbytes
            if pool == "p1":
                assert off + nbytes <= self._phase_base + 110 * 1024, (name, off + nbytes)
        assert off + nbytes <= self._arena_base + self.ARENA_BYTES, (
            name, pool, off + nbytes - self._arena_base)
        t = self.nc.alloc_sbuf_tensor_at(name, shape, dtype, offset=off)
        return t

    _ENG_PHYS = {"tensor": "sT", "vector": "sV", "scalar": "sS",
                 "gpsimd": "sG", "sync": "sQ"}
    _DMA_PHYS = {"sync": "dS", "scalar": "dA", "gpsimd": "dG",
                 "vector": "dV", "tensor": "dT"}

    def _phys(self, name):
        return self._semmap[name]

    def sem_handle(self, phys):
        if phys not in self.sems:
            self.sems[phys] = self.es.enter_context(self.nc.semaphore(phys))
            self.cnt.setdefault(phys, 0)
        return self.sems[phys]

    def _map_name(self, name, eng, is_dma):
        if name in self._semmap:
            return self._semmap[name]
        if name.startswith("bar"):
            phys = name
        elif is_dma:
            import zlib
            digit = int(name[-1]) if name[-1] in "01" else 0
            base = name.rstrip("01")
            parity = (2 * (zlib.crc32(base.encode()) % 2) + digit) % 4
            phys = self._DMA_PHYS[eng] + str(parity)
        else:
            phys = self._ENG_PHYS[eng]
        self._semmap[name] = phys
        return phys

    def inc(self, name, amt, eng=None, is_dma=False):
        phys = self._map_name(name, eng, is_dma)
        s = self.sem_handle(phys)
        self.cnt[phys] = self.cnt.get(phys, 0) + amt
        return s, self.cnt[phys]

    def tot(self, name):
        if name not in self._semmap:
            return 0
        return self.cnt.get(self._semmap[name], 0)

    def on(self, eng, fn):
        self.cur[eng].append(fn)

    def dma(self, eng, out, in_, semname, transpose=False):
        phys = self._map_name(semname, eng, True)
        prev = self.cnt.get(phys, 0)
        s = self.sem_handle(phys)
        if prev > 0:
            self.on(eng, lambda h, s=s, prev=prev: h.wait_ge(s, prev))
        self.cnt[phys] = prev + 16
        def fn(h, out=out, in_=in_, s=s, transpose=transpose):
            h.dma_start(out=out, in_=in_, transpose=transpose).then_inc(s, 16)
        self.on(eng, fn)
        return prev + 16

    def wait(self, eng, semname, val):
        if val <= 0:
            return
        if semname not in self._semmap:
            return
        s = self.sem_handle(self._semmap[semname])
        self.on(eng, lambda h, s=s, val=val: h.wait_ge(s, val))

    def op(self, eng, fn, semname=None, amt=1):
        if semname is None:
            self.on(eng, fn)
            return None
        s, v = self.inc(semname, amt, eng=eng)
        def wrapped(h, fn=fn, s=s, amt=amt):
            fn(h).then_inc(s, amt)
        self.on(eng, wrapped)
        return v

    def _declare_io(self):
        nc, c = self.nc, self.cfg
        D, I, H, S, TOKB = c["D"], c["I"], c["H"], c["S"], c["TOKB"]
        NM, ND, NI = c["NMAX"], c["ND"], c["NI"]
        dt = nc.dram_tensor
        self.xin = dt("xin", [S, D], F32, kind="ExternalInput").ap()
        self.wq = dt("wq", [H, D, 128], F32, kind="ExternalInput").ap()
        self.wk = dt("wk", [H, D, 128], F32, kind="ExternalInput").ap()
        self.wv = dt("wv", [H, D, 128], F32, kind="ExternalInput").ap()
        self.wo = dt("wo", [H, 128, D], F32, kind="ExternalInput").ap()
        self.bq = dt("bq", [1, H * 128], F32, kind="ExternalInput").ap()
        self.bk = dt("bk", [1, H * 128], F32, kind="ExternalInput").ap()
        self.bv = dt("bv", [1, H * 128], F32, kind="ExternalInput").ap()
        self.bo = dt("bo", [ND, 128], F32, kind="ExternalInput").ap()
        self.su_ug = dt("su_ug", [ND, 128], F32, kind="ExternalInput").ap()
        self.su_dp = dt("su_dp", [NI, 128], F32, kind="ExternalInput").ap()
        self.ws_ug = dt("ws_ug", [2 * NI, 128], F32, kind="ExternalInput").ap()
        self.ws_dp = dt("ws_dp", [ND, 128], F32, kind="ExternalInput").ap()
        self.w_up = dt("w_up", [I, D], F32, kind="ExternalInput").ap()
        self.w_gate = dt("w_gate", [I, D], F32, kind="ExternalInput").ap()
        self.w_down = dt("w_down", [D, I], F32, kind="ExternalInput").ap()
        for nm, sh in {"eye32": [128, 128], "ones_row": [1, TOKB],
                       "ones_c32": [128, 1], "o1x128_32": [1, 128]}.items():
            setattr(self, nm, dt(nm, sh, F32, kind="ExternalInput").ap())
        shapes16 = {"eye16": [128, 128], "h128c": [128, 128], "hnic": [NI, NI],
                    "masksc": [4, 128, TOKB], "ones_r16": [1, TOKB],
                    "ones_c16": [128, 1], "o1x128": [1, 128]}
        for nm, sh in shapes16.items():
            setattr(self, nm, dt(nm, sh, BF16, kind="ExternalInput").ap())
        self.out = dt("out", [NM, D], F32, kind="ExternalOutput").ap()
        self.wqkv16 = dt("wqkv16", [3, H, D, 128], BF16)
        self.wo16s = dt("wo16s", [H, 128, D], BF16)
        self.k_s = dt("k_s", [H, 128, S], BF16)
        self.v_s = dt("v_s", [H, S, 128], BF16)
        self.q_s = dt("q_s", [H, 128, NM], BF16)
        self.qn_s = dt("qn_s", [H, NM], BF16)
        self.z_s = dt("z_s", [H, 128, NM], BF16)
        self.xfm_s = dt("xfm_s", [ND, 128, NM], F32)
        self.h_s = dt("h_s", [ND, 128, NM], F32)
        self.wup16 = dt("wup16", [I, D], BF16)
        self.wgt16 = dt("wgt16", [I, D], BF16)
        self.wdn16 = dt("wdn16", [D, I], BF16)
        self.wupT = dt("wupT", [I // TOKB, ND, 128, TOKB], BF16)
        self.wgtT = dt("wgtT", [I // TOKB, ND, 128, TOKB], BF16)
        self.u_s = dt("u_s", [NI, 128, D], BF16)
        self.wdnT = dt("wdnT", [NI, 128, D], BF16)
        self.g_s = dt("g_s", [ND, NI, 128, 128], BF16)

    def _alloc(self):
        nc, c = self.nc, self.cfg
        D, I, H, TOKB, ND, NI = c["D"], c["I"], c["H"], c["TOKB"], c["ND"], c["NI"]
        NM, S, GRP = c["NMAX"], c["S"], c["GRP"]
        P = lambda name, shape, dtype: self._salloc(name, shape, dtype, "persist")
        self.c_eye32 = P("c_eye32", [128, 128], F32)
        self.c_eye16 = P("c_eye16", [128, 128], BF16)
        self.c_h128 = P("c_h128", [128, 128], BF16)
        self.c_hni = P("c_hni", [NI, NI], BF16)
        self.c_masks = P("c_masks", [128, 4, TOKB], BF16)
        self.c_ones_row = P("c_ones_row", [1, TOKB], F32)
        self.c_ones_r16 = P("c_ones_r16", [1, TOKB], BF16)
        self.c_ones_c16 = P("c_ones_c16", [128, 1], BF16)
        self.c_ones_c32 = P("c_ones_c32", [128, 1], F32)
        self.c_o1x128 = P("c_o1x128", [1, 128], BF16)
        self.c_o1x128_32 = P("c_o1x128_32", [1, 128], F32)
        self.v_su_ug = P("v_su_ug", [128, ND], F32)
        self.v_sup = P("v_sup", [128, NI], F32)
        self.v_sgt = P("v_sgt", [128, NI], F32)
        self.v_sdn = P("v_sdn", [128, ND], F32)
        self.v_sudp = P("v_sudp", [128, NI], F32)
        self.v_bo = P("v_bo", [128, ND], F32)
        self.r_bq = P("r_bq", [1, H * 128], BF16)
        self.r_bk = P("r_bk", [1, H * 128], BF16)
        self.r_bv = P("r_bv", [1, H * 128], BF16)
        
        self.r_stage = P("r_stage", [128, 128], F32)
        self.knmax = P("knmax", [1, H * (S // TOKB)], F32)
        self.mkneg = P("mkneg", [1, H], F32)
        self._phase_base = ((self._persist_off + 63) & ~63)
        budget = self._arena_base + self.ARENA_BYTES
        assert self._phase_base - self._arena_base < 60 * 1024, self._phase_base

        def A(name, shape, dtype, pool):
            t = self._salloc(name, shape, dtype, pool)
            assert self._phase_off[pool] <= budget, (name, pool, self._phase_off[pool])
            return t
        self.nrm = A("nrm", [1, TOKB], F32, "p0")
        self.tmpr = A("tmpr", [1, TOKB], F32, "p0")
        self.xt32 = A("xt32", [128, TOKB // 128, D], F32, "p0")
        self.xt16 = A("xt16", [128, TOKB // 128, D], BF16, "p0")
        self.xfm16 = A("xfm16", [128, ND, GRP], BF16, "p0")
        self.xfm32 = A("xfm32", [128, ND, 128], F32, "p0")
        self.wh16 = A("wh16", [128, 2, 3, ND, 128], BF16, "p0")
        self.kvq16 = A("kvq16", [128, 3, 2, TOKB], BF16, "p0")
        self.vtr16 = A("vtr16", [128, 2, TOKB // 128, 128], BF16, "p0")
        self.sq16 = A("sq16", [128, 2, TOKB], BF16, "p0")
        self.nrow = A("nrow", [1, 2, TOKB], BF16, "p0")
        self.r_b32 = A("r_b32", [1, H * 128], F32, "p0")
        self.wcast32 = A("wcast32", [128, 2, 2048], F32, "wp2")
        self.wcast16 = A("wcast16", [128, 2, 2048], BF16, "wp2")
        self.wtT = A("wtT", [128, 16, TOKB], BF16, "wp")
        self.ubuf = A("ubuf", [128, 2, TOKB], BF16, "wp")
        self.sbB = A("sbB", [NI, 12, TOKB], BF16, "wp")
        self.gout = A("gout", [NI, 2, TOKB], BF16, "wp")
        self.kbuf = A("kbuf", [128, 2, S], BF16, "p1")
        self.vbuf = A("vbuf", [128, 2, S // 128, 128], BF16, "p1")
        self.qbuf = A("qbuf", [128, 2, NM], BF16, "p1")
        self.qnrow = A("qnrow", [1, 2, NM], BF16, "p1")
        self.ebuf = A("ebuf", [128, 3, 2, TOKB], BF16, "p1")
        self.emsk = A("emsk", [128, 3, 2, TOKB], BF16, "p1")
        self.dacc = A("dacc", [128, 2, TOKB], F32, "p1")
        self.crow = A("crow", [1, 2, TOKB], BF16, "p1")
        self.dinv16 = A("dinv16", [1, TOKB], BF16, "p1")
        self.dinvb = A("dinvb", [128, TOKB], BF16, "p1")
        self.lnrow = A("lnrow", [1, TOKB], F32, "p1")
        self.dinv32 = A("dinv32", [1, TOKB], F32, "p1")
        self.z16 = A("z16", [128, 2, TOKB], BF16, "p1")
        self.wob = A("wob", [128, 2, H * 128], BF16, "p23")
        self.xst = A("xst", [128, 2, TOKB], F32, "p23")
        self.y32 = A("y32", [128, ND, TOKB], F32, "p23")
        self.hst = A("hst", [128, 2, TOKB], F32, "p23")
        self.hb16 = A("hb16", [128, 2, ND, TOKB], BF16, "p23")
        assert int(math.log2(ND)) % 2 == 0
        self.h2b = self.hb16[:, 1]
        self.wtb = A("wtb", [128, 2, 2, ND, 128], BF16, "p23")
        self.g16 = A("g16", [128, TOKB], BF16, "p23")
        self.z1 = A("z1", [128, NI // 2, TOKB], BF16, "p23")
        self.gwb = A("gwb", [128, 2, NI // 4, 128], BF16, "p23")
        self.mrow = A("mrow", [1, TOKB], F32, "p23")
        self.vrow = A("vrow", [1, TOKB], F32, "p23")
        self.mb = A("mb", [128, TOKB], F32, "p23")
        self.rb = A("rb", [128, TOKB], F32, "p23")
        self.ot32 = A("ot32", [128, 2, TOKB // 128, 128], F32, "p23")
        self.ysq = A("ysq", [128, 2, TOKB], F32, "p23")
        self.oap = A("oap", [128, 2, TOKB], F32, "p23")
        self.wrow = A("wrow", [1, TOKB], F32, "p23")
        self.rrow = A("rrow", [1, TOKB], F32, "p23")
        for pool, off in self._phase_off.items():
            assert off <= budget, (pool, off - self._arena_base)
        PS = nc.alloc_psum_tensor
        self.psA = PS("psA", [128, 2, 512], F32)
        self.psB = PS("psB", [128, 2, 512], F32)
        self.psC = PS("psC", [128, 512], F32)
        self.psD = PS("psD", [128, 512], F32)
        self.psE = PS("psE", [128, 512], F32)
        self.psF = PS("psF", [128, 512], F32)


    def emit_pre(self):
        b, c = self, self.cfg
        D, I, H, TOKB, ND, NI = c["D"], c["I"], c["H"], c["TOKB"], c["ND"], c["NI"]
        self.cur = self.pre_ops
        consts = [
            (self.c_eye32, self.eye32), (self.c_eye16, self.eye16),
            (self.c_h128, self.h128c), (self.c_hni, self.hnic),
            (self.c_ones_row, self.ones_row), (self.c_ones_r16, self.ones_r16),
            (self.c_ones_c16, self.ones_c16), (self.c_ones_c32, self.ones_c32),
            (self.c_o1x128, self.o1x128), (self.c_o1x128_32, self.o1x128_32),
        ]
        for dst, src in consts:
            b.dma("sync", dst[:], src, "c_ld")
        b.dma("sync", self.c_masks[:], self.masksc.rearrange("m p t -> p m t"), "c_ld")
        for i, (dst, srcb) in enumerate([(self.r_bq, self.bq), (self.r_bk, self.bk),
                                         (self.r_bv, self.bv)]):
            if i > 0:
                b.wait("sync", "pre_dve", b.tot("pre_dve"))
            b.dma("sync", self.r_b32[:], srcb, "b_ld")
            b.wait("vector", "b_ld", b.tot("b_ld"))
            b.op("vector", lambda h, dst=dst: h.tensor_copy(
                out=dst[:], in_=self.r_b32[:]), "pre_dve")
        vecs = [
            (self.su_ug, ND, [(self.v_su_ug, 0, ND, 1.0)]),
            (self.su_dp, NI, [(self.v_sudp, 0, NI, 1.0)]),
            (self.ws_ug, 2 * NI, [(self.v_sup, 0, NI, 1.0 / math.sqrt(D)),
                                  (self.v_sgt, NI, NI, 1.0 / math.sqrt(D))]),
            (self.ws_dp, ND, [(self.v_sdn, 0, ND, 1.0 / math.sqrt(I))]),
            (self.bo, ND, [(self.v_bo, 0, ND, 1.0)]),
        ]
        for src, rows, outs in vecs:
            b.wait("sync", "pre_pe", b.tot("pre_pe"))
            b.dma("sync", self.r_stage[0:rows, :], src, "v_ld")
            b.wait("tensor", "v_ld", b.tot("v_ld"))
            b.wait("tensor", "pre_dve", b.tot("pre_dve"))
            b.op("tensor", lambda h, rows=rows: h.transpose(
                out=self.psF[0:128, 0:rows], in_=self.r_stage[0:rows, :],
                identity=self.c_eye32[0:rows, 0:rows]), "pre_pe")
            b.wait("vector", "pre_pe", b.tot("pre_pe"))
            for dst, off, n, scale in outs:
                if scale == 1.0:
                    b.op("vector", lambda h, dst=dst, off=off, n=n: h.tensor_copy(
                        out=dst[:, 0:n], in_=self.psF[:, off:off + n]), "pre_dve")
                else:
                    b.op("vector", lambda h, dst=dst, off=off, n=n, s=scale: h.tensor_scalar(
                        out=dst[:, 0:n], in0=self.psF[:, off:off + n],
                        scalar1=s, scalar2=None, op0=ALU.mult), "pre_dve")
        b.wait("vector", "pre_dve", b.tot("pre_dve"))
        b.op("vector", lambda h: h.tensor_tensor(
            out=self.v_sup[:], in0=self.v_sup[:], in1=self.v_sudp[:],
            op=ALU.mult), "pre_dve")
        jobs = []
        for m, w in [(0, self.wq), (1, self.wk), (2, self.wv)]:
            for hh in range(H):
                jobs.append((w[hh].rearrange("(c p) e -> p c e", p=128),
                             self.wqkv16[m, hh].rearrange("(c p) e -> p c e", p=128)))
        for hh in range(H):
            jobs.append((self.wo[hh], self.wo16s[hh]))
        list(self._cast_jobs(jobs, "wc"))
        self.pre_totals = dict(self.cnt)

    def barrier_pre(self):
        for eng in ENGS:
            for phys, val in self.pre_totals.items():
                if val > 0 and not phys.startswith("bar"):
                    h = self.sem_handle(phys)
                    self.on(eng, lambda hh, h=h, val=val: hh.wait_ge(h, val))

    def emit_p0(self, v):
        b, c = self, self.cfg
        D, H, TOKB, ND = c["D"], c["H"], c["TOKB"], c["ND"]
        GRP, S = c["GRP"], c["S"]
        s_v, e_v = c["bounds"][v], c["bounds"][v + 1]
        self._p0mm_after = {}
        self._wslot_after = {}
        NT = S // TOKB
        TS = TOKB // 128
        eye16, eye32 = self.c_eye16, self.c_eye32
        psq = [self.psA[:, 0], self.psA[:, 1], self.psB[:, 0], self.psB[:, 1]]
        n_own = 0
        ngr = (e_v + GRP - 1) // GRP
        qd_n = 0
        fq_n = 0
        it_n = 0
        xe_hist, xef_hist = {}, {}
        xfs_last = 0

        for g in range(ngr):
            gs, ge = g * GRP, min(e_v, (g + 1) * GRP)
            tiles = list(range(gs // TOKB, ge // TOKB))
            for ti, t in enumerate(tiles):
                own = (t * TOKB >= s_v)
                xr = self.xin[t * TOKB:(t + 1) * TOKB].rearrange(
                    "(s p) d -> p s d", p=128)
                xld_h, xc_h = {}, {}
                for ts in range(TS):
                    if g + ti > 0:
                        b.wait("sync", "xc", b.tot("xc"))
                        if b.tot("ptf") > 0:
                            b.wait("sync", "ptf", b.tot("ptf"))
                    b.dma("sync", self.xt32[:, ts], xr[:, ts], "xld")
                    xld_h[ts] = b.tot("xld")
                for ts in range(TS):
                    b.wait("scalar", "xld", xld_h[ts])
                    if b.tot("pt") > 0:
                        b.wait("scalar", "pt", b.tot("pt") if ts == 0 else 0)
                    b.op("scalar", lambda h, ts=ts: h.copy(
                        out=self.xt16[:, ts], in_=self.xt32[:, ts]), "xc")
                    xc_h[ts] = b.tot("xc")
                for ts in range(TS):
                    xc_now = xc_h[ts]
                    for dq in range(ND // 4):
                        slot = qd_n % 4
                        ps16 = psq[slot].bitcast(BF16)
                        b.wait("tensor", "xc", xc_now)
                        if qd_n >= 4:
                            b.wait("tensor", "xe", xe_hist[qd_n - 4])
                        for j in range(4):
                            dc = dq * 4 + j
                            b.op("tensor",
                                 lambda h, ts=ts, dc=dc, ps16=ps16, j=j: h.transpose(
                                     out=ps16[:, j * 128:(j + 1) * 128],
                                     in_=self.xt16[:, ts, dc * 128:(dc + 1) * 128],
                                     identity=eye16[:]),
                                 "pt" if j == 3 else None)
                        b.wait("vector", "pt", b.tot("pt"))
                        go = (t * TOKB - gs) + ts * 128
                        b.op("vector", lambda h, dq=dq, ps16=ps16, go=go: h.tensor_copy(
                            out=self.xfm16[:, dq * 4:(dq + 1) * 4, go:go + 128],
                            in_=ps16[:, 0:512].rearrange("p (c e) -> p c e", e=128)),
                            "xe")
                        xe_hist[qd_n] = b.tot("xe")
                        qd_n += 1
                if own:
                    for ts in range(TS):
                        if n_own > 0 or ts > 0:
                            b.wait("vector", "xfs", xfs_last)
                        for dq in range(ND // 4):
                            ps32 = [self.psC, self.psF][fq_n % 2]
                            if fq_n >= 2:
                                b.wait("tensor", "xef", xef_hist[fq_n - 2])
                            for j in range(4):
                                dc = dq * 4 + j
                                b.op("tensor",
                                     lambda h, ts=ts, dc=dc, ps32=ps32, j=j: h.transpose(
                                         out=ps32[:, j * 128:(j + 1) * 128],
                                         in_=self.xt32[:, ts, dc * 128:(dc + 1) * 128],
                                         identity=eye32[:]),
                                     "ptf" if j == 3 else None)
                            b.wait("vector", "ptf", b.tot("ptf"))
                            b.op("vector", lambda h, dq=dq, ps32=ps32: h.tensor_copy(
                                out=self.xfm32[:, dq * 4:(dq + 1) * 4, :],
                                in_=ps32.rearrange("p (c e) -> p c e", e=128)), "xef")
                            xef_hist[fq_n] = b.tot("xef")
                            fq_n += 1
                        b.wait("gpsimd", "xef", b.tot("xef"))
                        xfs_last = b.dma("gpsimd",
                            self.xfm_s.rearrange("c p n -> p c n")[
                                :, :, (t * TOKB - s_v) + ts * 128:
                                (t * TOKB - s_v) + ts * 128 + 128],
                            self.xfm32[:], "xfs")
                    n_own += 1
            for hh in range(H):
                slot = hh % 2
                wsem = f"whl{slot}"
                if slot in self._wslot_after:
                    b.wait("sync", "p0v", self._wslot_after[slot])
                for m in range(3):
                    b.dma("sync", self.wh16[:, slot, m],
                          self.wqkv16[m, hh].rearrange("(c p) e -> p c e", p=128), wsem)
                b.wait("tensor", wsem, b.tot(wsem))
                for ti, t in enumerate(tiles):
                    own = (t * TOKB >= s_v)
                    sl = it_n % 2
                    goff = t * TOKB - gs
                    if it_n >= 2:
                        b.wait("vector", f"ks_st{sl}", b.tot(f"ks_st{sl}"))
                        b.wait("vector", f"vt_st{sl}", b.tot(f"vt_st{sl}"))
                        b.wait("vector", f"qs_st{sl}", b.tot(f"qs_st{sl}"))
                    b.wait("tensor", "xe", b.tot("xe"))
                    for dc in range(ND):
                        b.op("tensor", lambda h, dc=dc, slot=slot, goff=goff: h.matmul(
                            out=self.psA[:, 0, 0:TOKB], lhsT=self.wh16[:, slot, 1, dc],
                            rhs=self.xfm16[:, dc, goff:goff + TOKB],
                            start=(dc == 0), stop=False, skip_group_check=True))
                    b.op("tensor", lambda h, hh=hh: h.matmul(
                        out=self.psA[:, 0, 0:TOKB], lhsT=self.r_bk[0:1, hh * 128:(hh + 1) * 128],
                        rhs=self.c_ones_r16[0:1, :], start=False, stop=True,
                        skip_group_check=True), "p0k")
                    for dc in range(ND):
                        b.op("tensor", lambda h, dc=dc, slot=slot, goff=goff: h.matmul(
                            out=self.psA[:, 1, 0:TOKB], lhsT=self.wh16[:, slot, 2, dc],
                            rhs=self.xfm16[:, dc, goff:goff + TOKB],
                            start=(dc == 0), stop=False, skip_group_check=True))
                    b.op("tensor", lambda h, hh=hh: h.matmul(
                        out=self.psA[:, 1, 0:TOKB], lhsT=self.r_bv[0:1, hh * 128:(hh + 1) * 128],
                        rhs=self.c_ones_r16[0:1, :], start=False, stop=True,
                        skip_group_check=True), "p0v")
                    if own:
                        for dc in range(ND):
                            b.op("tensor", lambda h, dc=dc, slot=slot, goff=goff: h.matmul(
                                out=self.psB[:, 0, 0:TOKB], lhsT=self.wh16[:, slot, 0, dc],
                                rhs=self.xfm16[:, dc, goff:goff + TOKB],
                                start=(dc == 0), stop=False, skip_group_check=True))
                        b.op("tensor", lambda h, hh=hh: h.matmul(
                            out=self.psB[:, 0, 0:TOKB], lhsT=self.r_bq[0:1, hh * 128:(hh + 1) * 128],
                            rhs=self.c_ones_r16[0:1, :], start=False, stop=True,
                            skip_group_check=True), "p0q")
                    self._wslot_after[slot] = b.tot("p0v")
                    b.wait("vector", "p0k", b.tot("p0k"))
                    b.op("vector", lambda h, sl=sl: h.tensor_copy(
                        out=self.kvq16[:, 1, sl], in_=self.psA[:, 0, 0:TOKB]), "p0ek")
                    b.wait("vector", "p0v", b.tot("p0v"))
                    b.op("vector", lambda h, sl=sl: h.tensor_copy(
                        out=self.kvq16[:, 2, sl], in_=self.psA[:, 1, 0:TOKB]), "p0ev")
                    if own:
                        b.wait("vector", "p0q", b.tot("p0q"))
                        b.op("vector", lambda h, sl=sl: h.tensor_copy(
                            out=self.kvq16[:, 0, sl], in_=self.psB[:, 0, 0:TOKB]), "p0eq")
                    b.wait("tensor", "p0ev", b.tot("p0ev"))
                    if it_n >= 1:
                        b.wait("tensor", "p0evt", b.tot("p0evt"))
                    psv16 = self.psB[:, 1].bitcast(BF16)
                    for ts in range(TS):
                        b.op("tensor", lambda h, ts=ts, sl=sl, psv16=psv16: h.transpose(
                            out=psv16[:, ts * 128:(ts + 1) * 128],
                            in_=self.kvq16[:, 2, sl][:, ts * 128:(ts + 1) * 128],
                            identity=eye16[:]), "p0vt" if ts == TS - 1 else None)
                    b.wait("vector", "p0vt", b.tot("p0vt"))
                    b.op("vector", lambda h, sl=sl, psv16=psv16: h.tensor_copy(
                        out=self.vtr16[:, sl], in_=psv16[:, 0:TS * 128].rearrange(
                            "p (s e) -> p s e", e=128)), "p0evt")
                    b.wait("scalar", "p0ek", b.tot("p0ek"))
                    if it_n >= 1:
                        b.wait("scalar", "p0n", b.tot("p0n"))
                    b.op("scalar", lambda h, sl=sl: h.square(
                        out=self.sq16[:, sl], in_=self.kvq16[:, 1, sl]), "p0s")
                    b.wait("tensor", "p0s", b.tot("p0s"))
                    if it_n >= 1:
                        b.wait("tensor", "p0r", b.tot("p0r"))
                        b.wait("tensor", "p0rs", b.tot("p0rs"))
                    b.op("tensor", lambda h, sl=sl: h.matmul(
                        out=self.psE[0:1, 0:TOKB], lhsT=self.c_ones_c16[:],
                        rhs=self.sq16[:, sl], start=True, stop=True,
                        skip_group_check=True), "p0n")
                    b.wait("vector", "p0n", b.tot("p0n"))
                    b.op("vector", lambda h, hh=hh, t=t: h.reduce_max(
                        out=self.knmax[0:1, hh * NT + t:hh * NT + t + 1],
                        in_=self.psE[0:1, 0:TOKB], axis=mybir.AxisListType.X), "p0r")
                    if own:
                        b.wait("scalar", "p0eq", b.tot("p0eq"))
                        b.wait("scalar", "p0n", b.tot("p0n"))
                        b.op("scalar", lambda h, sl=sl: h.square(
                            out=self.sq16[:, sl], in_=self.kvq16[:, 0, sl]), "p0s")
                        b.wait("tensor", "p0s", b.tot("p0s"))
                        b.wait("tensor", "p0r", b.tot("p0r"))
                        b.wait("tensor", "p0rs", b.tot("p0rs"))
                        b.op("tensor", lambda h, sl=sl: h.matmul(
                            out=self.psE[0:1, 0:TOKB], lhsT=self.c_ones_c16[:],
                            rhs=self.sq16[:, sl], start=True, stop=True,
                            skip_group_check=True), "p0n")
                        b.wait("scalar", "p0n", b.tot("p0n"))
                        b.wait("scalar", f"qn_st{sl}", b.tot(f"qn_st{sl}"))
                        b.op("scalar", lambda h, sl=sl: h.activation(
                            out=self.nrow[0:1, sl], in_=self.psE[0:1, 0:TOKB],
                            func=AF.Sqrt), "p0rs")
                    b.wait("gpsimd", "p0ek", b.tot("p0ek"))
                    b.dma("gpsimd", self.k_s[hh][:, t * TOKB:(t + 1) * TOKB],
                          self.kvq16[:, 1, sl], f"ks_st{sl}")
                    b.wait("gpsimd", "p0evt", b.tot("p0evt"))
                    b.dma("gpsimd", self.v_s[hh].rearrange(
                        "(s p) e -> p s e", p=128)[:, t * TS:(t + 1) * TS, :],
                        self.vtr16[:, sl], f"vt_st{sl}")
                    if own:
                        b.wait("gpsimd", "p0eq", b.tot("p0eq"))
                        b.dma("gpsimd", self.q_s[hh][:, (t * TOKB - s_v):
                                                     (t * TOKB - s_v) + TOKB],
                              self.kvq16[:, 0, sl], f"qs_st{sl}")
                        b.wait("gpsimd", "p0rs", b.tot("p0rs"))
                        b.dma("gpsimd", self.qn_s[hh:hh + 1,
                                                  (t * TOKB - s_v):(t * TOKB - s_v) + TOKB],
                              self.nrow[0:1, sl], f"qn_st{sl}")
                    it_n += 1
        b.wait("vector", "p0r", b.tot("p0r"))
        for hh in range(H):
            b.op("vector", lambda h, hh=hh: h.reduce_max(
                out=self.tmpr[0:1, hh:hh + 1],
                in_=self.knmax[0:1, hh * NT:hh * NT + (e_v // TOKB)],
                axis=mybir.AxisListType.X), "mkr")
        b.wait("scalar", "mkr", b.tot("mkr"))
        b.op("scalar", lambda h: h.activation(
            out=self.nrm[0:1, 0:H], in_=self.tmpr[0:1, 0:H], func=AF.Sqrt), "mks")
        b.wait("gpsimd", "mks", b.tot("mks"))
        b.op("gpsimd", lambda h: h.tensor_scalar_mul(
            out=self.mkneg[0:1, 0:H], in0=self.nrm[0:1, 0:H], scalar1=-1.0), "mkn")


    def barrier(self, name):
        s, tot = self.inc(name, 5, eng="sync")
        for eng in ENGS:
            phys = self._ENG_PHYS[eng]
            if self.cnt.get(phys, 0) > 0:
                hh = self.sem_handle(phys)
                val = self.cnt[phys]
                self.on(eng, lambda h, hh=hh, val=val: h.wait_ge(hh, val))
            self.on(eng, lambda h, s=s: h.nop().then_inc(s, 1))
        for eng in ENGS:
            self.on(eng, lambda h, s=s, tot=tot: h.wait_ge(s, tot))

    def _cast_jobs(self, jobs, tag, chunk=0, cast_eng="scalar"):
        b = self
        lh = {}

        def bufs(n):
            sap, dap = jobs[n]
            sl = n % 2
            w = int(np.prod(sap.shape[1:]))
            src32 = self.wcast32[:, sl, 0:w]
            src16 = self.wcast16[:, sl, 0:w]
            shp = sap.shape[1:]
            if len(shp) == 2:
                src32 = src32.rearrange("p (c e) -> p c e", e=shp[1])
                src16 = src16.rearrange("p (c e) -> p c e", e=shp[1])
            return sap, dap, src32, src16, sl

        def load(n):
            sap, dap, src32, src16, sl = bufs(n)
            if n >= 2:
                b.wait("sync", f"{tag}_c", b.tot(f"{tag}_c"))
            b.dma("sync", src32, sap, f"{tag}_l{sl}")
            lh[n] = b.tot(f"{tag}_l{sl}")

        load(0)
        for n in range(len(jobs)):
            sap, dap, src32, src16, sl = bufs(n)
            if n + 1 < len(jobs):
                load(n + 1)
            b.wait(cast_eng, f"{tag}_l{sl}", lh[n])
            b.wait(cast_eng, f"{tag}_s{sl}", b.tot(f"{tag}_s{sl}"))
            if cast_eng == "scalar":
                b.op(cast_eng, lambda h, a=src16, bb=src32: h.copy(out=a, in_=bb),
                     f"{tag}_c")
            else:
                b.op(cast_eng, lambda h, a=src16, bb=src32: h.tensor_copy(
                    out=a, in_=bb), f"{tag}_c")
            b.wait("gpsimd", f"{tag}_c", b.tot(f"{tag}_c"))
            b.dma("gpsimd", dap, src16, f"{tag}_s{sl}")
            if chunk and n % chunk == chunk - 1:
                yield

    def emit_wprep(self, v):
        b, c = self, self.cfg
        D, I, TOKB, ND, NI = c["D"], c["I"], c["TOKB"], c["ND"], c["NI"]
        jobs = []
        for wsrc, wdst in ((self.w_up, self.wup16), (self.w_gate, self.wgt16),
                           (self.w_down, self.wdn16)):
            wid = wsrc.shape[1]
            cw = min(2048, wid)
            for r in range(wsrc.shape[0] // 128):
                for col in range(max(1, wid // cw)):
                    jobs.append((wsrc[r * 128:(r + 1) * 128, col * cw:(col + 1) * cw],
                                 wdst[r * 128:(r + 1) * 128, col * cw:(col + 1) * cw]))
        yield from self._cast_jobs(jobs, "wp", chunk=8, cast_eng="vector")
        for eng in ("sync", "gpsimd"):
            b.wait(eng, "wp_s0", b.tot("wp_s0"))
            b.wait(eng, "wp_s1", b.tot("wp_s1"))
        tjobs = []
        for wsrc, wdst in ((self.wup16, self.wupT), (self.wgt16, self.wgtT)):
            for ug in range(I // TOKB):
                for dc in range(ND):
                    tjobs.append((wsrc[ug * TOKB:(ug + 1) * TOKB,
                                       dc * 128:(dc + 1) * 128], wdst[ug, dc]))
        for a in range(NI):
            for ob in range(D // TOKB):
                tjobs.append((self.wdn16[ob * TOKB:(ob + 1) * TOKB,
                                         a * 128:(a + 1) * 128],
                              self.wdnT[a][:, ob * TOKB:(ob + 1) * TOKB]))
        tl_hist = {}

        def load_t(n):
            sl = n % 16
            b.wait("sync", f"wt_s{sl % 2}", b.tot(f"wt_s{sl % 2}"))
            b.dma("sync", self.wtT[:, sl, 0:TOKB], tjobs[n][0],
                  f"wt_l{sl % 2}", transpose=True)
            tl_hist[n] = b.tot(f"wt_l{sl % 2}")

        for k in range(min(12, len(tjobs))):
            load_t(k)
        for n in range(len(tjobs)):
            sl = n % 16
            if n + 12 < len(tjobs):
                load_t(n + 12)
            b.wait("gpsimd", f"wt_l{sl % 2}", tl_hist[n])
            b.dma("gpsimd", tjobs[n][1], self.wtT[:, sl, 0:TOKB], f"wt_s{sl % 2}")
            if n % 16 == 15:
                yield
        b.wait("sync", "wt_s0", b.tot("wt_s0"))
        b.wait("sync", "wt_s1", b.tot("wt_s1"))
        ja = [(a, ob) for a in range(NI) for ob in range(D // TOKB)]
        wal_hist = {}

        def load_a(n):
            a, ob = ja[n]
            sl = n % 16
            b.wait("sync", f"wa_m{sl % 2}", b.tot(f"wa_m{sl % 2}"))
            b.dma("sync", self.wtT[:, sl, 0:TOKB],
                  self.wdnT[a][:, ob * TOKB:(ob + 1) * TOKB], f"wa_l{sl % 2}")
            wal_hist[n] = b.tot(f"wa_l{sl % 2}")

        for k in range(min(12, len(ja))):
            load_a(k)
        for n in range(len(ja)):
            a, ob = ja[n]
            sl = n % 16
            esl = n % 2
            if n + 12 < len(ja):
                load_a(n + 12)
            b.wait("tensor", f"wa_l{sl % 2}", wal_hist[n])
            b.wait("tensor", f"wa_e{esl}", b.tot(f"wa_e{esl}"))
            b.op("tensor", lambda h, sl=sl, esl=esl: h.matmul(
                out=self.psA[:, esl, 0:TOKB], lhsT=self.c_h128[:],
                rhs=self.wtT[:, sl, 0:TOKB], start=True, stop=True,
                skip_group_check=True), f"wa_m{sl % 2}")
            b.wait("vector", f"wa_m{sl % 2}", b.tot(f"wa_m{sl % 2}"))
            b.wait("vector", f"wa_s{esl}", b.tot(f"wa_s{esl}"))
            b.op("vector", lambda h, esl=esl: h.tensor_copy(
                out=self.ubuf[:, esl, 0:TOKB], in_=self.psA[:, esl, 0:TOKB]),
                f"wa_e{esl}")
            b.wait("gpsimd", f"wa_e{esl}", b.tot(f"wa_e{esl}"))
            b.dma("gpsimd", self.u_s[a][:, ob * TOKB:(ob + 1) * TOKB],
                  self.ubuf[:, esl, 0:TOKB], f"wa_s{esl}")
            if n % 16 == 15:
                yield
        b.wait("sync", "wa_s0", b.tot("wa_s0"))
        b.wait("sync", "wa_s1", b.tot("wa_s1"))
        jb = [(bb, ob) for bb in range(128) for ob in range(D // TOKB)]
        wbl_hist = {}
        TS4 = TOKB // 128

        def load_b(n):
            bb, ob = jb[n]
            sl = n % 12
            b.wait("sync", f"wb_m{sl % 2}", b.tot(f"wb_m{sl % 2}"))
            b.dma("sync", self.sbB[:, sl, 0:TOKB],
                  self.u_s[:, bb, ob * TOKB:(ob + 1) * TOKB], f"wb_l{sl % 2}")
            wbl_hist[n] = b.tot(f"wb_l{sl % 2}")

        for k in range(min(10, len(jb))):
            load_b(k)
        for n in range(len(jb)):
            bb, ob = jb[n]
            sl = n % 12
            esl = n % 2
            if n + 10 < len(jb):
                load_b(n + 10)
            b.wait("tensor", f"wb_l{sl % 2}", wbl_hist[n])
            b.wait("tensor", f"wb_e{esl}", b.tot(f"wb_e{esl}"))
            b.op("tensor", lambda h, sl=sl, esl=esl: h.matmul(
                out=self.psB[0:NI, esl, 0:TOKB], lhsT=self.c_hni[:],
                rhs=self.sbB[0:NI, sl, 0:TOKB], start=True, stop=True,
                skip_group_check=True), f"wb_m{sl % 2}")
            b.wait("vector", f"wb_m{sl % 2}", b.tot(f"wb_m{sl % 2}"))
            b.wait("vector", f"wb_s{esl}", b.tot(f"wb_s{esl}"))
            b.op("vector", lambda h, esl=esl: h.tensor_copy(
                out=self.gout[:, esl, 0:TOKB], in_=self.psB[0:NI, esl, 0:TOKB]),
                f"wb_e{esl}")
            b.wait("gpsimd", f"wb_e{esl}", b.tot(f"wb_e{esl}"))
            b.dma("gpsimd",
                  self.g_s.rearrange("c i b o -> i b c o")[
                      :, bb, ob * TS4:(ob + 1) * TS4, :],
                  self.gout[0:NI, esl, 0:TOKB].rearrange(
                      "a (c o) -> a c o", o=128), f"wb_s{esl}")
            if n % 16 == 15:
                yield

    def emit_p1(self, v, wgen=None):
        b, c = self, self.cfg
        H, TOKB, S = c["H"], c["TOKB"], c["S"]
        s_v, e_v = c["bounds"][v], c["bounds"][v + 1]
        n_v = e_v - s_v
        TS = TOKB // 128
        nqb = n_v // TOKB
        gp = 0
        gq = 0
        exp_hist, dacc_hist, av_hist, dm_hist = {}, {}, {}, {}
        self._crow_hist = {}
        after = {0: {}, 1: {}}
        lv_hist = {}
        dz_hist, pd_hist = {}, {}
        pend_close = []

        def load_head(hh):
            hs = hh % 2
            lsem = f"p1l{hs}"
            for key, val in after[hs].items():
                b.wait("sync", key, val)
            b.dma("sync", self.kbuf[:, hs, 0:e_v], self.k_s[hh][:, 0:e_v], lsem)
            b.dma("sync", self.vbuf[:, hs, 0:e_v // 128, :],
                  self.v_s[hh].rearrange("(s p) e -> p s e", p=128)[:, 0:e_v // 128, :],
                  lsem)
            b.dma("sync", self.qbuf[:, hs, 0:n_v], self.q_s[hh][:, 0:n_v], lsem)
            b.dma("sync", self.qnrow[0:1, hs, 0:n_v], self.qn_s[hh:hh + 1, 0:n_v], lsem)
            lv_hist[hh] = b.tot(lsem)

        load_head(0)
        for hh in range(H):
            hs = hh % 2
            lsem = f"p1l{hs}"
            if hh + 1 < H:
                load_head(hh + 1)
            lv = lv_hist[hh]
            for eng in ("tensor", "vector"):
                b.wait(eng, lsem, lv)
            cr_hist = {}

            def emit_crow(qbi2, hs=hs, hh=hh, cr_hist=cr_hist):
                cs2 = qbi2 % 2
                b.wait("vector", "p1s", self._crow_hist.get(cs2, 0))
                b.op("vector", lambda h, hs=hs, qbi=qbi2, hh=hh, cs=cs2:
                     h.tensor_scalar(
                         out=self.crow[0:1, cs, 0:TOKB],
                         in0=self.qnrow[0:1, hs, qbi * TOKB:(qbi + 1) * TOKB],
                         scalar1=self.mkneg[0:1, hh:hh + 1],
                         scalar2=C_OFF * c["SQ"], op0=ALU.mult, op1=ALU.add), "d_cr")
                cr_hist[qbi2] = b.tot("d_cr")

            emit_crow(0)
            for qbi in range(nqb):
                qa = s_v + qbi * TOKB
                n_k = (qa + TOKB) // 128
                npairs = n_k // 2
                cs = qbi % 2
                qpar = gq % 2
                psAV = [self.psC, self.psF][qpar]
                dslice = self.dacc[:, qpar]
                b.wait("tensor", "d_cr", cr_hist[qbi])

                pend = None

                def emit_av(srcs, lp, gpp, psAV=psAV, qpar=qpar, n_k=n_k, hs=hs):
                    if lp == 0:
                        b.wait("tensor", "d_z", dz_hist.get(qpar, 0))
                    for j in range(2):
                        kc = 2 * lp + j
                        b.op("tensor", lambda h, hs=hs, kc=kc, sj=srcs[j], lp=lp,
                             j=j, n_k=n_k, psAV=psAV: h.matmul(
                                 out=psAV[:, 0:TOKB],
                                 lhsT=self.vbuf[:, hs, kc, :],
                                 rhs=sj, start=(lp == 0 and j == 0),
                                 stop=(kc == n_k - 1),
                                 skip_group_check=True), "p_av" if j == 1 else None)
                    av_hist[gpp] = b.tot("p_av")

                for lp in range(npairs):
                    psX = [self.psA, self.psB][gp % 2]
                    se = gp % 3
                    if gp >= 2:
                        b.wait("tensor", "a_e", exp_hist[gp - 2])
                    for j in range(2):
                        kc = 2 * lp + j
                        b.op("tensor", lambda h, psX=psX, j=j, hs=hs, kc=kc, qbi=qbi:
                             h.matmul(out=psX[:, j, 0:TOKB],
                                      lhsT=self.kbuf[:, hs, kc * 128:(kc + 1) * 128],
                                      rhs=self.qbuf[:, hs, qbi * TOKB:(qbi + 1) * TOKB],
                                      start=True, stop=False, skip_group_check=True))
                        b.op("tensor", lambda h, psX=psX, j=j, cs=cs: h.matmul(
                            out=psX[:, j, 0:TOKB], lhsT=self.c_o1x128[:],
                            rhs=self.crow[0:1, cs, 0:TOKB], start=False, stop=True,
                            skip_group_check=True), "p1s" if j == 1 else None)
                    if pend is not None:
                        b.wait("tensor", "a_e", exp_hist[gp - 1])
                        b.wait("tensor", "d_m", dm_hist.get(gp - 1, 0))
                        emit_av(*pend)
                        pend = None
                    b.wait("scalar", "p1s", b.tot("p1s"))
                    if gp >= 3:
                        b.wait("scalar", "d_d", dacc_hist[gp - 3])
                        b.wait("scalar", "p_av", av_hist.get(gp - 3) or 0)
                    b.op("scalar", lambda h, psX=psX, se=se: h.activation(
                        out=self.ebuf[:, se], in_=psX[:, :, 0:TOKB], func=AF.Exp,
                        scale=1.0 / c["SQ"]), "a_e")
                    exp_hist[gp] = b.tot("a_e")
                    b.wait("vector", "a_e", b.tot("a_e"))
                    srcs = []
                    ep = gp % 3
                    for j in range(2):
                        kc = 2 * lp + j
                        if kc >= qa // 128:
                            m = kc - qa // 128
                            b.wait("vector", "p_av", av_hist.get(gp - 3) or 0)
                            b.op("vector", lambda h, se=se, j=j, m=m, ep=ep:
                                 h.tensor_tensor(
                                     out=self.emsk[:, ep, j], in0=self.ebuf[:, se, j],
                                     in1=self.c_masks[:, m, :], op=ALU.mult), "d_m")
                            b.wait("vector", "d_m", b.tot("d_m"))
                            srcs.append(self.emsk[:, ep, j])
                        else:
                            srcs.append(self.ebuf[:, se, j])
                    dm_hist[gp] = b.tot("d_m")
                    for j in range(2):
                        if lp == 0 and j == 0:
                            b.wait("vector", "p_d", pd_hist.get(qpar, 0))
                            b.op("vector", lambda h, s0=srcs[0], dslice=dslice:
                                 h.tensor_copy(out=dslice, in_=s0), "d_d")
                        else:
                            b.wait("vector", "d_d", b.tot("d_d"))
                            b.op("vector", lambda h, sj=srcs[j], dslice=dslice:
                                 h.tensor_tensor(
                                     out=dslice, in0=dslice,
                                     in1=sj, op=ALU.add), "d_d")
                    dacc_hist[gp] = b.tot("d_d")
                    pend = (srcs, lp, gp)
                    gp += 1
                    if pend_close and lp <= 1:
                        pend_close.pop(0)()
                    if lp == 0 and qbi + 1 < nqb:
                        emit_crow(qbi + 1)
                b.wait("tensor", "a_e", exp_hist[gp - 1])
                b.wait("tensor", "d_m", dm_hist.get(gp - 1, 0))
                emit_av(*pend)
                self._crow_hist[cs] = b.tot("p1s")
                while pend_close:
                    pend_close.pop(0)()
                dd_tot = b.tot("d_d")
                pav_tot = b.tot("p_av")
                zs = qbi % 2

                def close1(dd_tot=dd_tot, dslice=dslice, qpar=qpar):
                    b.wait("tensor", "d_d", dd_tot)
                    b.wait("tensor", "a_l", b.tot("a_l"))
                    b.op("tensor", lambda h, dslice=dslice: h.matmul(
                        out=self.psE[0:1, 0:TOKB], lhsT=self.c_ones_c32[:],
                        rhs=dslice, start=True, stop=True,
                        skip_group_check=True), "p_d")
                    pd_hist[qpar] = b.tot("p_d")
                    b.wait("scalar", "p_d", b.tot("p_d"))
                    b.op("scalar", lambda h: h.activation(
                        out=self.lnrow[0:1, 0:TOKB], in_=self.psE[0:1, 0:TOKB],
                        func=AF.Ln), "a_l")
                    b.wait("scalar", "a_l", b.tot("a_l"))
                    b.wait("scalar", "d_c", b.tot("d_c"))
                    b.op("scalar", lambda h: h.activation(
                        out=self.dinv32[0:1, 0:TOKB], in_=self.lnrow[0:1, 0:TOKB],
                        func=AF.Exp, scale=-1.0), "a_l")
                    b.wait("vector", "a_l", b.tot("a_l"))
                    b.wait("vector", "p_b", b.tot("p_b"))
                    b.op("vector", lambda h: h.tensor_copy(
                        out=self.dinv16[0:1, 0:TOKB],
                        in_=self.dinv32[0:1, 0:TOKB]), "d_c")

                def close2(pav_tot=pav_tot, psAV=psAV, zs=zs, hh=hh, qbi=qbi,
                           qpar=qpar):
                    b.wait("tensor", "d_c", b.tot("d_c"))
                    b.wait("tensor", "d_bv", b.tot("d_bv"))
                    b.op("tensor", lambda h: h.matmul(
                        out=self.psD[:, 0:TOKB], lhsT=self.c_o1x128[:],
                        rhs=self.dinv16[0:1, 0:TOKB], start=True, stop=True,
                        skip_group_check=True), "p_b")
                    b.wait("vector", "p_b", b.tot("p_b"))
                    b.op("vector", lambda h: h.tensor_copy(
                        out=self.dinvb[:, 0:TOKB], in_=self.psD[:, 0:TOKB]), "d_bv")
                    b.wait("vector", "d_bv", b.tot("d_bv"))
                    b.wait("vector", "p_av", pav_tot)
                    b.wait("vector", f"z_st{zs}", b.tot(f"z_st{zs}"))
                    b.op("vector", lambda h, zs=zs, psAV=psAV: h.tensor_tensor(
                        out=self.z16[:, zs], in0=psAV[:, 0:TOKB],
                        in1=self.dinvb[:, 0:TOKB], op=ALU.mult), "d_z")
                    dz_hist[qpar] = b.tot("d_z")
                    b.wait("gpsimd", "d_z", b.tot("d_z"))
                    b.dma("gpsimd", self.z_s[hh][:, qbi * TOKB:(qbi + 1) * TOKB],
                          self.z16[:, zs], f"z_st{zs}")

                pend_close = [close1, close2]
                gq += 1
            after[hs] = {"p1s": b.tot("p1s"), "p_av": b.tot("p_av"),
                         "d_cr": b.tot("d_cr")}
            if wgen is not None:
                for _ in range(7):
                    if next(wgen, "END") == "END":
                        wgen = None
                        break
        while pend_close:
            pend_close.pop(0)()

    def emit_p23(self, v):
        b, c = self, self.cfg
        D, I, H, TOKB, ND, NI = c["D"], c["I"], c["H"], c["TOKB"], c["ND"], c["NI"]
        s_v, e_v = c["bounds"][v], c["bounds"][v + 1]
        n_v = e_v - s_v
        TS = TOKB // 128
        nst = int(math.log2(ND))
        ntb = n_v // TOKB
        wob_hist, psA_hist = {}, {}
        p3u_hist, p3g_hist, p3h_hist, h2_hist = {}, {}, {}, {}
        for tb in range(ntb):
            t0 = tb * TOKB
            b.wait("sync", "p3dn", b.tot("p3dn"))
            b.wait("sync", "d_u", b.tot("d_u"))
            for hh2 in range(H):
                b.dma("sync", self.z1[:, hh2, 0:TOKB],
                      self.z_s[hh2][:, t0:t0 + TOKB], "zl")
            zl_tot = b.tot("zl")
            owl_hist = {}

            def load_oproj(dc):
                slw = dc % 2
                if slw in wob_hist:
                    b.wait("scalar", "p2o", wob_hist[slw])
                b.dma("scalar", self.wob[:, slw, :].rearrange("p (hh d) -> p hh d", d=128),
                      self.wo16s.rearrange("hh p d -> p hh d")[:, :, dc * 128:(dc + 1) * 128],
                      f"p2wl{slw}")
                b.wait("sync", "dve_y", psA_hist.get(slw, 0))
                b.dma("sync", self.xst[:, slw, 0:TOKB],
                      self.xfm_s[dc][:, t0:t0 + TOKB], f"p2xl{slw}")
                owl_hist[dc] = (b.tot(f"p2wl{slw}"), b.tot(f"p2xl{slw}"))

            load_oproj(0)
            for dc in range(ND):
                slw = dc % 2
                if dc + 1 < ND:
                    load_oproj(dc + 1)
                b.wait("tensor", f"p2wl{slw}", owl_hist[dc][0])
                b.wait("tensor", f"p2xl{slw}", owl_hist[dc][1])
                if dc == 0:
                    b.wait("tensor", "zl", zl_tot)
                b.wait("tensor", "dve_y", psA_hist.get(slw, 0))
                for hh2 in range(H):
                    b.op("tensor", lambda h, slw=slw, hh2=hh2: h.matmul(
                        out=self.psA[:, slw, 0:TOKB],
                        lhsT=self.wob[:, slw, hh2 * 128:(hh2 + 1) * 128],
                        rhs=self.z1[:, hh2, 0:TOKB],
                        start=(hh2 == 0), stop=(hh2 == H - 1),
                        skip_group_check=True), "p2o" if hh2 == H - 1 else None)
                wob_hist[slw] = b.tot("p2o")
                b.wait("vector", "p2o", b.tot("p2o"))
                b.op("vector", lambda h, slw=slw, dc=dc: h.scalar_tensor_tensor(
                    out=self.y32[:, dc, 0:TOKB], in0=self.psA[:, slw, 0:TOKB],
                    scalar=self.v_bo[:, dc:dc + 1], in1=self.xst[:, slw, 0:TOKB],
                    op0=ALU.add, op1=ALU.add), "dve_y")
                psA_hist[slw] = b.tot("dve_y")
                b.wait("scalar", "dve_y", b.tot("dve_y"))
                b.wait("scalar", "p2s2", b.tot("p2s2"))
                b.op("scalar", lambda h, dc=dc: h.square(
                    out=self.ysq[:, dc % 2], in_=self.y32[:, dc, 0:TOKB]), "a_sq")
                b.wait("tensor", "dve_y", b.tot("dve_y"))
                b.op("tensor", lambda h, dc=dc: h.matmul(
                    out=self.psE[0:1, 0:TOKB], lhsT=self.c_ones_c32[:],
                    rhs=self.y32[:, dc, 0:TOKB], start=(dc == 0), stop=(dc == ND - 1),
                    skip_group_check=True), "p2sy")
                b.wait("tensor", "a_sq", b.tot("a_sq"))
                b.op("tensor", lambda h, dc=dc: h.matmul(
                    out=self.psF[0:1, 0:TOKB], lhsT=self.c_ones_c32[:],
                    rhs=self.ysq[:, dc % 2], start=(dc == 0), stop=(dc == ND - 1),
                    skip_group_check=True), "p2s2")
            self._ln_rows(v, "p2sy", "p2s2", D)
            for dc in range(ND):
                hsl = dc % 2
                if dc == 0:
                    b.wait("vector", "d_b", b.tot("d_b"))
                b.op("vector", lambda h, dc=dc: h.tensor_sub(
                    out=self.ysq[:, dc % 2], in0=self.y32[:, dc, 0:TOKB],
                    in1=self.mb[:, 0:TOKB]), "g_t")
                b.wait("vector", f"h_st{hsl}", b.tot(f"h_st{hsl}"))
                b.op("vector", lambda h, hsl=hsl, dc=dc: h.tensor_mul(
                    out=self.hst[:, hsl, 0:TOKB], in0=self.ysq[:, dc % 2],
                    in1=self.rb[:, 0:TOKB]), "d_h")
                b.op("vector", lambda h, dc=dc, hsl=hsl: h.tensor_scalar_mul(
                    out=self.hb16[:, 0, dc, 0:TOKB], in0=self.hst[:, hsl, 0:TOKB],
                    scalar1=self.v_su_ug[:, dc:dc + 1]), "g_su")
                b.wait("gpsimd", "d_h", b.tot("d_h"))
                b.dma("gpsimd", self.h_s[dc][:, t0:t0 + TOKB],
                      self.hst[:, hsl, 0:TOKB], f"h_st{hsl}")
            b.wait("vector", "g_su", b.tot("g_su"))
            b.wait("vector", "p3u", b.tot("p3u"))
            for st in range(nst):
                bit = 1 << st
                if st > 0:
                    b.wait("vector", "d_f", b.tot("d_f"))
                for dc in range(ND):
                    src0 = dc & ~bit
                    src1 = dc | bit
                    if dc & bit:
                        b.op("vector", lambda h, st=st, dc=dc, src0=src0, src1=src1:
                             h.tensor_sub(
                                 out=self.hb16[:, (st + 1) % 2, dc, 0:TOKB],
                                 in0=self.hb16[:, st % 2, src0, 0:TOKB],
                                 in1=self.hb16[:, st % 2, src1, 0:TOKB]), "d_f")
                    else:
                        b.op("vector", lambda h, st=st, dc=dc, src0=src0, src1=src1:
                             h.tensor_add(
                                 out=self.hb16[:, (st + 1) % 2, dc, 0:TOKB],
                                 in0=self.hb16[:, st % 2, src0, 0:TOKB],
                                 in1=self.hb16[:, st % 2, src1, 0:TOKB]), "d_f")
            fb = nst % 2
            b.wait("tensor", "d_f", b.tot("d_f"))
            for dc in range(ND):
                b.wait("tensor", "d_h2", h2_hist.get(dc % 2, 0))
                b.wait("tensor", "dve_y", psA_hist.get(dc % 2, 0))
                b.op("tensor", lambda h, dc=dc, fb=fb: h.matmul(
                    out=self.psA[:, dc % 2, 0:TOKB], lhsT=self.c_h128[:],
                    rhs=self.hb16[:, fb, dc, 0:TOKB], start=True, stop=True,
                    skip_group_check=True), "p2h")
                b.wait("vector", "p2h", b.tot("p2h"))
                b.op("vector", lambda h, dc=dc: h.tensor_copy(
                    out=self.h2b[:, dc, 0:TOKB], in_=self.psA[:, dc % 2, 0:TOKB]),
                    "d_h2")
                h2_hist[dc % 2] = b.tot("d_h2")
            b.wait("tensor", "d_h2", b.tot("d_h2"))
            b.wait("tensor", "a_g", b.tot("a_g"))
            NH = NI // 2
            uwl_hist = {}

            def load_ug(uc):
                sl = uc % 2
                ug, wi = uc // TS, uc % TS
                if uc >= 2:
                    b.wait("sync", "d_u", p3u_hist.get(sl, 0))
                    b.wait("gpsimd", "d_u", p3u_hist.get(sl, 0))
                b.dma("sync", self.wtb[:, sl, 0],
                      self.wupT[ug].rearrange("cc p u -> p cc u")[
                          :, :, wi * 128:(wi + 1) * 128], f"p3wl{sl}")
                b.dma("gpsimd", self.wtb[:, sl, 1],
                      self.wgtT[ug].rearrange("cc p u -> p cc u")[
                          :, :, wi * 128:(wi + 1) * 128], f"p3wg{sl}")
                uwl_hist[uc] = (b.tot(f"p3wl{sl}"), b.tot(f"p3wg{sl}"))

            for hf in range(2):
                b.wait("tensor", "d_y0", b.tot("d_y0"))
                b.wait("tensor", "d_y2", b.tot("d_y2"))
                if hf == 0:
                    load_ug(0)
                for uci in range(NH):
                    uc = hf * NH + uci
                    sl = uc % 2
                    if uc + 1 < NI:
                        load_ug(uc + 1)
                    b.wait("tensor", f"p3wl{sl}", uwl_hist[uc][0])
                    b.wait("tensor", f"p3wg{sl}", uwl_hist[uc][1])
                    if uc >= 2:
                        b.wait("tensor", "d_u", p3u_hist.get(sl, 0))
                    for dc in range(ND):
                        b.op("tensor", lambda h, sl=sl, dc=dc: h.matmul(
                            out=self.psA[:, sl, 0:TOKB],
                            lhsT=self.wtb[:, sl, 0, dc],
                            rhs=self.h2b[:, dc, 0:TOKB], start=(dc == 0),
                            stop=(dc == ND - 1), skip_group_check=True))
                    for dc in range(ND):
                        b.op("tensor", lambda h, sl=sl, dc=dc: h.matmul(
                            out=self.psB[:, sl, 0:TOKB],
                            lhsT=self.wtb[:, sl, 1, dc],
                            rhs=self.h2b[:, dc, 0:TOKB], start=(dc == 0),
                            stop=(dc == ND - 1), skip_group_check=True),
                            "p3u" if dc == ND - 1 else None)
                    b.wait("scalar", "p3u", b.tot("p3u"))
                    b.wait("scalar", "d_u", b.tot("d_u"))
                    b.op("scalar", lambda h, sl=sl, uc=uc: h.activation(
                        out=self.g16[:, 0:TOKB], in_=self.psB[:, sl, 0:TOKB],
                        func=AF.Silu, scale=self.v_sgt[:, uc:uc + 1]), "a_g")
                    b.wait("vector", "a_g", b.tot("a_g"))
                    b.op("vector", lambda h, sl=sl, uc=uc, uci=uci: h.scalar_tensor_tensor(
                        out=self.z1[:, uci, 0:TOKB], in0=self.psA[:, sl, 0:TOKB],
                        scalar=self.v_sup[:, uc:uc + 1], in1=self.g16[:, 0:TOKB],
                        op0=ALU.mult, op1=ALU.mult), "d_u")
                    p3u_hist[sl] = b.tot("d_u")
                b.wait("tensor", "d_u", b.tot("d_u"))
                gl_hist = {}
                hl_hist = {}

                def load_g(oc, q):
                    s2 = (oc * 2 + q) % 2
                    b.wait("scalar", "p3dn", p3g_hist.get(s2, 0))
                    qa0 = hf * NH + q * (NI // 4)
                    b.dma("scalar", self.gwb[:, s2],
                          self.g_s[oc, qa0:qa0 + NI // 4].rearrange(
                              "i b o -> b i o"), f"p3gl{s2}")
                    gl_hist[(oc, q)] = b.tot(f"p3gl{s2}")

                def load_h(oc):
                    so = oc % 2
                    b.wait("gpsimd", "d_y2", p3h_hist.get(so, 0))
                    b.dma("gpsimd", self.hst[:, so, 0:TOKB],
                          self.h_s[oc][:, t0:t0 + TOKB], f"p3hl{so}")
                    hl_hist[oc] = b.tot(f"p3hl{so}")

                load_g(0, 0)
                if hf == 1:
                    b.wait("gpsimd", "h_st0", b.tot("h_st0"))
                    b.wait("gpsimd", "h_st1", b.tot("h_st1"))
                    load_h(0)
                for oc in range(ND):
                    so = oc % 2
                    if oc + 1 < ND:
                        if hf == 1:
                            load_h(oc + 1)
                    b.wait("tensor", "d_y0", b.tot("d_y0"))
                    b.wait("tensor", "d_y2", b.tot("d_y2"))
                    for q in range(2):
                        s2 = (oc * 2 + q) % 2
                        if (oc, q) != (ND - 1, 1):
                            nxt = (oc, q + 1) if q == 0 else (oc + 1, 0)
                            load_g(*nxt)
                        b.wait("tensor", f"p3gl{s2}", gl_hist[(oc, q)])
                        for i in range(NI // 4):
                            uci = q * (NI // 4) + i
                            b.op("tensor", lambda h, s2=s2, i=i, oc=oc, uci=uci:
                                 h.matmul(
                                     out=self.psA[:, oc % 2, 0:TOKB],
                                     lhsT=self.gwb[:, s2, i, :],
                                     rhs=self.z1[:, uci, 0:TOKB],
                                     start=(uci == 0), stop=(uci == NH - 1),
                                     skip_group_check=True),
                                 "p3dn" if i == NI // 4 - 1 else None)
                        p3g_hist[s2] = b.tot("p3dn")
                    b.wait("vector", "p3dn", b.tot("p3dn"))
                    if hf == 0:
                        b.op("vector", lambda h, oc=oc: h.tensor_scalar_mul(
                            out=self.y32[:, oc, 0:TOKB],
                            in0=self.psA[:, oc % 2, 0:TOKB],
                            scalar1=self.v_sdn[:, oc:oc + 1]), "d_y0")
                    else:
                        b.wait("vector", f"p3hl{so}", hl_hist[oc])
                        b.op("vector", lambda h, oc=oc, so=so: h.scalar_tensor_tensor(
                            out=self.ysq[:, oc % 2], in0=self.psA[:, oc % 2, 0:TOKB],
                            scalar=self.v_sdn[:, oc:oc + 1], in1=self.hst[:, so, 0:TOKB],
                            op0=ALU.mult, op1=ALU.add), "d_y2")
                        b.wait("vector", "d_y2", b.tot("d_y2"))
                        b.op("vector", lambda h, oc=oc: h.tensor_tensor(
                            out=self.y32[:, oc, 0:TOKB], in0=self.y32[:, oc, 0:TOKB],
                            in1=self.ysq[:, oc % 2], op=ALU.add), "d_y2")
                        p3h_hist[so] = b.tot("d_y2")
                        b.wait("scalar", "d_y2", b.tot("d_y2"))
                        b.wait("scalar", "p3s2", b.tot("p3s2"))
                        b.wait("scalar", "d_h", b.tot("d_h"))
                        b.op("scalar", lambda h, oc=oc: h.square(
                            out=self.ysq[:, oc % 2], in_=self.y32[:, oc, 0:TOKB]),
                            "a_sq")
                        b.wait("tensor", "d_y2", b.tot("d_y2"))
                        b.op("tensor", lambda h, oc=oc: h.matmul(
                            out=self.psE[0:1, 0:TOKB], lhsT=self.c_ones_c32[:],
                            rhs=self.y32[:, oc, 0:TOKB], start=(oc == 0),
                            stop=(oc == ND - 1), skip_group_check=True), "p3sy")
                        b.wait("tensor", "a_sq", b.tot("a_sq"))
                        b.op("tensor", lambda h, oc=oc: h.matmul(
                            out=self.psF[0:1, 0:TOKB], lhsT=self.c_ones_c32[:],
                            rhs=self.ysq[:, oc % 2], start=(oc == 0),
                            stop=(oc == ND - 1), skip_group_check=True), "p3s2")
            self._ln_rows(v, "p3sy", "p3s2", D)
            trb_hist = {}
            dot_hist = {}
            for dc in range(ND):
                osl = dc % 2
                pstr = [self.psF, self.psC][osl]
                if dc == 0:
                    b.wait("vector", "d_b", b.tot("d_b"))
                b.op("vector", lambda h, dc=dc: h.tensor_sub(
                    out=self.ysq[:, dc % 2], in0=self.y32[:, dc, 0:TOKB],
                    in1=self.mb[:, 0:TOKB]), "g_t2")
                b.wait("vector", "p3tr", trb_hist.get(osl, 0))
                b.op("vector", lambda h, osl=osl, dc=dc: h.tensor_mul(
                    out=self.oap[:, osl, 0:TOKB], in0=self.ysq[:, dc % 2],
                    in1=self.rb[:, 0:TOKB]), "d_o")
                b.wait("tensor", "d_o", b.tot("d_o"))
                b.wait("tensor", "d_ot", dot_hist.get(osl, 0))
                for ts in range(TS):
                    b.op("tensor", lambda h, osl=osl, ts=ts, pstr=pstr: h.transpose(
                        out=pstr[:, ts * 128:(ts + 1) * 128],
                        in_=self.oap[:, osl, ts * 128:(ts + 1) * 128],
                        identity=self.c_eye32[:]), "p3tr" if ts == TS - 1 else None)
                trb_hist[osl] = b.tot("p3tr")
                b.wait("vector", "p3tr", b.tot("p3tr"))
                b.wait("vector", f"o_st{osl}", b.tot(f"o_st{osl}"))
                b.op("vector", lambda h, osl=osl, pstr=pstr: h.tensor_copy(
                    out=self.ot32[:, osl, 0:TS, :],
                    in_=pstr[:, 0:TS * 128].rearrange("p (s e) -> p s e", e=128)),
                    "d_ot")
                dot_hist[osl] = b.tot("d_ot")
                b.wait("gpsimd", "d_ot", b.tot("d_ot"))
                b.dma("gpsimd",
                      self.out.rearrange("(t p) d -> p t d", p=128)[
                          :, tb * TS:(tb + 1) * TS, dc * 128:(dc + 1) * 128],
                      self.ot32[:, osl, 0:TS, :], f"o_st{osl}")
        b.wait("gpsimd", "o_st0", b.tot("o_st0"))
        b.wait("gpsimd", "o_st1", b.tot("o_st1"))

    def _ln_rows(self, v, sy, s2, D):
        b = self
        TOKB = self.cfg["TOKB"]
        b.wait("vector", sy, b.tot(sy))
        b.wait("vector", "a_r", b.tot("a_r"))
        b.op("vector", lambda h: h.tensor_scalar(
            out=self.mrow[0:1, 0:TOKB], in0=self.psE[0:1, 0:TOKB],
            scalar1=1.0 / D, scalar2=None, op0=ALU.mult), "d_r")
        b.wait("vector", s2, b.tot(s2))
        b.op("vector", lambda h: h.tensor_scalar(
            out=self.vrow[0:1, 0:TOKB], in0=self.psF[0:1, 0:TOKB],
            scalar1=1.0 / D, scalar2=None, op0=ALU.mult), "d_r")
        b.wait("scalar", "d_r", b.tot("d_r"))
        b.op("scalar", lambda h: h.square(
            out=self.wrow[0:1, 0:TOKB], in_=self.mrow[0:1, 0:TOKB]), "a_r")
        b.wait("vector", "a_r", b.tot("a_r"))
        b.op("vector", lambda h: h.scalar_tensor_tensor(
            out=self.vrow[0:1, 0:TOKB], in0=self.vrow[0:1, 0:TOKB],
            scalar=LN_EPS, in1=self.wrow[0:1, 0:TOKB],
            op0=ALU.add, op1=ALU.subtract), "d_r")
        b.wait("scalar", "d_r", b.tot("d_r"))
        b.op("scalar", lambda h: h.activation(
            out=self.wrow[0:1, 0:TOKB], in_=self.vrow[0:1, 0:TOKB],
            func=AF.Ln), "a_r")
        b.wait("scalar", "a_r", b.tot("a_r"))
        b.op("scalar", lambda h: h.activation(
            out=self.rrow[0:1, 0:TOKB], in_=self.wrow[0:1, 0:TOKB],
            func=AF.Exp, scale=-0.5), "a_r")
        b.wait("tensor", "d_r", b.tot("d_r"))
        b.wait("tensor", "a_r", b.tot("a_r"))
        b.wait("tensor", "d_b", b.tot("d_b"))
        b.wait("tensor", "d_ot", b.tot("d_ot"))
        b.op("tensor", lambda h: h.matmul(
            out=self.psD[:, 0:TOKB], lhsT=self.c_o1x128_32[:],
            rhs=self.mrow[0:1, 0:TOKB], start=True, stop=True,
            skip_group_check=True), "p_br")
        b.op("tensor", lambda h: h.matmul(
            out=self.psC[:, 0:TOKB], lhsT=self.c_o1x128_32[:],
            rhs=self.rrow[0:1, 0:TOKB], start=True, stop=True,
            skip_group_check=True), "p_br")
        b.wait("vector", "p_br", b.tot("p_br"))
        b.op("vector", lambda h: h.tensor_copy(
            out=self.mb[:, 0:TOKB], in_=self.psD[:, 0:TOKB]), "d_b")
        b.op("vector", lambda h: h.tensor_copy(
            out=self.rb[:, 0:TOKB], in_=self.psC[:, 0:TOKB]), "d_b")

    def build(self):
        b = self
        self.cur = self.pre_ops
        self.emit_pre()
        pre_snapshot = dict(self.cnt)
        for v in range(4):
            self.cnt = dict(pre_snapshot)
            self.cur = self.var_ops[v]
            self.barrier_pre()
            self.emit_p0(v)
            self._drain_gpsimd(["ks_st0", "ks_st1", "vt_st0", "vt_st1",
                                "qs_st0", "qs_st1", "qn_st0", "qn_st1",
                                "xfs", "mkn"])
            self.barrier("bar0")
            wgen = self.emit_wprep(v)
            self.emit_p1(v, wgen)
            for _ in wgen:
                pass
            self._drain_gpsimd(["wp_s0", "wp_s1", "wt_s0", "wt_s1",
                                "wa_s0", "wa_s1", "wb_s0", "wb_s1",
                                "z_st0", "z_st1"])
            self.barrier("bar2")
            self.emit_p23(v)
        with self.es, self.nc.Block() as block:
            def section(eng_name):
                def emit(h):
                    for fn in self.pre_ops[eng_name]:
                        fn(h)
                    idx = h.bass.partition_id() % 4
                    for vv in h.Switch(idx, 4):
                        for fn in self.var_ops[vv][eng_name]:
                            fn(h)
                return emit
            block.sync(section("sync"))
            block.tensor(section("tensor"))
            block.vector(section("vector"))
            block.scalar(section("scalar"))
            block.gpsimd(section("gpsimd"))
        return self.nc

    def _drain_gpsimd(self, semnames):
        for s in semnames:
            if self.tot(s) > 0:
                self.wait("gpsimd", s, self.tot(s))


def build_nc(cfg):
    kb = KB(cfg)
    return kb.build()


_NC_CACHE = {}


def kernel(L0_attn_x, L0_attn_W_Q, L0_attn_W_K, L0_attn_W_V, L0_attn_W_O,
           L0_attn_b_Q, L0_attn_b_K, L0_attn_b_V, L0_attn_b_O,
           L0_ffn_SU_ug, L0_ffn_SU_dp, L0_ffn_Wscale_ug, L0_ffn_Wscale_dp,
           L0_ffn_W_up, L0_ffn_W_gate, L0_ffn_W_down, _small_cfg=None,
           _run=None):
    cfg = _small_cfg or make_cfg()
    H, D, S, NM = cfg["H"], cfg["D"], cfg["S"], cfg["NMAX"]
    x = np.ascontiguousarray(np.asarray(L0_attn_x), dtype=np.float32)
    consts = host_consts(cfg)
    base = {
        "wq": np.ascontiguousarray(L0_attn_W_Q, dtype=np.float32),
        "wk": np.ascontiguousarray(L0_attn_W_K, dtype=np.float32),
        "wv": np.ascontiguousarray(L0_attn_W_V, dtype=np.float32),
        "wo": np.ascontiguousarray(L0_attn_W_O, dtype=np.float32),
        "bq": np.ascontiguousarray(L0_attn_b_Q, dtype=np.float32).reshape(1, H * 128),
        "bk": np.ascontiguousarray(L0_attn_b_K, dtype=np.float32).reshape(1, H * 128),
        "bv": np.ascontiguousarray(L0_attn_b_V, dtype=np.float32).reshape(1, H * 128),
        "bo": np.ascontiguousarray(L0_attn_b_O, dtype=np.float32).reshape(-1, 128),
        "su_ug": np.ascontiguousarray(L0_ffn_SU_ug, dtype=np.float32).reshape(-1, 128),
        "su_dp": np.ascontiguousarray(L0_ffn_SU_dp, dtype=np.float32).reshape(-1, 128),
        "ws_ug": np.ascontiguousarray(L0_ffn_Wscale_ug, dtype=np.float32).reshape(-1, 128),
        "ws_dp": np.ascontiguousarray(L0_ffn_Wscale_dp, dtype=np.float32).reshape(-1, 128),
        "w_up": np.ascontiguousarray(L0_ffn_W_up, dtype=np.float32),
        "w_gate": np.ascontiguousarray(L0_ffn_W_gate, dtype=np.float32),
        "w_down": np.ascontiguousarray(L0_ffn_W_down, dtype=np.float32),
    }
    base.update(consts)
    in_maps = []
    for core in range(8):
        m = dict(base)
        m["xin"] = np.ascontiguousarray(x[core // 4])
        in_maps.append(m)
    key = (cfg["S"], cfg["D"])
    if key not in _NC_CACHE:
        _NC_CACHE[key] = build_nc(cfg)
    nc = _NC_CACHE[key]
    if _run is not None:
        results = _run(nc, in_maps)
    else:
        res = run_bass_kernel_spmd(nc, in_maps, core_ids=list(range(8)))
        results = [res.results[i] for i in range(8)]
    outp = np.empty((2, S, D), dtype=np.float32)
    bounds = cfg["bounds"]
    for core in range(8):
        bb, v = core // 4, core % 4
        s_v, e_v = bounds[v], bounds[v + 1]
        outp[bb, s_v:e_v, :] = np.asarray(results[core]["out"]).reshape(NM, D)[
            0:e_v - s_v]
    return outp


# version 15
# speedup vs baseline: 1.1385x; 1.0283x over previous
import contextlib
import math

import numpy as np
import ml_dtypes

import concourse.bass as bass
import concourse.mybir as mybir
from concourse.bass_utils import run_bass_kernel_spmd

F32 = mybir.dt.float32
BF16 = mybir.dt.bfloat16
BF = ml_dtypes.bfloat16
AF = mybir.ActivationFunctionType
ALU = mybir.AluOpType

ENGS = ("sync", "tensor", "vector", "scalar", "gpsimd")
LN_EPS = 1e-5
C_OFF = 10.0


def make_cfg(small=False):
    c = {}
    if small:
        c.update(S=1024, D=512, I=1024, H=2, TOKB=256,
                 bounds=(0, 256, 512, 768, 1024))
    else:
        c.update(S=8192, D=2048, I=8192, H=16, TOKB=512,
                 bounds=(0, 2560, 4608, 6656, 8192))
    c["B"] = 2
    c["DH"] = 128
    c["ND"] = c["D"] // 128
    c["NI"] = c["I"] // 128
    c["NMAX"] = max(b - a for a, b in zip(c["bounds"], c["bounds"][1:]))
    c["SQ"] = math.sqrt(float(c["DH"]))
    c["GRP"] = min(1536, c["S"])
    return c


def hadamard(n):
    h = np.array([[1.0]], dtype=np.float32)
    while h.shape[0] < n:
        h = np.block([[h, h], [h, -h]])
    return h


def host_consts(cfg):
    TOKB = cfg["TOKB"]
    masks = np.zeros((4, 128, TOKB), dtype=np.float32)
    for m in range(4):
        a = m * 128
        for p in range(128):
            masks[m, p, :] = (a + p <= np.arange(TOKB)).astype(np.float32)
    return {
        "eye32": np.eye(128, dtype=np.float32),
        "eye16": np.eye(128, dtype=np.float32).astype(BF),
        "h128c": hadamard(128).astype(BF),
        "hnic": hadamard(cfg["NI"]).astype(BF),
        "masksc": masks.astype(BF),
        "ones_row": np.ones((1, TOKB), dtype=np.float32),
        "ones_r16": np.ones((1, TOKB), dtype=np.float32).astype(BF),
        "ones_c16": np.ones((128, 1), dtype=np.float32).astype(BF),
        "ones_c32": np.ones((128, 1), dtype=np.float32),
        "o1x128": np.ones((1, 128), dtype=np.float32).astype(BF),
        "o1x128_32": np.ones((1, 128), dtype=np.float32),
    }


def _esize(dt):
    return 2 if dt == BF16 else 4


class KB:
    def __init__(self, cfg):
        self.cfg = cfg
        self.nc = bass.Bass("TRN2", target_bir_lowering=False, debug=False)
        self.es = contextlib.ExitStack()
        self.sems = {}
        self.cnt = {}
        self.var_ops = [{e: [] for e in ENGS} for _ in range(4)]
        self.pre_ops = {e: [] for e in ENGS}
        self.post_ops = {e: [] for e in ENGS}
        self.cur = None
        self._persist_off = 0
        self._semmap = {}
        self._phase_base = None
        self._phase_off = {}
        self._declare_io()
        self._alloc()

    ARENA_BYTES = 190 * 1024

    def _salloc(self, name, shape, dtype, pool):
        if not hasattr(self, "_arena_base"):
            arena = self.nc.alloc_sbuf_tensor("arena0", [128, self.ARENA_BYTES],
                                              mybir.dt.uint8)
            self._arena_base = self.nc.lookup_mloc(arena).addr
            self._persist_off = self._arena_base
        nbytes = int(np.prod(shape[1:])) * _esize(dtype)
        nbytes = (nbytes + 31) & ~31
        if pool == "persist":
            off = self._persist_off
            self._persist_off = off + nbytes
        else:
            base = self._phase_base
            if pool in ("wp", "wp2"):
                base = self._phase_base + 110 * 1024
            off = self._phase_off.get(pool, base)
            self._phase_off[pool] = off + nbytes
            if pool == "p1":
                assert off + nbytes <= self._phase_base + 110 * 1024, (name, off + nbytes)
        assert off + nbytes <= self._arena_base + self.ARENA_BYTES, (
            name, pool, off + nbytes - self._arena_base)
        t = self.nc.alloc_sbuf_tensor_at(name, shape, dtype, offset=off)
        return t

    _ENG_PHYS = {"tensor": "sT", "vector": "sV", "scalar": "sS",
                 "gpsimd": "sG", "sync": "sQ"}
    _DMA_PHYS = {"sync": "dS", "scalar": "dA", "gpsimd": "dG",
                 "vector": "dV", "tensor": "dT"}

    def _phys(self, name):
        return self._semmap[name]

    def sem_handle(self, phys):
        if phys not in self.sems:
            self.sems[phys] = self.es.enter_context(self.nc.semaphore(phys))
            self.cnt.setdefault(phys, 0)
        return self.sems[phys]

    def _map_name(self, name, eng, is_dma):
        if name in self._semmap:
            return self._semmap[name]
        if name.startswith("bar"):
            phys = name
        elif is_dma:
            import zlib
            digit = int(name[-1]) if name[-1] in "01" else 0
            base = name.rstrip("01")
            parity = (2 * (zlib.crc32(base.encode()) % 2) + digit) % 4
            phys = self._DMA_PHYS[eng] + str(parity)
        else:
            phys = self._ENG_PHYS[eng]
        self._semmap[name] = phys
        return phys

    def inc(self, name, amt, eng=None, is_dma=False):
        phys = self._map_name(name, eng, is_dma)
        s = self.sem_handle(phys)
        self.cnt[phys] = self.cnt.get(phys, 0) + amt
        return s, self.cnt[phys]

    def tot(self, name):
        if name not in self._semmap:
            return 0
        return self.cnt.get(self._semmap[name], 0)

    def on(self, eng, fn):
        self.cur[eng].append(fn)

    def dma(self, eng, out, in_, semname, transpose=False):
        phys = self._map_name(semname, eng, True)
        prev = self.cnt.get(phys, 0)
        s = self.sem_handle(phys)
        if prev > 0:
            self.on(eng, lambda h, s=s, prev=prev: h.wait_ge(s, prev))
        self.cnt[phys] = prev + 16
        def fn(h, out=out, in_=in_, s=s, transpose=transpose):
            h.dma_start(out=out, in_=in_, transpose=transpose).then_inc(s, 16)
        self.on(eng, fn)
        return prev + 16

    def wait(self, eng, semname, val):
        if val <= 0:
            return
        if semname not in self._semmap:
            return
        s = self.sem_handle(self._semmap[semname])
        self.on(eng, lambda h, s=s, val=val: h.wait_ge(s, val))

    def op(self, eng, fn, semname=None, amt=1):
        if semname is None:
            self.on(eng, fn)
            return None
        s, v = self.inc(semname, amt, eng=eng)
        def wrapped(h, fn=fn, s=s, amt=amt):
            fn(h).then_inc(s, amt)
        self.on(eng, wrapped)
        return v

    def _declare_io(self):
        nc, c = self.nc, self.cfg
        D, I, H, S, TOKB = c["D"], c["I"], c["H"], c["S"], c["TOKB"]
        NM, ND, NI = c["NMAX"], c["ND"], c["NI"]
        dt = nc.dram_tensor
        self.xin = dt("xin", [S, D], F32, kind="ExternalInput").ap()
        self.wq = dt("wq", [H, D, 128], F32, kind="ExternalInput").ap()
        self.wk = dt("wk", [H, D, 128], F32, kind="ExternalInput").ap()
        self.wv = dt("wv", [H, D, 128], F32, kind="ExternalInput").ap()
        self.wo = dt("wo", [H, 128, D], F32, kind="ExternalInput").ap()
        self.bq = dt("bq", [1, H * 128], F32, kind="ExternalInput").ap()
        self.bk = dt("bk", [1, H * 128], F32, kind="ExternalInput").ap()
        self.bv = dt("bv", [1, H * 128], F32, kind="ExternalInput").ap()
        self.bo = dt("bo", [ND, 128], F32, kind="ExternalInput").ap()
        self.su_ug = dt("su_ug", [ND, 128], F32, kind="ExternalInput").ap()
        self.su_dp = dt("su_dp", [NI, 128], F32, kind="ExternalInput").ap()
        self.ws_ug = dt("ws_ug", [2 * NI, 128], F32, kind="ExternalInput").ap()
        self.ws_dp = dt("ws_dp", [ND, 128], F32, kind="ExternalInput").ap()
        self.w_up = dt("w_up", [I, D], F32, kind="ExternalInput").ap()
        self.w_gate = dt("w_gate", [I, D], F32, kind="ExternalInput").ap()
        self.w_down = dt("w_down", [D, I], F32, kind="ExternalInput").ap()
        for nm, sh in {"eye32": [128, 128], "ones_row": [1, TOKB],
                       "ones_c32": [128, 1], "o1x128_32": [1, 128]}.items():
            setattr(self, nm, dt(nm, sh, F32, kind="ExternalInput").ap())
        shapes16 = {"eye16": [128, 128], "h128c": [128, 128], "hnic": [NI, NI],
                    "masksc": [4, 128, TOKB], "ones_r16": [1, TOKB],
                    "ones_c16": [128, 1], "o1x128": [1, 128]}
        for nm, sh in shapes16.items():
            setattr(self, nm, dt(nm, sh, BF16, kind="ExternalInput").ap())
        self.out = dt("out", [NM, D], F32, kind="ExternalOutput").ap()
        self.wqkv16 = dt("wqkv16", [3, H, D, 128], BF16)
        self.wo16s = dt("wo16s", [H, 128, D], BF16)
        self.k_s = dt("k_s", [H, 128, S], BF16)
        self.v_s = dt("v_s", [H, S, 128], BF16)
        self.q_s = dt("q_s", [H, 128, NM], BF16)
        self.qn_s = dt("qn_s", [H, NM], BF16)
        self.z_s = dt("z_s", [H, 128, NM], BF16)
        self.xfm_s = dt("xfm_s", [ND, 128, NM], F32)
        self.h_s = dt("h_s", [ND, 128, NM], F32)
        self.wup16 = dt("wup16", [I, D], BF16)
        self.wgt16 = dt("wgt16", [I, D], BF16)
        self.wdn16 = dt("wdn16", [D, I], BF16)
        self.wupT = dt("wupT", [I // TOKB, ND, 128, TOKB], BF16)
        self.wgtT = dt("wgtT", [I // TOKB, ND, 128, TOKB], BF16)
        self.u_s = dt("u_s", [NI, 128, D], BF16)
        self.wdnT = dt("wdnT", [NI, 128, D], BF16)
        self.g_s = dt("g_s", [ND, NI, 128, 128], BF16)

    def _alloc(self):
        nc, c = self.nc, self.cfg
        D, I, H, TOKB, ND, NI = c["D"], c["I"], c["H"], c["TOKB"], c["ND"], c["NI"]
        NM, S, GRP = c["NMAX"], c["S"], c["GRP"]
        P = lambda name, shape, dtype: self._salloc(name, shape, dtype, "persist")
        self.c_eye32 = P("c_eye32", [128, 128], F32)
        self.c_eye16 = P("c_eye16", [128, 128], BF16)
        self.c_h128 = P("c_h128", [128, 128], BF16)
        self.c_hni = P("c_hni", [NI, NI], BF16)
        self.c_masks = P("c_masks", [128, 4, TOKB], BF16)
        self.c_ones_row = P("c_ones_row", [1, TOKB], F32)
        self.c_ones_r16 = P("c_ones_r16", [1, TOKB], BF16)
        self.c_ones_c16 = P("c_ones_c16", [128, 1], BF16)
        self.c_ones_c32 = P("c_ones_c32", [128, 1], F32)
        self.c_o1x128 = P("c_o1x128", [1, 128], BF16)
        self.c_o1x128_32 = P("c_o1x128_32", [1, 128], F32)
        self.v_su_ug = P("v_su_ug", [128, ND], F32)
        self.v_sup = P("v_sup", [128, NI], F32)
        self.v_sgt = P("v_sgt", [128, NI], F32)
        self.v_sdn = P("v_sdn", [128, ND], F32)
        self.v_sudp = P("v_sudp", [128, NI], F32)
        self.v_bo = P("v_bo", [128, ND], F32)
        self.r_bq = P("r_bq", [1, H * 128], BF16)
        self.r_bk = P("r_bk", [1, H * 128], BF16)
        self.r_bv = P("r_bv", [1, H * 128], BF16)
        
        self.r_stage = P("r_stage", [128, 128], F32)
        self.knmax = P("knmax", [1, H * (S // TOKB)], F32)
        self.mkneg = P("mkneg", [1, H], F32)
        self._phase_base = ((self._persist_off + 63) & ~63)
        budget = self._arena_base + self.ARENA_BYTES
        assert self._phase_base - self._arena_base < 60 * 1024, self._phase_base

        def A(name, shape, dtype, pool):
            t = self._salloc(name, shape, dtype, pool)
            assert self._phase_off[pool] <= budget, (name, pool, self._phase_off[pool])
            return t
        self.nrm = A("nrm", [1, TOKB], F32, "p0")
        self.tmpr = A("tmpr", [1, TOKB], F32, "p0")
        self.xt32 = A("xt32", [128, TOKB // 128, D], F32, "p0")
        self.xt16 = A("xt16", [128, TOKB // 128, D], BF16, "p0")
        self.xfm16 = A("xfm16", [128, ND, GRP], BF16, "p0")
        self.xfm32 = A("xfm32", [128, ND, 128], F32, "p0")
        self.wh16 = A("wh16", [128, 2, 3, ND, 128], BF16, "p0")
        self.kvq16 = A("kvq16", [128, 3, 2, TOKB], BF16, "p0")
        self.vtr16 = A("vtr16", [128, 2, TOKB // 128, 128], BF16, "p0")
        self.sq16 = A("sq16", [128, 2, TOKB], BF16, "p0")
        self.nrow = A("nrow", [1, 2, TOKB], BF16, "p0")
        self.r_b32 = A("r_b32", [1, H * 128], F32, "p0")
        self.wcast32 = A("wcast32", [128, 2, 2048], F32, "wp2")
        self.wcast16 = A("wcast16", [128, 2, 2048], BF16, "wp2")
        self.wtT = A("wtT", [128, 16, TOKB], BF16, "wp")
        self.ubuf = A("ubuf", [128, 2, TOKB], BF16, "wp")
        self.sbB = A("sbB", [NI, 12, TOKB], BF16, "wp")
        self.gout = A("gout", [NI, 2, TOKB], BF16, "wp")
        self.kbuf = A("kbuf", [128, 2, S], BF16, "p1")
        self.vbuf = A("vbuf", [128, 2, S // 128, 128], BF16, "p1")
        self.qbuf = A("qbuf", [128, 2, NM], BF16, "p1")
        self.qnrow = A("qnrow", [1, 2, NM], BF16, "p1")
        self.ebuf = A("ebuf", [128, 3, 2, TOKB], BF16, "p1")
        self.emsk = A("emsk", [128, 2, 2, TOKB], BF16, "p1")
        self.dacc = A("dacc", [128, 2, TOKB], F32, "p1")
        self.crow = A("crow", [1, 2, TOKB], BF16, "p1")
        self.dinv16 = A("dinv16", [1, TOKB], BF16, "p1")
        self.dinvb = A("dinvb", [128, TOKB], BF16, "p1")
        self.lnrow = A("lnrow", [1, TOKB], F32, "p1")
        self.dinv32 = A("dinv32", [1, TOKB], F32, "p1")
        self.z16 = A("z16", [128, 2, TOKB], BF16, "p1")
        self.wob = A("wob", [128, 2, H * 128], BF16, "p23")
        self.xst = A("xst", [128, 2, TOKB], F32, "p23")
        self.y32 = A("y32", [128, ND, TOKB], F32, "p23")
        self.hst = A("hst", [128, 2, TOKB], F32, "p23")
        self.hb16 = A("hb16", [128, 2, ND, TOKB], BF16, "p23")
        assert int(math.log2(ND)) % 2 == 0
        self.h2b = self.hb16[:, 1]
        self.wtb = A("wtb", [128, 2, 2, ND, 128], BF16, "p23")
        self.g16 = A("g16", [128, TOKB], BF16, "p23")
        self.z1 = A("z1", [128, NI // 2, TOKB], BF16, "p23")
        self.gwb = A("gwb", [128, 2, NI // 4, 128], BF16, "p23")
        self.mrow = A("mrow", [1, TOKB], F32, "p23")
        self.vrow = A("vrow", [1, TOKB], F32, "p23")
        self.mb = A("mb", [128, TOKB], F32, "p23")
        self.rb = A("rb", [128, TOKB], F32, "p23")
        self.ot32 = A("ot32", [128, 2, TOKB // 128, 128], F32, "p23")
        self.ysq = A("ysq", [128, 2, TOKB], F32, "p23")
        self.oap = A("oap", [128, 2, TOKB], F32, "p23")
        self.wrow = A("wrow", [1, TOKB], F32, "p23")
        self.rrow = A("rrow", [1, TOKB], F32, "p23")
        for pool, off in self._phase_off.items():
            assert off <= budget, (pool, off - self._arena_base)
        PS = nc.alloc_psum_tensor
        self.psA = PS("psA", [128, 2, 512], F32)
        self.psB = PS("psB", [128, 2, 512], F32)
        self.psC = PS("psC", [128, 512], F32)
        self.psD = PS("psD", [128, 512], F32)
        self.psE = PS("psE", [128, 512], F32)
        self.psF = PS("psF", [128, 512], F32)


    def emit_pre(self):
        b, c = self, self.cfg
        D, I, H, TOKB, ND, NI = c["D"], c["I"], c["H"], c["TOKB"], c["ND"], c["NI"]
        self.cur = self.pre_ops
        consts = [
            (self.c_eye32, self.eye32), (self.c_eye16, self.eye16),
            (self.c_h128, self.h128c), (self.c_hni, self.hnic),
            (self.c_ones_row, self.ones_row), (self.c_ones_r16, self.ones_r16),
            (self.c_ones_c16, self.ones_c16), (self.c_ones_c32, self.ones_c32),
            (self.c_o1x128, self.o1x128), (self.c_o1x128_32, self.o1x128_32),
        ]
        for dst, src in consts:
            b.dma("sync", dst[:], src, "c_ld")
        b.dma("sync", self.c_masks[:], self.masksc.rearrange("m p t -> p m t"), "c_ld")
        for i, (dst, srcb) in enumerate([(self.r_bq, self.bq), (self.r_bk, self.bk),
                                         (self.r_bv, self.bv)]):
            if i > 0:
                b.wait("sync", "pre_dve", b.tot("pre_dve"))
            b.dma("sync", self.r_b32[:], srcb, "b_ld")
            b.wait("vector", "b_ld", b.tot("b_ld"))
            b.op("vector", lambda h, dst=dst: h.tensor_copy(
                out=dst[:], in_=self.r_b32[:]), "pre_dve")
        vecs = [
            (self.su_ug, ND, [(self.v_su_ug, 0, ND, 1.0)]),
            (self.su_dp, NI, [(self.v_sudp, 0, NI, 1.0)]),
            (self.ws_ug, 2 * NI, [(self.v_sup, 0, NI, 1.0 / math.sqrt(D)),
                                  (self.v_sgt, NI, NI, 1.0 / math.sqrt(D))]),
            (self.ws_dp, ND, [(self.v_sdn, 0, ND, 1.0 / math.sqrt(I))]),
            (self.bo, ND, [(self.v_bo, 0, ND, 1.0)]),
        ]
        for src, rows, outs in vecs:
            b.wait("sync", "pre_pe", b.tot("pre_pe"))
            b.dma("sync", self.r_stage[0:rows, :], src, "v_ld")
            b.wait("tensor", "v_ld", b.tot("v_ld"))
            b.wait("tensor", "pre_dve", b.tot("pre_dve"))
            b.op("tensor", lambda h, rows=rows: h.transpose(
                out=self.psF[0:128, 0:rows], in_=self.r_stage[0:rows, :],
                identity=self.c_eye32[0:rows, 0:rows]), "pre_pe")
            b.wait("vector", "pre_pe", b.tot("pre_pe"))
            for dst, off, n, scale in outs:
                if scale == 1.0:
                    b.op("vector", lambda h, dst=dst, off=off, n=n: h.tensor_copy(
                        out=dst[:, 0:n], in_=self.psF[:, off:off + n]), "pre_dve")
                else:
                    b.op("vector", lambda h, dst=dst, off=off, n=n, s=scale: h.tensor_scalar(
                        out=dst[:, 0:n], in0=self.psF[:, off:off + n],
                        scalar1=s, scalar2=None, op0=ALU.mult), "pre_dve")
        b.wait("vector", "pre_dve", b.tot("pre_dve"))
        b.op("vector", lambda h: h.tensor_tensor(
            out=self.v_sup[:], in0=self.v_sup[:], in1=self.v_sudp[:],
            op=ALU.mult), "pre_dve")
        jobs = []
        for m, w in [(0, self.wq), (1, self.wk), (2, self.wv)]:
            for hh in range(H):
                jobs.append((w[hh].rearrange("(c p) e -> p c e", p=128),
                             self.wqkv16[m, hh].rearrange("(c p) e -> p c e", p=128)))
        for hh in range(H):
            jobs.append((self.wo[hh], self.wo16s[hh]))
        list(self._cast_jobs(jobs, "wc"))
        self.pre_totals = dict(self.cnt)

    def barrier_pre(self):
        for eng in ENGS:
            for phys, val in self.pre_totals.items():
                if val > 0 and not phys.startswith("bar"):
                    h = self.sem_handle(phys)
                    self.on(eng, lambda hh, h=h, val=val: hh.wait_ge(h, val))

    def emit_p0(self, v):
        b, c = self, self.cfg
        D, H, TOKB, ND = c["D"], c["H"], c["TOKB"], c["ND"]
        GRP, S = c["GRP"], c["S"]
        s_v, e_v = c["bounds"][v], c["bounds"][v + 1]
        self._p0mm_after = {}
        self._wslot_after = {}
        NT = S // TOKB
        TS = TOKB // 128
        eye16, eye32 = self.c_eye16, self.c_eye32
        psq = [self.psA[:, 0], self.psA[:, 1], self.psB[:, 0], self.psB[:, 1]]
        n_own = 0
        ngr = (e_v + GRP - 1) // GRP
        qd_n = 0
        fq_n = 0
        it_n = 0
        xe_hist, xef_hist = {}, {}
        xfs_last = 0

        for g in range(ngr):
            gs, ge = g * GRP, min(e_v, (g + 1) * GRP)
            tiles = list(range(gs // TOKB, ge // TOKB))
            for ti, t in enumerate(tiles):
                own = (t * TOKB >= s_v)
                xr = self.xin[t * TOKB:(t + 1) * TOKB].rearrange(
                    "(s p) d -> p s d", p=128)
                xld_h, xc_h = {}, {}
                for ts in range(TS):
                    if g + ti > 0:
                        b.wait("sync", "xc", b.tot("xc"))
                        if b.tot("ptf") > 0:
                            b.wait("sync", "ptf", b.tot("ptf"))
                    b.dma("sync", self.xt32[:, ts], xr[:, ts], "xld")
                    xld_h[ts] = b.tot("xld")
                for ts in range(TS):
                    b.wait("scalar", "xld", xld_h[ts])
                    if b.tot("pt") > 0:
                        b.wait("scalar", "pt", b.tot("pt") if ts == 0 else 0)
                    b.op("scalar", lambda h, ts=ts: h.copy(
                        out=self.xt16[:, ts], in_=self.xt32[:, ts]), "xc")
                    xc_h[ts] = b.tot("xc")
                for ts in range(TS):
                    xc_now = xc_h[ts]
                    for dq in range(ND // 4):
                        slot = qd_n % 4
                        ps16 = psq[slot].bitcast(BF16)
                        b.wait("tensor", "xc", xc_now)
                        if qd_n >= 4:
                            b.wait("tensor", "xe", xe_hist[qd_n - 4])
                        for j in range(4):
                            dc = dq * 4 + j
                            b.op("tensor",
                                 lambda h, ts=ts, dc=dc, ps16=ps16, j=j: h.transpose(
                                     out=ps16[:, j * 128:(j + 1) * 128],
                                     in_=self.xt16[:, ts, dc * 128:(dc + 1) * 128],
                                     identity=eye16[:]),
                                 "pt" if j == 3 else None)
                        b.wait("vector", "pt", b.tot("pt"))
                        go = (t * TOKB - gs) + ts * 128
                        b.op("vector", lambda h, dq=dq, ps16=ps16, go=go: h.tensor_copy(
                            out=self.xfm16[:, dq * 4:(dq + 1) * 4, go:go + 128],
                            in_=ps16[:, 0:512].rearrange("p (c e) -> p c e", e=128)),
                            "xe")
                        xe_hist[qd_n] = b.tot("xe")
                        qd_n += 1
                if own:
                    for ts in range(TS):
                        if n_own > 0 or ts > 0:
                            b.wait("vector", "xfs", xfs_last)
                        for dq in range(ND // 4):
                            ps32 = [self.psC, self.psF][fq_n % 2]
                            if fq_n >= 2:
                                b.wait("tensor", "xef", xef_hist[fq_n - 2])
                            for j in range(4):
                                dc = dq * 4 + j
                                b.op("tensor",
                                     lambda h, ts=ts, dc=dc, ps32=ps32, j=j: h.transpose(
                                         out=ps32[:, j * 128:(j + 1) * 128],
                                         in_=self.xt32[:, ts, dc * 128:(dc + 1) * 128],
                                         identity=eye32[:]),
                                     "ptf" if j == 3 else None)
                            b.wait("vector", "ptf", b.tot("ptf"))
                            b.op("vector", lambda h, dq=dq, ps32=ps32: h.tensor_copy(
                                out=self.xfm32[:, dq * 4:(dq + 1) * 4, :],
                                in_=ps32.rearrange("p (c e) -> p c e", e=128)), "xef")
                            xef_hist[fq_n] = b.tot("xef")
                            fq_n += 1
                        b.wait("gpsimd", "xef", b.tot("xef"))
                        xfs_last = b.dma("gpsimd",
                            self.xfm_s.rearrange("c p n -> p c n")[
                                :, :, (t * TOKB - s_v) + ts * 128:
                                (t * TOKB - s_v) + ts * 128 + 128],
                            self.xfm32[:], "xfs")
                    n_own += 1
            for hh in range(H):
                slot = hh % 2
                wsem = f"whl{slot}"
                if slot in self._wslot_after:
                    b.wait("sync", "p0v", self._wslot_after[slot])
                for m in range(3):
                    b.dma("sync", self.wh16[:, slot, m],
                          self.wqkv16[m, hh].rearrange("(c p) e -> p c e", p=128), wsem)
                b.wait("tensor", wsem, b.tot(wsem))
                for ti, t in enumerate(tiles):
                    own = (t * TOKB >= s_v)
                    sl = it_n % 2
                    goff = t * TOKB - gs
                    if it_n >= 2:
                        b.wait("vector", f"ks_st{sl}", b.tot(f"ks_st{sl}"))
                        b.wait("vector", f"vt_st{sl}", b.tot(f"vt_st{sl}"))
                        b.wait("vector", f"qs_st{sl}", b.tot(f"qs_st{sl}"))
                    b.wait("tensor", "xe", b.tot("xe"))
                    for dc in range(ND):
                        b.op("tensor", lambda h, dc=dc, slot=slot, goff=goff: h.matmul(
                            out=self.psA[:, 0, 0:TOKB], lhsT=self.wh16[:, slot, 1, dc],
                            rhs=self.xfm16[:, dc, goff:goff + TOKB],
                            start=(dc == 0), stop=False, skip_group_check=True))
                    b.op("tensor", lambda h, hh=hh: h.matmul(
                        out=self.psA[:, 0, 0:TOKB], lhsT=self.r_bk[0:1, hh * 128:(hh + 1) * 128],
                        rhs=self.c_ones_r16[0:1, :], start=False, stop=True,
                        skip_group_check=True), "p0k")
                    for dc in range(ND):
                        b.op("tensor", lambda h, dc=dc, slot=slot, goff=goff: h.matmul(
                            out=self.psA[:, 1, 0:TOKB], lhsT=self.wh16[:, slot, 2, dc],
                            rhs=self.xfm16[:, dc, goff:goff + TOKB],
                            start=(dc == 0), stop=False, skip_group_check=True))
                    b.op("tensor", lambda h, hh=hh: h.matmul(
                        out=self.psA[:, 1, 0:TOKB], lhsT=self.r_bv[0:1, hh * 128:(hh + 1) * 128],
                        rhs=self.c_ones_r16[0:1, :], start=False, stop=True,
                        skip_group_check=True), "p0v")
                    if own:
                        for dc in range(ND):
                            b.op("tensor", lambda h, dc=dc, slot=slot, goff=goff: h.matmul(
                                out=self.psB[:, 0, 0:TOKB], lhsT=self.wh16[:, slot, 0, dc],
                                rhs=self.xfm16[:, dc, goff:goff + TOKB],
                                start=(dc == 0), stop=False, skip_group_check=True))
                        b.op("tensor", lambda h, hh=hh: h.matmul(
                            out=self.psB[:, 0, 0:TOKB], lhsT=self.r_bq[0:1, hh * 128:(hh + 1) * 128],
                            rhs=self.c_ones_r16[0:1, :], start=False, stop=True,
                            skip_group_check=True), "p0q")
                    self._wslot_after[slot] = b.tot("p0v")
                    b.wait("vector", "p0k", b.tot("p0k"))
                    b.op("vector", lambda h, sl=sl: h.tensor_copy(
                        out=self.kvq16[:, 1, sl], in_=self.psA[:, 0, 0:TOKB]), "p0ek")
                    b.wait("vector", "p0v", b.tot("p0v"))
                    b.op("vector", lambda h, sl=sl: h.tensor_copy(
                        out=self.kvq16[:, 2, sl], in_=self.psA[:, 1, 0:TOKB]), "p0ev")
                    if own:
                        b.wait("vector", "p0q", b.tot("p0q"))
                        b.op("vector", lambda h, sl=sl: h.tensor_copy(
                            out=self.kvq16[:, 0, sl], in_=self.psB[:, 0, 0:TOKB]), "p0eq")
                    b.wait("tensor", "p0ev", b.tot("p0ev"))
                    if it_n >= 1:
                        b.wait("tensor", "p0evt", b.tot("p0evt"))
                    psv16 = self.psB[:, 1].bitcast(BF16)
                    for ts in range(TS):
                        b.op("tensor", lambda h, ts=ts, sl=sl, psv16=psv16: h.transpose(
                            out=psv16[:, ts * 128:(ts + 1) * 128],
                            in_=self.kvq16[:, 2, sl][:, ts * 128:(ts + 1) * 128],
                            identity=eye16[:]), "p0vt" if ts == TS - 1 else None)
                    b.wait("vector", "p0vt", b.tot("p0vt"))
                    b.op("vector", lambda h, sl=sl, psv16=psv16: h.tensor_copy(
                        out=self.vtr16[:, sl], in_=psv16[:, 0:TS * 128].rearrange(
                            "p (s e) -> p s e", e=128)), "p0evt")
                    b.wait("scalar", "p0ek", b.tot("p0ek"))
                    if it_n >= 1:
                        b.wait("scalar", "p0n", b.tot("p0n"))
                    b.op("scalar", lambda h, sl=sl: h.square(
                        out=self.sq16[:, sl], in_=self.kvq16[:, 1, sl]), "p0s")
                    b.wait("tensor", "p0s", b.tot("p0s"))
                    if it_n >= 1:
                        b.wait("tensor", "p0r", b.tot("p0r"))
                        b.wait("tensor", "p0rs", b.tot("p0rs"))
                    b.op("tensor", lambda h, sl=sl: h.matmul(
                        out=self.psE[0:1, 0:TOKB], lhsT=self.c_ones_c16[:],
                        rhs=self.sq16[:, sl], start=True, stop=True,
                        skip_group_check=True), "p0n")
                    b.wait("vector", "p0n", b.tot("p0n"))
                    b.op("vector", lambda h, hh=hh, t=t: h.reduce_max(
                        out=self.knmax[0:1, hh * NT + t:hh * NT + t + 1],
                        in_=self.psE[0:1, 0:TOKB], axis=mybir.AxisListType.X), "p0r")
                    if own:
                        b.wait("scalar", "p0eq", b.tot("p0eq"))
                        b.wait("scalar", "p0n", b.tot("p0n"))
                        b.op("scalar", lambda h, sl=sl: h.square(
                            out=self.sq16[:, sl], in_=self.kvq16[:, 0, sl]), "p0s")
                        b.wait("tensor", "p0s", b.tot("p0s"))
                        b.wait("tensor", "p0r", b.tot("p0r"))
                        b.wait("tensor", "p0rs", b.tot("p0rs"))
                        b.op("tensor", lambda h, sl=sl: h.matmul(
                            out=self.psE[0:1, 0:TOKB], lhsT=self.c_ones_c16[:],
                            rhs=self.sq16[:, sl], start=True, stop=True,
                            skip_group_check=True), "p0n")
                        b.wait("scalar", "p0n", b.tot("p0n"))
                        b.wait("scalar", f"qn_st{sl}", b.tot(f"qn_st{sl}"))
                        b.op("scalar", lambda h, sl=sl: h.activation(
                            out=self.nrow[0:1, sl], in_=self.psE[0:1, 0:TOKB],
                            func=AF.Sqrt), "p0rs")
                    b.wait("gpsimd", "p0ek", b.tot("p0ek"))
                    b.dma("gpsimd", self.k_s[hh][:, t * TOKB:(t + 1) * TOKB],
                          self.kvq16[:, 1, sl], f"ks_st{sl}")
                    b.wait("gpsimd", "p0evt", b.tot("p0evt"))
                    b.dma("gpsimd", self.v_s[hh].rearrange(
                        "(s p) e -> p s e", p=128)[:, t * TS:(t + 1) * TS, :],
                        self.vtr16[:, sl], f"vt_st{sl}")
                    if own:
                        b.wait("gpsimd", "p0eq", b.tot("p0eq"))
                        b.dma("gpsimd", self.q_s[hh][:, (t * TOKB - s_v):
                                                     (t * TOKB - s_v) + TOKB],
                              self.kvq16[:, 0, sl], f"qs_st{sl}")
                        b.wait("gpsimd", "p0rs", b.tot("p0rs"))
                        b.dma("gpsimd", self.qn_s[hh:hh + 1,
                                                  (t * TOKB - s_v):(t * TOKB - s_v) + TOKB],
                              self.nrow[0:1, sl], f"qn_st{sl}")
                    it_n += 1
        b.wait("vector", "p0r", b.tot("p0r"))
        for hh in range(H):
            b.op("vector", lambda h, hh=hh: h.reduce_max(
                out=self.tmpr[0:1, hh:hh + 1],
                in_=self.knmax[0:1, hh * NT:hh * NT + (e_v // TOKB)],
                axis=mybir.AxisListType.X), "mkr")
        b.wait("scalar", "mkr", b.tot("mkr"))
        b.op("scalar", lambda h: h.activation(
            out=self.nrm[0:1, 0:H], in_=self.tmpr[0:1, 0:H], func=AF.Sqrt), "mks")
        b.wait("gpsimd", "mks", b.tot("mks"))
        b.op("gpsimd", lambda h: h.tensor_scalar_mul(
            out=self.mkneg[0:1, 0:H], in0=self.nrm[0:1, 0:H], scalar1=-1.0), "mkn")


    def barrier(self, name):
        s, tot = self.inc(name, 5, eng="sync")
        for eng in ENGS:
            phys = self._ENG_PHYS[eng]
            if self.cnt.get(phys, 0) > 0:
                hh = self.sem_handle(phys)
                val = self.cnt[phys]
                self.on(eng, lambda h, hh=hh, val=val: h.wait_ge(hh, val))
            self.on(eng, lambda h, s=s: h.nop().then_inc(s, 1))
        for eng in ENGS:
            self.on(eng, lambda h, s=s, tot=tot: h.wait_ge(s, tot))

    def _cast_jobs(self, jobs, tag, chunk=0, cast_eng="scalar"):
        b = self
        lh = {}

        def bufs(n):
            sap, dap = jobs[n]
            sl = n % 2
            w = int(np.prod(sap.shape[1:]))
            src32 = self.wcast32[:, sl, 0:w]
            src16 = self.wcast16[:, sl, 0:w]
            shp = sap.shape[1:]
            if len(shp) == 2:
                src32 = src32.rearrange("p (c e) -> p c e", e=shp[1])
                src16 = src16.rearrange("p (c e) -> p c e", e=shp[1])
            return sap, dap, src32, src16, sl

        def load(n):
            sap, dap, src32, src16, sl = bufs(n)
            if n >= 2:
                b.wait("sync", f"{tag}_c", b.tot(f"{tag}_c"))
            b.dma("sync", src32, sap, f"{tag}_l{sl}")
            lh[n] = b.tot(f"{tag}_l{sl}")

        load(0)
        for n in range(len(jobs)):
            sap, dap, src32, src16, sl = bufs(n)
            if n + 1 < len(jobs):
                load(n + 1)
            b.wait(cast_eng, f"{tag}_l{sl}", lh[n])
            b.wait(cast_eng, f"{tag}_s{sl}", b.tot(f"{tag}_s{sl}"))
            if cast_eng == "scalar":
                b.op(cast_eng, lambda h, a=src16, bb=src32: h.copy(out=a, in_=bb),
                     f"{tag}_c")
            else:
                b.op(cast_eng, lambda h, a=src16, bb=src32: h.tensor_copy(
                    out=a, in_=bb), f"{tag}_c")
            b.wait("gpsimd", f"{tag}_c", b.tot(f"{tag}_c"))
            b.dma("gpsimd", dap, src16, f"{tag}_s{sl}")
            if chunk and n % chunk == chunk - 1:
                yield

    def emit_wprep(self, v):
        b, c = self, self.cfg
        D, I, TOKB, ND, NI = c["D"], c["I"], c["TOKB"], c["ND"], c["NI"]
        jobs = []
        for wsrc, wdst in ((self.w_up, self.wup16), (self.w_gate, self.wgt16),
                           (self.w_down, self.wdn16)):
            wid = wsrc.shape[1]
            cw = min(2048, wid)
            for r in range(wsrc.shape[0] // 128):
                for col in range(max(1, wid // cw)):
                    jobs.append((wsrc[r * 128:(r + 1) * 128, col * cw:(col + 1) * cw],
                                 wdst[r * 128:(r + 1) * 128, col * cw:(col + 1) * cw]))
        yield from self._cast_jobs(jobs, "wp", chunk=8, cast_eng="vector")
        for eng in ("sync", "gpsimd"):
            b.wait(eng, "wp_s0", b.tot("wp_s0"))
            b.wait(eng, "wp_s1", b.tot("wp_s1"))
        tjobs = []
        for wsrc, wdst in ((self.wup16, self.wupT), (self.wgt16, self.wgtT)):
            for ug in range(I // TOKB):
                for dc in range(ND):
                    tjobs.append((wsrc[ug * TOKB:(ug + 1) * TOKB,
                                       dc * 128:(dc + 1) * 128], wdst[ug, dc]))
        for a in range(NI):
            for ob in range(D // TOKB):
                tjobs.append((self.wdn16[ob * TOKB:(ob + 1) * TOKB,
                                         a * 128:(a + 1) * 128],
                              self.wdnT[a][:, ob * TOKB:(ob + 1) * TOKB]))
        tl_hist = {}

        def load_t(n):
            sl = n % 16
            b.wait("sync", f"wt_s{sl % 2}", b.tot(f"wt_s{sl % 2}"))
            b.dma("sync", self.wtT[:, sl, 0:TOKB], tjobs[n][0],
                  f"wt_l{sl % 2}", transpose=True)
            tl_hist[n] = b.tot(f"wt_l{sl % 2}")

        for k in range(min(12, len(tjobs))):
            load_t(k)
        for n in range(len(tjobs)):
            sl = n % 16
            if n + 12 < len(tjobs):
                load_t(n + 12)
            b.wait("gpsimd", f"wt_l{sl % 2}", tl_hist[n])
            b.dma("gpsimd", tjobs[n][1], self.wtT[:, sl, 0:TOKB], f"wt_s{sl % 2}")
            if n % 16 == 15:
                yield
        b.wait("sync", "wt_s0", b.tot("wt_s0"))
        b.wait("sync", "wt_s1", b.tot("wt_s1"))
        ja = [(a, ob) for a in range(NI) for ob in range(D // TOKB)]
        wal_hist = {}

        def load_a(n):
            a, ob = ja[n]
            sl = n % 16
            b.wait("sync", f"wa_m{sl % 2}", b.tot(f"wa_m{sl % 2}"))
            b.dma("sync", self.wtT[:, sl, 0:TOKB],
                  self.wdnT[a][:, ob * TOKB:(ob + 1) * TOKB], f"wa_l{sl % 2}")
            wal_hist[n] = b.tot(f"wa_l{sl % 2}")

        for k in range(min(12, len(ja))):
            load_a(k)
        for n in range(len(ja)):
            a, ob = ja[n]
            sl = n % 16
            esl = n % 2
            if n + 12 < len(ja):
                load_a(n + 12)
            b.wait("tensor", f"wa_l{sl % 2}", wal_hist[n])
            b.wait("tensor", f"wa_e{esl}", b.tot(f"wa_e{esl}"))
            b.op("tensor", lambda h, sl=sl, esl=esl: h.matmul(
                out=self.psA[:, esl, 0:TOKB], lhsT=self.c_h128[:],
                rhs=self.wtT[:, sl, 0:TOKB], start=True, stop=True,
                skip_group_check=True), f"wa_m{sl % 2}")
            b.wait("vector", f"wa_m{sl % 2}", b.tot(f"wa_m{sl % 2}"))
            b.wait("vector", f"wa_s{esl}", b.tot(f"wa_s{esl}"))
            b.op("vector", lambda h, esl=esl: h.tensor_copy(
                out=self.ubuf[:, esl, 0:TOKB], in_=self.psA[:, esl, 0:TOKB]),
                f"wa_e{esl}")
            b.wait("gpsimd", f"wa_e{esl}", b.tot(f"wa_e{esl}"))
            b.dma("gpsimd", self.u_s[a][:, ob * TOKB:(ob + 1) * TOKB],
                  self.ubuf[:, esl, 0:TOKB], f"wa_s{esl}")
            if n % 16 == 15:
                yield
        b.wait("sync", "wa_s0", b.tot("wa_s0"))
        b.wait("sync", "wa_s1", b.tot("wa_s1"))
        jb = [(bb, ob) for bb in range(128) for ob in range(D // TOKB)]
        wbl_hist = {}
        TS4 = TOKB // 128

        def load_b(n):
            bb, ob = jb[n]
            sl = n % 12
            b.wait("sync", f"wb_m{sl % 2}", b.tot(f"wb_m{sl % 2}"))
            b.dma("sync", self.sbB[:, sl, 0:TOKB],
                  self.u_s[:, bb, ob * TOKB:(ob + 1) * TOKB], f"wb_l{sl % 2}")
            wbl_hist[n] = b.tot(f"wb_l{sl % 2}")

        for k in range(min(10, len(jb))):
            load_b(k)
        for n in range(len(jb)):
            bb, ob = jb[n]
            sl = n % 12
            esl = n % 2
            if n + 10 < len(jb):
                load_b(n + 10)
            b.wait("tensor", f"wb_l{sl % 2}", wbl_hist[n])
            b.wait("tensor", f"wb_e{esl}", b.tot(f"wb_e{esl}"))
            b.op("tensor", lambda h, sl=sl, esl=esl: h.matmul(
                out=self.psB[0:NI, esl, 0:TOKB], lhsT=self.c_hni[:],
                rhs=self.sbB[0:NI, sl, 0:TOKB], start=True, stop=True,
                skip_group_check=True), f"wb_m{sl % 2}")
            b.wait("vector", f"wb_m{sl % 2}", b.tot(f"wb_m{sl % 2}"))
            b.wait("vector", f"wb_s{esl}", b.tot(f"wb_s{esl}"))
            b.op("vector", lambda h, esl=esl: h.tensor_copy(
                out=self.gout[:, esl, 0:TOKB], in_=self.psB[0:NI, esl, 0:TOKB]),
                f"wb_e{esl}")
            b.wait("gpsimd", f"wb_e{esl}", b.tot(f"wb_e{esl}"))
            b.dma("gpsimd",
                  self.g_s.rearrange("c i b o -> i b c o")[
                      :, bb, ob * TS4:(ob + 1) * TS4, :],
                  self.gout[0:NI, esl, 0:TOKB].rearrange(
                      "a (c o) -> a c o", o=128), f"wb_s{esl}")
            if n % 16 == 15:
                yield

    def emit_p1(self, v, wgen=None):
        b, c = self, self.cfg
        H, TOKB, S = c["H"], c["TOKB"], c["S"]
        s_v, e_v = c["bounds"][v], c["bounds"][v + 1]
        n_v = e_v - s_v
        TS = TOKB // 128
        nqb = n_v // TOKB
        gp = 0
        gq = 0
        exp_hist, dacc_hist, av_hist, dm_hist = {}, {}, {}, {}
        self._crow_hist = {}
        after = {0: {}, 1: {}}
        lv_hist = {}
        dz_hist, pd_hist = {}, {}
        pend_close = []

        def load_head(hh):
            hs = hh % 2
            lsem = f"p1l{hs}"
            for key, val in after[hs].items():
                b.wait("sync", key, val)
            b.dma("sync", self.kbuf[:, hs, 0:e_v], self.k_s[hh][:, 0:e_v], lsem)
            b.dma("sync", self.vbuf[:, hs, 0:e_v // 128, :],
                  self.v_s[hh].rearrange("(s p) e -> p s e", p=128)[:, 0:e_v // 128, :],
                  lsem)
            b.dma("sync", self.qbuf[:, hs, 0:n_v], self.q_s[hh][:, 0:n_v], lsem)
            b.dma("sync", self.qnrow[0:1, hs, 0:n_v], self.qn_s[hh:hh + 1, 0:n_v], lsem)
            lv_hist[hh] = b.tot(lsem)

        load_head(0)
        for hh in range(H):
            hs = hh % 2
            lsem = f"p1l{hs}"
            if hh + 1 < H:
                load_head(hh + 1)
            lv = lv_hist[hh]
            for eng in ("tensor", "vector"):
                b.wait(eng, lsem, lv)
            cr_hist = {}

            def emit_crow(qbi2, hs=hs, hh=hh, cr_hist=cr_hist):
                cs2 = qbi2 % 2
                b.wait("vector", "p1s", self._crow_hist.get(cs2, 0))
                b.op("vector", lambda h, hs=hs, qbi=qbi2, hh=hh, cs=cs2:
                     h.tensor_scalar(
                         out=self.crow[0:1, cs, 0:TOKB],
                         in0=self.qnrow[0:1, hs, qbi * TOKB:(qbi + 1) * TOKB],
                         scalar1=self.mkneg[0:1, hh:hh + 1],
                         scalar2=C_OFF * c["SQ"], op0=ALU.mult, op1=ALU.add), "d_cr")
                cr_hist[qbi2] = b.tot("d_cr")

            emit_crow(0)
            for qbi in range(nqb):
                qa = s_v + qbi * TOKB
                n_k = (qa + TOKB) // 128
                npairs = n_k // 2
                cs = qbi % 2
                qpar = gq % 2
                psAV = [self.psC, self.psF][qpar]
                dslice = self.dacc[:, qpar]
                b.wait("tensor", "d_cr", cr_hist[qbi])

                pend = None

                def emit_av(srcs, lp, gpp, psAV=psAV, qpar=qpar, n_k=n_k, hs=hs):
                    if lp == 0:
                        b.wait("tensor", "d_z", dz_hist.get(qpar, 0))
                    for j in range(2):
                        kc = 2 * lp + j
                        b.op("tensor", lambda h, hs=hs, kc=kc, sj=srcs[j], lp=lp,
                             j=j, n_k=n_k, psAV=psAV: h.matmul(
                                 out=psAV[:, 0:TOKB],
                                 lhsT=self.vbuf[:, hs, kc, :],
                                 rhs=sj, start=(lp == 0 and j == 0),
                                 stop=(kc == n_k - 1),
                                 skip_group_check=True), "p_av" if j == 1 else None)
                    av_hist[gpp] = b.tot("p_av")

                for lp in range(npairs):
                    psX = [self.psA, self.psB][gp % 2]
                    se = gp % 3
                    if gp >= 2:
                        b.wait("tensor", "a_e", exp_hist[gp - 2])
                    for j in range(2):
                        kc = 2 * lp + j
                        b.op("tensor", lambda h, psX=psX, j=j, hs=hs, kc=kc, qbi=qbi:
                             h.matmul(out=psX[:, j, 0:TOKB],
                                      lhsT=self.kbuf[:, hs, kc * 128:(kc + 1) * 128],
                                      rhs=self.qbuf[:, hs, qbi * TOKB:(qbi + 1) * TOKB],
                                      start=True, stop=False, skip_group_check=True))
                        b.op("tensor", lambda h, psX=psX, j=j, cs=cs: h.matmul(
                            out=psX[:, j, 0:TOKB], lhsT=self.c_o1x128[:],
                            rhs=self.crow[0:1, cs, 0:TOKB], start=False, stop=True,
                            skip_group_check=True), "p1s" if j == 1 else None)
                    if pend is not None:
                        b.wait("tensor", "a_e", exp_hist[gp - 1])
                        b.wait("tensor", "d_m", dm_hist.get(gp - 1, 0))
                        emit_av(*pend)
                        pend = None
                    b.wait("scalar", "p1s", b.tot("p1s"))
                    if gp >= 3:
                        b.wait("scalar", "d_d", dacc_hist[gp - 3])
                        b.wait("scalar", "p_av", av_hist.get(gp - 3) or 0)
                    b.op("scalar", lambda h, psX=psX, se=se: h.activation(
                        out=self.ebuf[:, se], in_=psX[:, :, 0:TOKB], func=AF.Exp,
                        scale=1.0 / c["SQ"]), "a_e")
                    exp_hist[gp] = b.tot("a_e")
                    b.wait("vector", "a_e", b.tot("a_e"))
                    srcs = []
                    ep = gp % 2
                    for j in range(2):
                        kc = 2 * lp + j
                        if kc >= qa // 128:
                            m = kc - qa // 128
                            b.wait("vector", "p_av", b.tot("p_av"))
                            b.op("vector", lambda h, se=se, j=j, m=m, ep=ep:
                                 h.tensor_tensor(
                                     out=self.emsk[:, ep, j], in0=self.ebuf[:, se, j],
                                     in1=self.c_masks[:, m, :], op=ALU.mult), "d_m")
                            b.wait("vector", "d_m", b.tot("d_m"))
                            srcs.append(self.emsk[:, ep, j])
                        else:
                            srcs.append(self.ebuf[:, se, j])
                    dm_hist[gp] = b.tot("d_m")
                    for j in range(2):
                        if lp == 0 and j == 0:
                            b.wait("vector", "p_d", pd_hist.get(qpar, 0))
                            b.op("vector", lambda h, s0=srcs[0], dslice=dslice:
                                 h.tensor_copy(out=dslice, in_=s0), "d_d")
                        else:
                            b.wait("vector", "d_d", b.tot("d_d"))
                            b.op("vector", lambda h, sj=srcs[j], dslice=dslice:
                                 h.tensor_tensor(
                                     out=dslice, in0=dslice,
                                     in1=sj, op=ALU.add), "d_d")
                    dacc_hist[gp] = b.tot("d_d")
                    pend = (srcs, lp, gp)
                    gp += 1
                    if pend_close and lp <= 1:
                        pend_close.pop(0)()
                    if lp == 0 and qbi + 1 < nqb:
                        emit_crow(qbi + 1)
                b.wait("tensor", "a_e", exp_hist[gp - 1])
                b.wait("tensor", "d_m", dm_hist.get(gp - 1, 0))
                emit_av(*pend)
                self._crow_hist[cs] = b.tot("p1s")
                while pend_close:
                    pend_close.pop(0)()
                dd_tot = b.tot("d_d")
                pav_tot = b.tot("p_av")
                zs = qbi % 2

                def close1(dd_tot=dd_tot, dslice=dslice, qpar=qpar):
                    b.wait("tensor", "d_d", dd_tot)
                    b.wait("tensor", "a_l", b.tot("a_l"))
                    b.op("tensor", lambda h, dslice=dslice: h.matmul(
                        out=self.psE[0:1, 0:TOKB], lhsT=self.c_ones_c32[:],
                        rhs=dslice, start=True, stop=True,
                        skip_group_check=True), "p_d")
                    pd_hist[qpar] = b.tot("p_d")
                    b.wait("scalar", "p_d", b.tot("p_d"))
                    b.op("scalar", lambda h: h.activation(
                        out=self.lnrow[0:1, 0:TOKB], in_=self.psE[0:1, 0:TOKB],
                        func=AF.Ln), "a_l")
                    b.wait("scalar", "a_l", b.tot("a_l"))
                    b.wait("scalar", "d_c", b.tot("d_c"))
                    b.op("scalar", lambda h: h.activation(
                        out=self.dinv32[0:1, 0:TOKB], in_=self.lnrow[0:1, 0:TOKB],
                        func=AF.Exp, scale=-1.0), "a_l")
                    b.wait("vector", "a_l", b.tot("a_l"))
                    b.wait("vector", "p_b", b.tot("p_b"))
                    b.op("vector", lambda h: h.tensor_copy(
                        out=self.dinv16[0:1, 0:TOKB],
                        in_=self.dinv32[0:1, 0:TOKB]), "d_c")

                def close2(pav_tot=pav_tot, psAV=psAV, zs=zs, hh=hh, qbi=qbi,
                           qpar=qpar):
                    b.wait("tensor", "d_c", b.tot("d_c"))
                    b.wait("tensor", "d_bv", b.tot("d_bv"))
                    b.op("tensor", lambda h: h.matmul(
                        out=self.psD[:, 0:TOKB], lhsT=self.c_o1x128[:],
                        rhs=self.dinv16[0:1, 0:TOKB], start=True, stop=True,
                        skip_group_check=True), "p_b")
                    b.wait("vector", "p_b", b.tot("p_b"))
                    b.op("vector", lambda h: h.tensor_copy(
                        out=self.dinvb[:, 0:TOKB], in_=self.psD[:, 0:TOKB]), "d_bv")
                    b.wait("vector", "d_bv", b.tot("d_bv"))
                    b.wait("vector", "p_av", pav_tot)
                    b.wait("vector", f"z_st{zs}", b.tot(f"z_st{zs}"))
                    b.op("vector", lambda h, zs=zs, psAV=psAV: h.tensor_tensor(
                        out=self.z16[:, zs], in0=psAV[:, 0:TOKB],
                        in1=self.dinvb[:, 0:TOKB], op=ALU.mult), "d_z")
                    dz_hist[qpar] = b.tot("d_z")
                    b.wait("gpsimd", "d_z", b.tot("d_z"))
                    b.dma("gpsimd", self.z_s[hh][:, qbi * TOKB:(qbi + 1) * TOKB],
                          self.z16[:, zs], f"z_st{zs}")

                pend_close = [close1, close2]
                gq += 1
            after[hs] = {"p1s": b.tot("p1s"), "p_av": b.tot("p_av"),
                         "d_cr": b.tot("d_cr")}
            if wgen is not None:
                for _ in range(7):
                    if next(wgen, "END") == "END":
                        wgen = None
                        break
        while pend_close:
            pend_close.pop(0)()

    def emit_p23(self, v):
        b, c = self, self.cfg
        D, I, H, TOKB, ND, NI = c["D"], c["I"], c["H"], c["TOKB"], c["ND"], c["NI"]
        s_v, e_v = c["bounds"][v], c["bounds"][v + 1]
        n_v = e_v - s_v
        TS = TOKB // 128
        nst = int(math.log2(ND))
        ntb = n_v // TOKB
        wob_hist, psA_hist = {}, {}
        p3u_hist, p3g_hist, p3h_hist, h2_hist = {}, {}, {}, {}
        for tb in range(ntb):
            t0 = tb * TOKB
            b.wait("sync", "p3dn", b.tot("p3dn"))
            b.wait("sync", "d_u", b.tot("d_u"))
            for hh2 in range(H):
                b.dma("sync", self.z1[:, hh2, 0:TOKB],
                      self.z_s[hh2][:, t0:t0 + TOKB], "zl")
            zl_tot = b.tot("zl")
            owl_hist = {}

            def load_oproj(dc):
                slw = dc % 2
                if slw in wob_hist:
                    b.wait("scalar", "p2o", wob_hist[slw])
                b.dma("scalar", self.wob[:, slw, :].rearrange("p (hh d) -> p hh d", d=128),
                      self.wo16s.rearrange("hh p d -> p hh d")[:, :, dc * 128:(dc + 1) * 128],
                      f"p2wl{slw}")
                b.wait("sync", "dve_y", psA_hist.get(slw, 0))
                b.dma("sync", self.xst[:, slw, 0:TOKB],
                      self.xfm_s[dc][:, t0:t0 + TOKB], f"p2xl{slw}")
                owl_hist[dc] = (b.tot(f"p2wl{slw}"), b.tot(f"p2xl{slw}"))

            load_oproj(0)
            for dc in range(ND):
                slw = dc % 2
                if dc + 1 < ND:
                    load_oproj(dc + 1)
                b.wait("tensor", f"p2wl{slw}", owl_hist[dc][0])
                b.wait("tensor", f"p2xl{slw}", owl_hist[dc][1])
                if dc == 0:
                    b.wait("tensor", "zl", zl_tot)
                b.wait("tensor", "dve_y", psA_hist.get(slw, 0))
                for hh2 in range(H):
                    b.op("tensor", lambda h, slw=slw, hh2=hh2: h.matmul(
                        out=self.psA[:, slw, 0:TOKB],
                        lhsT=self.wob[:, slw, hh2 * 128:(hh2 + 1) * 128],
                        rhs=self.z1[:, hh2, 0:TOKB],
                        start=(hh2 == 0), stop=(hh2 == H - 1),
                        skip_group_check=True), "p2o" if hh2 == H - 1 else None)
                wob_hist[slw] = b.tot("p2o")
                b.wait("vector", "p2o", b.tot("p2o"))
                b.op("vector", lambda h, slw=slw, dc=dc: h.scalar_tensor_tensor(
                    out=self.y32[:, dc, 0:TOKB], in0=self.psA[:, slw, 0:TOKB],
                    scalar=self.v_bo[:, dc:dc + 1], in1=self.xst[:, slw, 0:TOKB],
                    op0=ALU.add, op1=ALU.add), "dve_y")
                psA_hist[slw] = b.tot("dve_y")
                b.wait("scalar", "dve_y", b.tot("dve_y"))
                b.wait("scalar", "p2s2", b.tot("p2s2"))
                b.op("scalar", lambda h, dc=dc: h.square(
                    out=self.ysq[:, dc % 2], in_=self.y32[:, dc, 0:TOKB]), "a_sq")
                b.wait("tensor", "dve_y", b.tot("dve_y"))
                b.op("tensor", lambda h, dc=dc: h.matmul(
                    out=self.psE[0:1, 0:TOKB], lhsT=self.c_ones_c32[:],
                    rhs=self.y32[:, dc, 0:TOKB], start=(dc == 0), stop=(dc == ND - 1),
                    skip_group_check=True), "p2sy")
                b.wait("tensor", "a_sq", b.tot("a_sq"))
                b.op("tensor", lambda h, dc=dc: h.matmul(
                    out=self.psF[0:1, 0:TOKB], lhsT=self.c_ones_c32[:],
                    rhs=self.ysq[:, dc % 2], start=(dc == 0), stop=(dc == ND - 1),
                    skip_group_check=True), "p2s2")
            self._ln_rows(v, "p2sy", "p2s2", D)
            for dc in range(ND):
                hsl = dc % 2
                if dc == 0:
                    b.wait("vector", "d_b", b.tot("d_b"))
                b.op("vector", lambda h, dc=dc: h.tensor_sub(
                    out=self.ysq[:, dc % 2], in0=self.y32[:, dc, 0:TOKB],
                    in1=self.mb[:, 0:TOKB]), "g_t")
                b.wait("vector", f"h_st{hsl}", b.tot(f"h_st{hsl}"))
                b.op("vector", lambda h, hsl=hsl, dc=dc: h.tensor_mul(
                    out=self.hst[:, hsl, 0:TOKB], in0=self.ysq[:, dc % 2],
                    in1=self.rb[:, 0:TOKB]), "d_h")
                b.op("vector", lambda h, dc=dc, hsl=hsl: h.tensor_scalar_mul(
                    out=self.hb16[:, 0, dc, 0:TOKB], in0=self.hst[:, hsl, 0:TOKB],
                    scalar1=self.v_su_ug[:, dc:dc + 1]), "g_su")
                b.wait("gpsimd", "d_h", b.tot("d_h"))
                b.dma("gpsimd", self.h_s[dc][:, t0:t0 + TOKB],
                      self.hst[:, hsl, 0:TOKB], f"h_st{hsl}")
            b.wait("vector", "g_su", b.tot("g_su"))
            b.wait("vector", "p3u", b.tot("p3u"))
            for st in range(nst):
                bit = 1 << st
                if st > 0:
                    b.wait("vector", "d_f", b.tot("d_f"))
                for dc in range(ND):
                    src0 = dc & ~bit
                    src1 = dc | bit
                    if dc & bit:
                        b.op("vector", lambda h, st=st, dc=dc, src0=src0, src1=src1:
                             h.tensor_sub(
                                 out=self.hb16[:, (st + 1) % 2, dc, 0:TOKB],
                                 in0=self.hb16[:, st % 2, src0, 0:TOKB],
                                 in1=self.hb16[:, st % 2, src1, 0:TOKB]), "d_f")
                    else:
                        b.op("vector", lambda h, st=st, dc=dc, src0=src0, src1=src1:
                             h.tensor_add(
                                 out=self.hb16[:, (st + 1) % 2, dc, 0:TOKB],
                                 in0=self.hb16[:, st % 2, src0, 0:TOKB],
                                 in1=self.hb16[:, st % 2, src1, 0:TOKB]), "d_f")
            fb = nst % 2
            b.wait("tensor", "d_f", b.tot("d_f"))
            for dc in range(ND):
                b.wait("tensor", "d_h2", h2_hist.get(dc % 2, 0))
                b.wait("tensor", "dve_y", psA_hist.get(dc % 2, 0))
                b.op("tensor", lambda h, dc=dc, fb=fb: h.matmul(
                    out=self.psA[:, dc % 2, 0:TOKB], lhsT=self.c_h128[:],
                    rhs=self.hb16[:, fb, dc, 0:TOKB], start=True, stop=True,
                    skip_group_check=True), "p2h")
                b.wait("vector", "p2h", b.tot("p2h"))
                b.op("vector", lambda h, dc=dc: h.tensor_copy(
                    out=self.h2b[:, dc, 0:TOKB], in_=self.psA[:, dc % 2, 0:TOKB]),
                    "d_h2")
                h2_hist[dc % 2] = b.tot("d_h2")
            b.wait("tensor", "d_h2", b.tot("d_h2"))
            b.wait("tensor", "a_g", b.tot("a_g"))
            NH = NI // 2
            uwl_hist = {}

            def load_ug(uc):
                sl = uc % 2
                ug, wi = uc // TS, uc % TS
                if uc >= 2:
                    b.wait("sync", "d_u", p3u_hist.get(sl, 0))
                    b.wait("gpsimd", "d_u", p3u_hist.get(sl, 0))
                b.dma("sync", self.wtb[:, sl, 0],
                      self.wupT[ug].rearrange("cc p u -> p cc u")[
                          :, :, wi * 128:(wi + 1) * 128], f"p3wl{sl}")
                b.dma("gpsimd", self.wtb[:, sl, 1],
                      self.wgtT[ug].rearrange("cc p u -> p cc u")[
                          :, :, wi * 128:(wi + 1) * 128], f"p3wg{sl}")
                uwl_hist[uc] = (b.tot(f"p3wl{sl}"), b.tot(f"p3wg{sl}"))

            for hf in range(2):
                b.wait("tensor", "d_y0", b.tot("d_y0"))
                b.wait("tensor", "d_y2", b.tot("d_y2"))
                if hf == 0:
                    load_ug(0)
                for uci in range(NH):
                    uc = hf * NH + uci
                    sl = uc % 2
                    if uc + 1 < NI:
                        load_ug(uc + 1)
                    b.wait("tensor", f"p3wl{sl}", uwl_hist[uc][0])
                    b.wait("tensor", f"p3wg{sl}", uwl_hist[uc][1])
                    if uc >= 2:
                        b.wait("tensor", "d_u", p3u_hist.get(sl, 0))
                    for dc in range(ND):
                        b.op("tensor", lambda h, sl=sl, dc=dc: h.matmul(
                            out=self.psA[:, sl, 0:TOKB],
                            lhsT=self.wtb[:, sl, 0, dc],
                            rhs=self.h2b[:, dc, 0:TOKB], start=(dc == 0),
                            stop=(dc == ND - 1), skip_group_check=True))
                    for dc in range(ND):
                        b.op("tensor", lambda h, sl=sl, dc=dc: h.matmul(
                            out=self.psB[:, sl, 0:TOKB],
                            lhsT=self.wtb[:, sl, 1, dc],
                            rhs=self.h2b[:, dc, 0:TOKB], start=(dc == 0),
                            stop=(dc == ND - 1), skip_group_check=True),
                            "p3u" if dc == ND - 1 else None)
                    b.wait("scalar", "p3u", b.tot("p3u"))
                    b.wait("scalar", "d_u", b.tot("d_u"))
                    b.op("scalar", lambda h, sl=sl, uc=uc: h.activation(
                        out=self.g16[:, 0:TOKB], in_=self.psB[:, sl, 0:TOKB],
                        func=AF.Silu, scale=self.v_sgt[:, uc:uc + 1]), "a_g")
                    b.wait("vector", "a_g", b.tot("a_g"))
                    b.op("vector", lambda h, sl=sl, uc=uc, uci=uci: h.scalar_tensor_tensor(
                        out=self.z1[:, uci, 0:TOKB], in0=self.psA[:, sl, 0:TOKB],
                        scalar=self.v_sup[:, uc:uc + 1], in1=self.g16[:, 0:TOKB],
                        op0=ALU.mult, op1=ALU.mult), "d_u")
                    p3u_hist[sl] = b.tot("d_u")
                b.wait("tensor", "d_u", b.tot("d_u"))
                gl_hist = {}
                hl_hist = {}

                def load_g(oc, q):
                    s2 = (oc * 2 + q) % 2
                    b.wait("scalar", "p3dn", p3g_hist.get(s2, 0))
                    qa0 = hf * NH + q * (NI // 4)
                    b.dma("scalar", self.gwb[:, s2],
                          self.g_s[oc, qa0:qa0 + NI // 4].rearrange(
                              "i b o -> b i o"), f"p3gl{s2}")
                    gl_hist[(oc, q)] = b.tot(f"p3gl{s2}")

                def load_h(oc):
                    so = oc % 2
                    b.wait("gpsimd", "d_y2", p3h_hist.get(so, 0))
                    b.dma("gpsimd", self.hst[:, so, 0:TOKB],
                          self.h_s[oc][:, t0:t0 + TOKB], f"p3hl{so}")
                    hl_hist[oc] = b.tot(f"p3hl{so}")

                load_g(0, 0)
                if hf == 1:
                    b.wait("gpsimd", "h_st0", b.tot("h_st0"))
                    b.wait("gpsimd", "h_st1", b.tot("h_st1"))
                    load_h(0)
                for oc in range(ND):
                    so = oc % 2
                    if oc + 1 < ND:
                        if hf == 1:
                            load_h(oc + 1)
                    b.wait("tensor", "d_y0", b.tot("d_y0"))
                    b.wait("tensor", "d_y2", b.tot("d_y2"))
                    for q in range(2):
                        s2 = (oc * 2 + q) % 2
                        if (oc, q) != (ND - 1, 1):
                            nxt = (oc, q + 1) if q == 0 else (oc + 1, 0)
                            load_g(*nxt)
                        b.wait("tensor", f"p3gl{s2}", gl_hist[(oc, q)])
                        for i in range(NI // 4):
                            uci = q * (NI // 4) + i
                            b.op("tensor", lambda h, s2=s2, i=i, oc=oc, uci=uci:
                                 h.matmul(
                                     out=self.psA[:, oc % 2, 0:TOKB],
                                     lhsT=self.gwb[:, s2, i, :],
                                     rhs=self.z1[:, uci, 0:TOKB],
                                     start=(uci == 0), stop=(uci == NH - 1),
                                     skip_group_check=True),
                                 "p3dn" if i == NI // 4 - 1 else None)
                        p3g_hist[s2] = b.tot("p3dn")
                    b.wait("vector", "p3dn", b.tot("p3dn"))
                    if hf == 0:
                        b.op("vector", lambda h, oc=oc: h.tensor_scalar_mul(
                            out=self.y32[:, oc, 0:TOKB],
                            in0=self.psA[:, oc % 2, 0:TOKB],
                            scalar1=self.v_sdn[:, oc:oc + 1]), "d_y0")
                    else:
                        b.wait("vector", f"p3hl{so}", hl_hist[oc])
                        b.op("vector", lambda h, oc=oc, so=so: h.scalar_tensor_tensor(
                            out=self.ysq[:, oc % 2], in0=self.psA[:, oc % 2, 0:TOKB],
                            scalar=self.v_sdn[:, oc:oc + 1], in1=self.hst[:, so, 0:TOKB],
                            op0=ALU.mult, op1=ALU.add), "d_y2")
                        b.wait("vector", "d_y2", b.tot("d_y2"))
                        b.op("vector", lambda h, oc=oc: h.tensor_tensor(
                            out=self.y32[:, oc, 0:TOKB], in0=self.y32[:, oc, 0:TOKB],
                            in1=self.ysq[:, oc % 2], op=ALU.add), "d_y2")
                        p3h_hist[so] = b.tot("d_y2")
                        b.wait("scalar", "d_y2", b.tot("d_y2"))
                        b.wait("scalar", "p3s2", b.tot("p3s2"))
                        b.wait("scalar", "d_h", b.tot("d_h"))
                        b.op("scalar", lambda h, oc=oc: h.square(
                            out=self.ysq[:, oc % 2], in_=self.y32[:, oc, 0:TOKB]),
                            "a_sq")
                        b.wait("tensor", "d_y2", b.tot("d_y2"))
                        b.op("tensor", lambda h, oc=oc: h.matmul(
                            out=self.psE[0:1, 0:TOKB], lhsT=self.c_ones_c32[:],
                            rhs=self.y32[:, oc, 0:TOKB], start=(oc == 0),
                            stop=(oc == ND - 1), skip_group_check=True), "p3sy")
                        b.wait("tensor", "a_sq", b.tot("a_sq"))
                        b.op("tensor", lambda h, oc=oc: h.matmul(
                            out=self.psF[0:1, 0:TOKB], lhsT=self.c_ones_c32[:],
                            rhs=self.ysq[:, oc % 2], start=(oc == 0),
                            stop=(oc == ND - 1), skip_group_check=True), "p3s2")
            self._ln_rows(v, "p3sy", "p3s2", D)
            trb_hist = {}
            dot_hist = {}
            for dc in range(ND):
                osl = dc % 2
                pstr = [self.psF, self.psC][osl]
                if dc == 0:
                    b.wait("vector", "d_b", b.tot("d_b"))
                b.op("vector", lambda h, dc=dc: h.tensor_sub(
                    out=self.ysq[:, dc % 2], in0=self.y32[:, dc, 0:TOKB],
                    in1=self.mb[:, 0:TOKB]), "g_t2")
                b.wait("vector", "p3tr", trb_hist.get(osl, 0))
                b.op("vector", lambda h, osl=osl, dc=dc: h.tensor_mul(
                    out=self.oap[:, osl, 0:TOKB], in0=self.ysq[:, dc % 2],
                    in1=self.rb[:, 0:TOKB]), "d_o")
                b.wait("tensor", "d_o", b.tot("d_o"))
                b.wait("tensor", "d_ot", dot_hist.get(osl, 0))
                for ts in range(TS):
                    b.op("tensor", lambda h, osl=osl, ts=ts, pstr=pstr: h.transpose(
                        out=pstr[:, ts * 128:(ts + 1) * 128],
                        in_=self.oap[:, osl, ts * 128:(ts + 1) * 128],
                        identity=self.c_eye32[:]), "p3tr" if ts == TS - 1 else None)
                trb_hist[osl] = b.tot("p3tr")
                b.wait("vector", "p3tr", b.tot("p3tr"))
                b.wait("vector", f"o_st{osl}", b.tot(f"o_st{osl}"))
                b.op("vector", lambda h, osl=osl, pstr=pstr: h.tensor_copy(
                    out=self.ot32[:, osl, 0:TS, :],
                    in_=pstr[:, 0:TS * 128].rearrange("p (s e) -> p s e", e=128)),
                    "d_ot")
                dot_hist[osl] = b.tot("d_ot")
                b.wait("gpsimd", "d_ot", b.tot("d_ot"))
                b.dma("gpsimd",
                      self.out.rearrange("(t p) d -> p t d", p=128)[
                          :, tb * TS:(tb + 1) * TS, dc * 128:(dc + 1) * 128],
                      self.ot32[:, osl, 0:TS, :], f"o_st{osl}")
        b.wait("gpsimd", "o_st0", b.tot("o_st0"))
        b.wait("gpsimd", "o_st1", b.tot("o_st1"))

    def _ln_rows(self, v, sy, s2, D):
        b = self
        TOKB = self.cfg["TOKB"]
        b.wait("vector", sy, b.tot(sy))
        b.wait("vector", "a_r", b.tot("a_r"))
        b.op("vector", lambda h: h.tensor_scalar(
            out=self.mrow[0:1, 0:TOKB], in0=self.psE[0:1, 0:TOKB],
            scalar1=1.0 / D, scalar2=None, op0=ALU.mult), "d_r")
        b.wait("vector", s2, b.tot(s2))
        b.op("vector", lambda h: h.tensor_scalar(
            out=self.vrow[0:1, 0:TOKB], in0=self.psF[0:1, 0:TOKB],
            scalar1=1.0 / D, scalar2=None, op0=ALU.mult), "d_r")
        b.wait("scalar", "d_r", b.tot("d_r"))
        b.op("scalar", lambda h: h.square(
            out=self.wrow[0:1, 0:TOKB], in_=self.mrow[0:1, 0:TOKB]), "a_r")
        b.wait("vector", "a_r", b.tot("a_r"))
        b.op("vector", lambda h: h.scalar_tensor_tensor(
            out=self.vrow[0:1, 0:TOKB], in0=self.vrow[0:1, 0:TOKB],
            scalar=LN_EPS, in1=self.wrow[0:1, 0:TOKB],
            op0=ALU.add, op1=ALU.subtract), "d_r")
        b.wait("scalar", "d_r", b.tot("d_r"))
        b.op("scalar", lambda h: h.activation(
            out=self.wrow[0:1, 0:TOKB], in_=self.vrow[0:1, 0:TOKB],
            func=AF.Ln), "a_r")
        b.wait("scalar", "a_r", b.tot("a_r"))
        b.op("scalar", lambda h: h.activation(
            out=self.rrow[0:1, 0:TOKB], in_=self.wrow[0:1, 0:TOKB],
            func=AF.Exp, scale=-0.5), "a_r")
        b.wait("tensor", "d_r", b.tot("d_r"))
        b.wait("tensor", "a_r", b.tot("a_r"))
        b.wait("tensor", "d_b", b.tot("d_b"))
        b.wait("tensor", "d_ot", b.tot("d_ot"))
        b.op("tensor", lambda h: h.matmul(
            out=self.psD[:, 0:TOKB], lhsT=self.c_o1x128_32[:],
            rhs=self.mrow[0:1, 0:TOKB], start=True, stop=True,
            skip_group_check=True), "p_br")
        b.op("tensor", lambda h: h.matmul(
            out=self.psC[:, 0:TOKB], lhsT=self.c_o1x128_32[:],
            rhs=self.rrow[0:1, 0:TOKB], start=True, stop=True,
            skip_group_check=True), "p_br")
        b.wait("vector", "p_br", b.tot("p_br"))
        b.op("vector", lambda h: h.tensor_copy(
            out=self.mb[:, 0:TOKB], in_=self.psD[:, 0:TOKB]), "d_b")
        b.op("vector", lambda h: h.tensor_copy(
            out=self.rb[:, 0:TOKB], in_=self.psC[:, 0:TOKB]), "d_b")

    def build(self):
        b = self
        self.cur = self.pre_ops
        self.emit_pre()
        pre_snapshot = dict(self.cnt)
        for v in range(4):
            self.cnt = dict(pre_snapshot)
            self.cur = self.var_ops[v]
            self.barrier_pre()
            self.emit_p0(v)
            self._drain_gpsimd(["ks_st0", "ks_st1", "vt_st0", "vt_st1",
                                "qs_st0", "qs_st1", "qn_st0", "qn_st1",
                                "xfs", "mkn"])
            self.barrier("bar0")
            wgen = self.emit_wprep(v)
            self.emit_p1(v, wgen)
            for _ in wgen:
                pass
            self._drain_gpsimd(["wp_s0", "wp_s1", "wt_s0", "wt_s1",
                                "wa_s0", "wa_s1", "wb_s0", "wb_s1",
                                "z_st0", "z_st1"])
            self.barrier("bar2")
            self.emit_p23(v)
        with self.es, self.nc.Block() as block:
            def section(eng_name):
                def emit(h):
                    for fn in self.pre_ops[eng_name]:
                        fn(h)
                    idx = h.bass.partition_id() % 4
                    for vv in h.Switch(idx, 4):
                        for fn in self.var_ops[vv][eng_name]:
                            fn(h)
                return emit
            block.sync(section("sync"))
            block.tensor(section("tensor"))
            block.vector(section("vector"))
            block.scalar(section("scalar"))
            block.gpsimd(section("gpsimd"))
        return self.nc

    def _drain_gpsimd(self, semnames):
        for s in semnames:
            if self.tot(s) > 0:
                self.wait("gpsimd", s, self.tot(s))


def build_nc(cfg):
    kb = KB(cfg)
    return kb.build()


_NC_CACHE = {}


def kernel(L0_attn_x, L0_attn_W_Q, L0_attn_W_K, L0_attn_W_V, L0_attn_W_O,
           L0_attn_b_Q, L0_attn_b_K, L0_attn_b_V, L0_attn_b_O,
           L0_ffn_SU_ug, L0_ffn_SU_dp, L0_ffn_Wscale_ug, L0_ffn_Wscale_dp,
           L0_ffn_W_up, L0_ffn_W_gate, L0_ffn_W_down, _small_cfg=None,
           _run=None):
    cfg = _small_cfg or make_cfg()
    H, D, S, NM = cfg["H"], cfg["D"], cfg["S"], cfg["NMAX"]
    x = np.ascontiguousarray(np.asarray(L0_attn_x), dtype=np.float32)
    consts = host_consts(cfg)
    base = {
        "wq": np.ascontiguousarray(L0_attn_W_Q, dtype=np.float32),
        "wk": np.ascontiguousarray(L0_attn_W_K, dtype=np.float32),
        "wv": np.ascontiguousarray(L0_attn_W_V, dtype=np.float32),
        "wo": np.ascontiguousarray(L0_attn_W_O, dtype=np.float32),
        "bq": np.ascontiguousarray(L0_attn_b_Q, dtype=np.float32).reshape(1, H * 128),
        "bk": np.ascontiguousarray(L0_attn_b_K, dtype=np.float32).reshape(1, H * 128),
        "bv": np.ascontiguousarray(L0_attn_b_V, dtype=np.float32).reshape(1, H * 128),
        "bo": np.ascontiguousarray(L0_attn_b_O, dtype=np.float32).reshape(-1, 128),
        "su_ug": np.ascontiguousarray(L0_ffn_SU_ug, dtype=np.float32).reshape(-1, 128),
        "su_dp": np.ascontiguousarray(L0_ffn_SU_dp, dtype=np.float32).reshape(-1, 128),
        "ws_ug": np.ascontiguousarray(L0_ffn_Wscale_ug, dtype=np.float32).reshape(-1, 128),
        "ws_dp": np.ascontiguousarray(L0_ffn_Wscale_dp, dtype=np.float32).reshape(-1, 128),
        "w_up": np.ascontiguousarray(L0_ffn_W_up, dtype=np.float32),
        "w_gate": np.ascontiguousarray(L0_ffn_W_gate, dtype=np.float32),
        "w_down": np.ascontiguousarray(L0_ffn_W_down, dtype=np.float32),
    }
    base.update(consts)
    in_maps = []
    for core in range(8):
        m = dict(base)
        m["xin"] = np.ascontiguousarray(x[core // 4])
        in_maps.append(m)
    key = (cfg["S"], cfg["D"])
    if key not in _NC_CACHE:
        _NC_CACHE[key] = build_nc(cfg)
    nc = _NC_CACHE[key]
    if _run is not None:
        results = _run(nc, in_maps)
    else:
        res = run_bass_kernel_spmd(nc, in_maps, core_ids=list(range(8)))
        results = [res.results[i] for i in range(8)]
    outp = np.empty((2, S, D), dtype=np.float32)
    bounds = cfg["bounds"]
    for core in range(8):
        bb, v = core // 4, core % 4
        s_v, e_v = bounds[v], bounds[v + 1]
        outp[bb, s_v:e_v, :] = np.asarray(results[core]["out"]).reshape(NM, D)[
            0:e_v - s_v]
    return outp
